# version 19
# baseline (speedup 1.0000x reference)
import os
import numpy as np
import concourse.bass as bass
import concourse.mybir as mybir
from concourse.bass_utils import run_bass_kernel_spmd

F32 = mybir.dt.float32
BF16 = mybir.dt.bfloat16
AF = mybir.ActivationFunctionType
ALU = mybir.AluOpType
AX = mybir.AxisListType

S = 2048
PH = int(os.environ.get("ANT_PH", "5"))
DS = int(os.environ.get("ANT_DS", "9"))
D = 1024
DI = 2048
NS = 16
II = 2816
TC = 256
NJ = S // TC
EPS = 1e-5
MIDS = [0.25, 0.75, 1.25, 1.75, 2.5, 3.5, 5.0]
WTS = [0.5, 0.5, 0.5, 0.5, 1.0, 1.0, 2.0]

ENGS = ("vector", "scalar", "tensor", "gpsimd", "sync")


class _Buf:
    __slots__ = ("prod", "readers")

    def __init__(self):
        self.prod = None
        self.readers = []


class _Prog:

    SELF_SYNC = {"vector", "scalar", "tensor", "gpsimd", "sync"}

    def __init__(self, sems, mk_sem):
        self.atoms = {e: [] for e in ENGS}
        self.sems = sems
        self.mk_sem = mk_sem
        self.cnt = {e: 0 for e in ENGS}
        self.marks = {e: {} for e in ENGS}
        self.bufs = {}
        self.fams = {}
        self.nbar = 0
        self.sbar = mk_sem("sbar")

    def buf(self, name):
        if name not in self.bufs:
            self.bufs[name] = _Buf()
        return self.bufs[name]

    def _wait_raw(self, e, sem, val):
        if val <= 0:
            return
        k = id(sem)
        if self.marks[e].get(k, 0) >= val:
            return
        self.atoms[e].append(("w", sem, val))
        self.marks[e][k] = val

    def _wait_tok(self, e, tok):
        if tok is None:
            return
        if tok[0] == "F":
            fam = tok[1]
            sem, cnt, _q = self.fams[fam]
            self._wait_raw(e, sem, cnt)
        else:
            sem, val = tok[1], tok[2]
            if sem is self.sems.get(e) and e not in self.SELF_SYNC:
                return
            self._wait_raw(e, sem, val)

    def _deps(self, e, reads, writes):
        for n in reads:
            b = self.buf(n)
            self._wait_tok(e, b.prod)
        for n in writes:
            b = self.buf(n)
            self._wait_tok(e, b.prod)
            for r in b.readers:
                self._wait_tok(e, r)

    def _finish(self, tok, reads, writes):
        for n in writes:
            b = self.buf(n)
            b.prod = tok
            b.readers = []
        for n in reads:
            self.buf(n).readers.append(tok)

    def op(self, e, fn, reads=(), writes=()):
        self._deps(e, reads, writes)
        self.atoms[e].append(("c", fn, self.sems[e]))
        self.cnt[e] += 1
        self._finish(("E", self.sems[e], self.cnt[e]), reads, writes)

    def dma(self, e, out_ap, in_ap, reads=(), writes=(), fam="gen",
            slow=False, waw=True):
        if fam not in self.fams:
            self.fams[fam] = [self.mk_sem("f_" + fam), 0, e]
        rec = self.fams[fam]
        assert rec[2] == e, f"fam {fam} used on two queues"
        for n in reads:
            self._wait_tok(e, self.buf(n).prod)
        for n in writes:
            b = self.buf(n)
            if waw:
                self._wait_tok(e, b.prod)
            for r in b.readers:
                self._wait_tok(e, r)
        self.atoms[e].append(("d", out_ap, in_ap, rec[0], slow))
        rec[1] += 16
        self._finish(("F", fam), reads, writes)

    def barrier(self):
        self.nbar += 1
        tgt = 5 * self.nbar
        for e in ENGS:
            for fam, (sem, cnt, q) in self.fams.items():
                if q == e and cnt:
                    self._wait_raw(e, sem, cnt)
            self.atoms[e].append(("b", self.sbar, tgt))
            self.marks[e][id(self.sbar)] = tgt
        for b in self.bufs.values():
            b.readers = []

    def drain(self):
        for e in ENGS:
            for fam, (sem, cnt, q) in self.fams.items():
                if q == e and cnt:
                    self._wait_raw(e, sem, cnt)


def build_nc():
    nc = bass.Bass("TRN2", target_bir_lowering=False)

    x_d = nc.dram_tensor("x", [S, D], F32, kind="ExternalInput").ap()
    W_in_d = nc.dram_tensor("W_in", [D, 4 * D], F32, kind="ExternalInput").ap()
    conv_w_d = nc.dram_tensor("conv_w", [DI, 4], F32, kind="ExternalInput").ap()
    conv_b_d = nc.dram_tensor("conv_b", [DI], F32, kind="ExternalInput").ap()
    W_xp_d = nc.dram_tensor("W_xproj", [DI, 33], F32, kind="ExternalInput").ap()
    w_dt_d = nc.dram_tensor("w_dt", [1, DI], F32, kind="ExternalInput").ap()
    b_dt_d = nc.dram_tensor("b_dt", [DI], F32, kind="ExternalInput").ap()
    A_log_d = nc.dram_tensor("A_log", [DI, NS], F32, kind="ExternalInput").ap()
    D_par_d = nc.dram_tensor("D_param", [DI], F32, kind="ExternalInput").ap()
    W_out_d = nc.dram_tensor("W_out", [DI, D], F32, kind="ExternalInput").ap()
    w_g_d = nc.dram_tensor("w_gate", [II, D], F32, kind="ExternalInput").ap()
    w_u_d = nc.dram_tensor("w_up", [II, D], F32, kind="ExternalInput").ap()
    w_d_d = nc.dram_tensor("w_down", [D, II], F32, kind="ExternalInput").ap()
    out_d = nc.dram_tensor("out", [S, D], F32, kind="ExternalOutput").ap()
    xz_s = nc.dram_tensor("xz_s", [4 * D, S], BF16, kind="Internal").ap()
    ffn_sc = nc.dram_tensor("ffn_sc", [S, D], F32, kind="Internal").ap()

    import contextlib
    with contextlib.ExitStack() as ctx:
        def sem(name):
            return ctx.enter_context(nc.semaphore(name))

        sv, ss, st_, sg, sy = (sem(n) for n in ("sv", "ss", "st", "sg", "sy"))

        P = _Prog(
            sems={"vector": sv, "scalar": ss, "tensor": st_, "gpsimd": sg,
                  "sync": sy},
            mk_sem=sem,
        )

        def sbt(stack, name, shape, dt=F32):
            return stack.enter_context(nc.sbuf_tensor(name, shape, dt))

        A_sb = [sbt(ctx, f"A_{d}", [128, NS]) for d in range(16)]
        cb_sb = sbt(ctx, "cb_sb", [128, 16])
        bdt_sb = sbt(ctx, "bdt_sb", [128, 16])
        dpar_sb = sbt(ctx, "dpar_sb", [128, 16])
        cw_sb = sbt(ctx, "cw_sb", [128, 64])
        ones_row = sbt(ctx, "ones_row", [1, 128], BF16)
        iden_f = sbt(ctx, "iden_f", [128, 128])
        iden_b = sbt(ctx, "iden_b", [128, 128], BF16)
        wxp_b = sbt(ctx, "wxp_b", [128, 16 * 33], BF16)
        hstate = sbt(ctx, "hstate", [128, 256])
        sm_a = sbt(ctx, "sm_a", [128, 32])
        sm_b = sbt(ctx, "sm_b", [128, 288])
        sm_c = sbt(ctx, "sm_c", [128, 288])

        p = [ctx.enter_context(nc.psum_tensor(f"p{i}", [128, 512], F32))
             for i in range(6)]
        pT_ = [ctx.enter_context(nc.psum_tensor(f"pT{w}", [128, 1024], BF16))
               for w in range(2)]

        def quantN(tag, src_name, src_ap, dst_name, dst_ap, F,
                   qt, qs, l0, l2, l3, smo):
            nb = F // 16
            smb = sm_b[:, smo:smo + nb]
            smc = sm_c[:, smo:smo + nb]
            P.op("vector", lambda v: v.tensor_reduce(
                out=smb, in_=src_ap.rearrange("p (b w) -> p b w", w=16),
                axis=AX.X, op=ALU.max, apply_absolute_value=True),
                reads=[src_name], writes=["qsc" + tag])
            P.op("vector", lambda v: v.tensor_scalar(
                out=smb, in0=smb, scalar1=1.0 / 6.0,
                scalar2=1e-12, op0=ALU.mult, op1=ALU.max),
                reads=["qsc" + tag], writes=["qsc" + tag])
            P.op("vector", lambda v: v.reciprocal(out=smc, in_=smb),
                 reads=["qsc" + tag], writes=["qrs" + tag])
            rb = smc.unsqueeze(2).broadcast_to([128, nb, 16])
            sb_ = smb.unsqueeze(2).broadcast_to([128, nb, 16])
            P.op("vector", lambda v: v.tensor_tensor(
                out=qt.rearrange("p (b w) -> p b w", w=16),
                in0=src_ap.rearrange("p (b w) -> p b w", w=16),
                in1=rb, op=ALU.mult),
                reads=[src_name, "qrs" + tag], writes=["qt" + tag])
            P.op("scalar", lambda s: s.activation(
                out=qs, in_=qt, func=AF.Square),
                reads=["qt" + tag], writes=["qsq" + tag])
            P.op("vector", lambda v: v.tensor_scalar(
                out=l0, in0=qs, scalar1=MIDS[0] * MIDS[0], scalar2=None,
                op0=ALU.is_gt), reads=["qsq" + tag], writes=["ql0" + tag])
            for i in (1, 2, 3):
                P.op("vector", lambda v, i=i: v.scalar_tensor_tensor(
                    out=l0, in0=qs, scalar=MIDS[i] * MIDS[i],
                    op0=ALU.is_gt, in1=l0, op1=ALU.add),
                    reads=["qsq" + tag, "ql0" + tag], writes=["ql0" + tag])
            P.op("vector", lambda v: v.tensor_scalar(
                out=l2, in0=qs, scalar1=MIDS[4] * MIDS[4], scalar2=None,
                op0=ALU.is_gt), reads=["qsq" + tag], writes=["ql2" + tag])
            P.op("vector", lambda v: v.scalar_tensor_tensor(
                out=l2, in0=qs, scalar=MIDS[5] * MIDS[5],
                op0=ALU.is_gt, in1=l2, op1=ALU.add),
                reads=["qsq" + tag, "ql2" + tag], writes=["ql2" + tag])
            P.op("vector", lambda v: v.tensor_scalar(
                out=l3, in0=qs, scalar1=MIDS[6] * MIDS[6], scalar2=2.0,
                op0=ALU.is_gt, op1=ALU.mult),
                reads=["qsq" + tag], writes=["ql3" + tag])
            P.op("gpsimd", lambda g: g.tensor_add(out=l2, in0=l2, in1=l3),
                 reads=["ql2" + tag, "ql3" + tag], writes=["ql2" + tag])
            P.op("vector", lambda v: v.scalar_tensor_tensor(
                out=l0, in0=l0, scalar=0.5, op0=ALU.mult,
                in1=l2, op1=ALU.add),
                reads=["ql0" + tag, "ql2" + tag], writes=["ql0" + tag])
            P.op("vector", lambda v: v.tensor_scalar(
                out=qs, in0=qt, scalar1=0.0, scalar2=2.0,
                op0=ALU.is_ge, op1=ALU.mult),
                reads=["qt" + tag], writes=["qsq" + tag])
            P.op("vector", lambda v: v.scalar_tensor_tensor(
                out=qs.rearrange("p (b w) -> p b w", w=16),
                in0=qs.rearrange("p (b w) -> p b w", w=16),
                scalar=-1.0, op0=ALU.add, in1=sb_, op1=ALU.mult),
                reads=["qsq" + tag, "qsc" + tag], writes=["qsq" + tag])
            P.op("vector", lambda v: v.tensor_tensor(
                out=dst_ap, in0=l0, in1=qs, op=ALU.mult),
                reads=["ql0" + tag, "qsq" + tag], writes=[dst_name])

        tg_par = [0]

        def transpose_group(src_name, src_ap_fn, n_tiles, dst_fn, dst_name,
                            evac_eng="scalar"):
            done = 0
            while done < n_tiles:
                cnt = min(8, n_tiles - done)
                pg = tg_par[0] & 1
                tg_par[0] += 1
                pT = pT_[pg]

                def mk_tr(t, done=done, cnt=cnt, pT=pT):
                    r = None
                    for kk in range(cnt):
                        r = t.transpose(
                            out=pT[:, kk * 128:(kk + 1) * 128],
                            in_=src_ap_fn(done + kk),
                            identity=iden_b[:])
                    return r
                P.op("tensor", mk_tr, reads=[src_name, "iden_b"],
                     writes=[f"pTx{pg}"])

                def mk_ev(s, done=done, cnt=cnt, pT=pT):
                    r = None
                    for kk in range(cnt):
                        r = s.copy(out=dst_fn(done + kk),
                                   in_=pT[:, kk * 128:(kk + 1) * 128])
                    return r
                P.op(evac_eng, mk_ev, reads=[f"pTx{pg}"], writes=[dst_name])
                done += cnt

        for d in range(16):
            P.dma("sync", A_sb[d][:], A_log_d[d * 128:(d + 1) * 128, :],
                  writes=[f"A_{d}"], fam="cst")
        P.dma("sync", cw_sb[:].rearrange("p (d k) -> p d k", k=4),
              conv_w_d.rearrange("(d p) k -> p d k", p=128), writes=["cw"],
              fam="cst")
        P.dma("sync", cb_sb[:], conv_b_d.rearrange("(d p) -> p d", p=128),
              writes=["cb"], fam="cst", slow=True)
        P.dma("sync", bdt_sb[:], b_dt_d.rearrange("(d p) -> p d", p=128),
              writes=["bdt"], fam="cst", slow=True)
        P.dma("sync", dpar_sb[:], D_par_d.rearrange("(d p) -> p d", p=128),
              writes=["dpar"], fam="cst", slow=True)
        for d in range(16):
            P.op("scalar", lambda s, d=d: s.activation(
                out=A_sb[d][:], in_=A_sb[d][:], func=AF.Exp),
                reads=[f"A_{d}"], writes=[f"A_{d}"])
            P.op("vector", lambda v, d=d: v.tensor_scalar(
                out=A_sb[d][:], in0=A_sb[d][:], scalar1=-1.0, scalar2=None,
                op0=ALU.mult), reads=[f"A_{d}"], writes=[f"A_{d}"])
        P.op("gpsimd", lambda g: g.iota(
            out=iden_f[:], pattern=[[1, 128]], base=0, channel_multiplier=-1,
            allow_small_or_imprecise_dtypes=True), writes=["idf"])
        P.op("gpsimd", lambda g: g.tensor_scalar(
            out=iden_f[:], in0=iden_f[:], scalar1=0.0, scalar2=None,
            op0=ALU.is_equal), reads=["idf"], writes=["idf"])
        P.op("gpsimd", lambda g: g.tensor_copy(out=iden_b[:], in_=iden_f[:]),
             reads=["idf"], writes=["iden_b"])
        P.op("gpsimd", lambda g: g.memset(ones_row[:], 1.0), writes=["ones"])
        P.op("gpsimd", lambda g: g.memset(hstate[:], 0.0),
             writes=[f"hst{d}" for d in range(16)])

        with contextlib.ExitStack() as sAB:
            xnqT_r = sbt(sAB, "xnqT_r", [128, 8 * S], BF16)
            with contextlib.ExitStack() as sA:
                xnT_r = sbt(sA, "xnT_r", [128, 8 * S], BF16)
                winb = sbt(sA, "winb", [128, 8 * 4 * D], BF16)
                qtA = sbt(sA, "qtA", [128, 2 * D])
                qsA = sbt(sA, "qsA", [128, 2 * D])
                l0A = sbt(sA, "l0A", [128, 2 * D], BF16)
                l2A = sbt(sA, "l2A", [128, 2 * D], BF16)
                l3A = sbt(sA, "l3A", [128, 2 * D], BF16)
                QOA = sbt(sA, "QOA", [128, 2 * D], BF16)
                xa_ = [sbt(sA, f"xa{w}", [128, D]) for w in range(2)]
                xnf_ = [sbt(sA, f"xnf{w}", [128, D]) for w in range(2)]
                xnb_ = [sbt(sA, f"xnb{w}", [128, D], BF16) for w in range(2)]
                ev_ = [sbt(sA, f"ev{w}", [128, 512], BF16) for w in range(2)]

                for d in range(16):
                    P.dma("gpsimd", wxp_b[:, d * 33:(d + 1) * 33],
                          W_xp_d[d * 128:(d + 1) * 128, :],
                          writes=["wxp"], fam="wxl", waw=False)
                for k in range(8):
                    P.dma("gpsimd", winb[:, k * 4 * D:(k + 1) * 4 * D],
                          W_in_d[k * 128:(k + 1) * 128, :],
                          writes=["winb"], fam="winl", waw=False)

                for i in range(16 if PH >= 2 else 0):
                    w = i % 2
                    xa = xa_[w]
                    xt = f"xa{w}"
                    sm = (lambda c, w=w: sm_a[:, w * 16 + c:w * 16 + c + 1])
                    P.dma("sync", xa[:], x_d[i * 128:(i + 1) * 128, :],
                          writes=[xt], fam=xt)
                    P.op("scalar", lambda s, xa=xa, w=w: s.activation(
                        out=qsA[:, w * D:(w + 1) * D], in_=xa[:],
                        func=AF.Square,
                        accum_out=sm_a[:, w * 16:w * 16 + 1]), reads=[xt],
                        writes=[f"ln_ss{w}", f"qsqA{w}"])
                    P.op("vector", lambda v, xa=xa, sm=sm: v.tensor_reduce(
                        out=sm(1), in_=xa[:], axis=AX.X, op=ALU.add),
                        reads=[xt], writes=[f"ln_sum{w}"])
                    P.op("vector", lambda v, sm=sm: v.tensor_scalar(
                        out=sm(2), in0=sm(1), scalar1=1.0 / D,
                        scalar2=None, op0=ALU.mult),
                        reads=[f"ln_sum{w}"], writes=[f"ln_mean{w}"])
                    P.op("vector", lambda v, sm=sm: v.tensor_mul(
                        out=sm(3), in0=sm(2), in1=sm(2)),
                        reads=[f"ln_mean{w}"], writes=[f"ln_m2{w}"])
                    P.op("vector", lambda v, sm=sm: v.scalar_tensor_tensor(
                        out=sm(4), in0=sm(0), scalar=1.0 / D,
                        op0=ALU.mult, in1=sm(3), op1=ALU.subtract),
                        reads=[f"ln_ss{w}", f"ln_m2{w}"],
                        writes=[f"ln_var{w}"])
                    P.op("vector", lambda v, sm=sm: v.tensor_scalar(
                        out=sm(4), in0=sm(4), scalar1=EPS,
                        scalar2=None, op0=ALU.add),
                        reads=[f"ln_var{w}"], writes=[f"ln_var{w}"])
                    P.op("scalar", lambda s, sm=sm: s.activation(
                        out=sm(5), in_=sm(4), func=AF.Ln),
                        reads=[f"ln_var{w}"], writes=[f"ln_lv{w}"])
                    P.op("vector", lambda v, sm=sm: v.tensor_scalar(
                        out=sm(5), in0=sm(5), scalar1=-0.5,
                        scalar2=None, op0=ALU.mult),
                        reads=[f"ln_lv{w}"], writes=[f"ln_lv{w}"])
                    P.op("scalar", lambda s, sm=sm: s.activation(
                        out=sm(6), in_=sm(5), func=AF.Exp),
                        reads=[f"ln_lv{w}"], writes=[f"ln_rstd{w}"])
                    P.op("vector", lambda v, xa=xa, w=w, sm=sm:
                         v.tensor_scalar(
                             out=xnf_[w][:], in0=xa[:], scalar1=sm(2),
                             scalar2=sm(6), op0=ALU.subtract, op1=ALU.mult),
                         reads=[xt, f"ln_mean{w}", f"ln_rstd{w}"],
                         writes=[f"xnf{w}"])
                    P.op("vector", lambda v, w=w: v.tensor_copy(
                        out=xnb_[w][:], in_=xnf_[w][:]),
                        reads=[f"xnf{w}"], writes=[f"xnb{w}"])
                    quantN(f"A{w}", f"xnf{w}", xnf_[w][:], f"xnq{w}",
                           QOA[:, w * D:(w + 1) * D], D,
                           qtA[:, w * D:(w + 1) * D],
                           qsA[:, w * D:(w + 1) * D],
                           l0A[:, w * D:(w + 1) * D],
                           l2A[:, w * D:(w + 1) * D],
                           l3A[:, w * D:(w + 1) * D], w * 96)
                    transpose_group(
                        f"xnb{w}",
                        lambda dj, w=w: xnb_[w][:, dj * 128:(dj + 1) * 128],
                        8,
                        lambda dj, i=i: xnT_r[:, dj * S + i * 128:
                                              dj * S + (i + 1) * 128],
                        "xnTr")
                    transpose_group(
                        f"xnq{w}",
                        lambda dj, w=w: QOA[:, w * D + dj * 128:
                                            w * D + (dj + 1) * 128], 8,
                        lambda dj, i=i: xnqT_r[:, dj * S + i * 128:
                                               dj * S + (i + 1) * 128],
                        "xnqTr")
                    if i % 4 == 3 and PH >= 4:
                        tt4 = i // 4
                        for fc in range(32):
                            pb = p[fc % 2]
                            pname = f"pg{fc % 2}"

                            def mk_mm(t, fc=fc, pb=pb, tt4=tt4):
                                r = None
                                for k in range(8):
                                    r = t.matmul(
                                        out=pb[:],
                                        lhsT=winb[:, k * 4 * D + fc * 128:
                                                  k * 4 * D +
                                                  (fc + 1) * 128],
                                        rhs=xnT_r[:, k * S + tt4 * 512:
                                                  k * S + (tt4 + 1) * 512],
                                        start=(k == 0), stop=(k == 7))
                                return r
                            P.op("tensor", mk_mm, reads=["winb", "xnTr"],
                                 writes=[pname])
                            ev = ev_[fc % 2]
                            ename = f"ev{fc % 2}"
                            P.op("scalar", lambda s, ev=ev, pb=pb: s.copy(
                                out=ev[:], in_=pb[:]), reads=[pname],
                                writes=[ename])
                            P.dma("sync",
                                  xz_s[fc * 128:(fc + 1) * 128,
                                       tt4 * 512:(tt4 + 1) * 512],
                                  ev[:], reads=[ename], writes=["Dxz"],
                                  fam="xzw", waw=False)
                P.barrier()

            W2 = 1408
            with contextlib.ExitStack() as sB:
                wgT = sbt(sB, "wgT", [128, 8 * II], BF16)
                wuT = sbt(sB, "wuT", [128, 8 * II], BF16)
                wdT = sbt(sB, "wdT", [128, 22 * D], BF16)
                qtB = sbt(sB, "qtB", [128, W2])
                qsB = sbt(sB, "qsB", [128, W2])
                l0B = sbt(sB, "l0B", [128, W2], BF16)
                l2B = sbt(sB, "l2B", [128, W2], BF16)
                l3B = sbt(sB, "l3B", [128, W2], BF16)
                QOB = sbt(sB, "QOB", [128, W2], BF16)
                wdl = [sbt(sB, "wdl0", [128, W2])] * 2
                hqT1 = sbt(sB, "hqT1", [128, W2], BF16)
                hqb2 = sbt(sB, "hqb2", [128, W2], BF16)
                dne = sbt(sB, "dne", [128, D])

                def q_half(tag, src_name, src_ap, dst_name, dst_ap, F, sl,
                           smo):
                    quantN(tag, src_name, src_ap, dst_name, dst_ap, F,
                           qtB[:, sl], qsB[:, sl], l0B[:, sl], l2B[:, sl],
                           l3B[:, sl], smo)

                for dmc in range(8 if PH >= 3 else 0):
                    for hf in range(2):
                        nm = "wdl0"
                        P.dma("sync", wdl[hf][:],
                              w_d_d[dmc * 128:(dmc + 1) * 128,
                                    hf * W2:(hf + 1) * W2],
                              writes=[nm], fam=nm)
                        q_half("Bh", nm, wdl[hf][:], "wdq",
                               QOB[:, 0:W2], W2, slice(0, W2), 192)
                        transpose_group(
                            "wdq",
                            lambda ic: QOB[:, ic * 128:(ic + 1) * 128], 11,
                            lambda ic, dmc=dmc, hf=hf:
                            wdT[:, (hf * 11 + ic) * D + dmc * 128:
                                (hf * 11 + ic) * D + (dmc + 1) * 128],
                            "wdT")
                P.barrier()
                for (src_w, dstT, nmT) in ((w_g_d, wgT, "wgT"),
                                           (w_u_d, wuT, "wuT")):
                    for it in range(22 if PH >= 3 else 0):
                        w = 0
                        nm = "wdl0"
                        P.dma("sync", wdl[w][:, 0:D],
                              src_w[it * 128:(it + 1) * 128, :],
                              writes=[nm], fam=nm)
                        q_half("Bh", nm, wdl[w][:, 0:D], "wq",
                               QOB[:, 0:D], D, slice(0, D), 192)
                        transpose_group(
                            "wq",
                            lambda dj: QOB[:, dj * 128:(dj + 1) * 128], 8,
                            lambda dj, it=it, dstT=dstT:
                            dstT[:, dj * II + it * 128:
                                 dj * II + (it + 1) * 128],
                            nmT)
                P.barrier()
                sgf = wdl[0]
                widths = (512, 512, 384)
                tails = []

                def emit_tail(rec):
                    ti, hf, par = rec
                    gbase = hf * 11
                    qsrc = QOB if par == 0 else hqb2
                    transpose_group(
                        f"hq{par}",
                        lambda ic, qsrc=qsrc:
                        qsrc[:, ic * 128:(ic + 1) * 128], 11,
                        lambda ic: hqT1[:, ic * 128:(ic + 1) * 128],
                        "hqT1")

                    def mk_down(t, gbase=gbase):
                        r = None
                        for ic in range(11):
                            lhs = hqT1[:, ic * 128:(ic + 1) * 128]
                            gic = gbase + ic
                            for mi in range(2):
                                r = t.matmul(
                                    out=p[mi][:, 0:512], lhsT=lhs,
                                    rhs=wdT[:, gic * D + mi * 512:
                                            gic * D + (mi + 1) * 512],
                                    start=(ic == 0), stop=(ic == 10))
                        return r
                    P.op("tensor", mk_down, reads=["hqT1", "wdT"],
                         writes=["pg0", "pg1"])
                    if hf == 0:
                        P.op("vector", lambda v: v.tensor_copy(
                            out=dne[:, 0:512], in_=p[0][:, 0:512]),
                            reads=["pg0"], writes=["dne"])
                        P.op("vector", lambda v: v.tensor_copy(
                            out=dne[:, 512:1024], in_=p[1][:, 0:512]),
                            reads=["pg1"], writes=["dne"])
                    else:
                        P.op("vector", lambda v: v.tensor_add(
                            out=dne[:, 0:512], in0=dne[:, 0:512],
                            in1=p[0][:, 0:512]),
                            reads=["pg0", "dne"], writes=["dne"])
                        P.op("vector", lambda v: v.tensor_add(
                            out=dne[:, 512:1024], in0=dne[:, 512:1024],
                            in1=p[1][:, 0:512]),
                            reads=["pg1", "dne"], writes=["dne"])
                        P.dma("sync", ffn_sc[ti * 128:(ti + 1) * 128, :],
                              dne[:], reads=["dne"], writes=["Dffn"],
                              fam="ffw", waw=False)

                for i in range(16 if PH >= 3 else 0):
                    for hf in range(2):
                        def mk_gu(t, i=i, hf=hf):
                            r = None
                            for kb in range(8):
                                lhs = xnqT_r[:, kb * S + i * 128:
                                             kb * S + (i + 1) * 128]
                                for gi in range(6):
                                    mi = gi % 3
                                    o0 = sum(widths[:mi])
                                    wT = wgT if gi < 3 else wuT
                                    r = t.matmul(
                                        out=p[(0 if gi < 3 else 3) + mi][
                                            :, 0:widths[mi]],
                                        lhsT=lhs,
                                        rhs=wT[:, kb * II + hf * W2 + o0:
                                               kb * II + hf * W2 + o0 +
                                               widths[mi]],
                                        start=(kb == 0), stop=(kb == 7))
                            return r
                        P.op("tensor", mk_gu,
                             reads=["xnqTr", "wgT", "wuT"],
                             writes=["pg0", "pg1", "pg2",
                                     "pu0", "pu1", "pu2"])

                        def mk_silu(s):
                            r = None
                            for mi in range(3):
                                o0 = sum(widths[:mi])
                                r = s.activation(
                                    out=sgf[:, o0:o0 + widths[mi]],
                                    in_=p[mi][:, 0:widths[mi]],
                                    func=AF.Silu)
                            return r
                        P.op("scalar", mk_silu,
                             reads=["pg0", "pg1", "pg2"], writes=["sgf"])

                        def mk_h(v):
                            r = None
                            for mi in range(3):
                                o0 = sum(widths[:mi])
                                r = v.tensor_mul(
                                    out=qsB[:, o0:o0 + widths[mi]],
                                    in0=sgf[:, o0:o0 + widths[mi]],
                                    in1=p[3 + mi][:, 0:widths[mi]])
                            return r
                        P.op("vector", mk_h,
                             reads=["sgf", "pu0", "pu1", "pu2"],
                             writes=["h_f", "qsqBh"])
                        par = (2 * i + hf) % 2
                        q_half("Bh", "h_f", qsB[:, 0:W2], f"hq{par}",
                               (QOB if par == 0 else hqb2)[:, 0:W2],
                               W2, slice(0, W2), 192)
                        if tails:
                            emit_tail(tails.pop(0))
                        tails.append((i, hf, par))
                while tails:
                    emit_tail(tails.pop(0))
                P.barrier()

        with contextlib.ExitStack() as sD:
            diag_cv = sbt(sD, "diag_cv", [128, 16 * 4 * 128], BF16)
            diag_D = sbt(sD, "diag_D", [128, 16 * 128], BF16)
            woutb = sbt(sD, "woutb", [128, 16 * D], BF16)
            wdt_row = sbt(sD, "wdt_row", [1, DI])
            xin_b = sbt(sD, "xin_b", [128, 16 * 260], BF16)
            sz_ = [sbt(sD, f"sz{w}", [128, 16 * TC], BF16) for w in range(2)]
            xc_ = [sbt(sD, f"xc{w}", [128, 16 * TC], BF16) for w in range(2)]
            delt = sbt(sD, "delt", [128, 16 * TC])
            u_b = sbt(sD, "u_b", [128, 16 * TC], BF16)
            dbx_ = [sbt(sD, f"dbx{w}", [128, NS * TC], BF16)
                    for w in range(2)]
            bcB = sbt(sD, "bcB", [128, NS * TC], BF16)
            bcC = sbt(sD, "bcC", [128, NS * TC], BF16)
            hbuf = sbt(sD, "hbuf", [128, NS * TC], BF16)
            y2T = sbt(sD, "y2T", [128, 16 * TC], BF16)
            ssm_f = sbt(sD, "ssm_f", [33, TC])
            bc_flat = sbt(sD, "bc_flat", [1, 16 * TC], BF16)
            arg_sp = sbt(sD, "arg_sp", [128, 2 * TC])
            na_sp = sbt(sD, "na_sp", [128, 2 * TC])
            dA_sb = sbt(sD, "dA_sb", [128, 2 * TC])
            xres = sbt(sD, "xres", [128, D])
            ffn_t = sbt(sD, "ffn_t", [128, D])
            out_t = sbt(sD, "out_t", [128, D])

            for d in range(16 if PH >= 5 else 0):
                for k in range(4):
                    off = (d * 4 + k) * 128
                    P.op("gpsimd", lambda g, d=d, k=k, off=off:
                         g.tensor_scalar(
                             out=diag_cv[:, off:off + 128], in0=iden_f[:],
                             scalar1=cw_sb[:, d * 4 + k:d * 4 + k + 1],
                             scalar2=None, op0=ALU.mult),
                         reads=["idf", "cw"], writes=["dgc"])
                P.op("gpsimd", lambda g, d=d: g.tensor_scalar(
                    out=diag_D[:, d * 128:(d + 1) * 128], in0=iden_f[:],
                    scalar1=dpar_sb[:, d:d + 1], scalar2=None, op0=ALU.mult),
                    reads=["idf", "dpar"], writes=["dgD"])

            P.dma("sync", wdt_row[:], w_dt_d, writes=["wdt_row"],
                  fam="wdtl")
            for k in range(16 if PH >= 5 else 0):
                P.dma("gpsimd", woutb[:, k * D:(k + 1) * D],
                      W_out_d[k * 128:(k + 1) * 128, :],
                      writes=["wor"], fam="wo", waw=False)

            h2_pend = []

            def emit_h2(c0):
                for h2 in range(2):
                    r0 = c0 + h2 * 128
                    for k in range(16):
                        def mk_wo1(t, h2=h2, k=k):
                            lhs = y2T[:, k * TC + h2 * 128:
                                      k * TC + h2 * 128 + 128]
                            r = None
                            for mi in range(2):
                                r = t.matmul(
                                    out=p[mi][:, 0:512], lhsT=lhs,
                                    rhs=woutb[:, k * D + mi * 512:
                                              k * D + (mi + 1) * 512],
                                    start=(k == 0), stop=(k == 15))
                            return r
                        P.op("tensor", mk_wo1,
                             reads=["wor"] + [f"y2_{dd}" for dd in range(16)],
                             writes=["pg0", "pg1"])
                    P.dma("sync", xres[:], x_d[r0:r0 + 128, :],
                          writes=["xres"], fam="xrs")
                    P.dma("sync", ffn_t[:], ffn_sc[r0:r0 + 128, :],
                          reads=["Dffn"], writes=["ffnt"], fam="fft")
                    P.op("vector", lambda v: v.tensor_add(
                        out=out_t[:, 0:512], in0=p[0][:, 0:512],
                        in1=xres[:, 0:512]),
                        reads=["pg0", "xres"], writes=["outt"])
                    P.op("vector", lambda v: v.tensor_add(
                        out=out_t[:, 512:1024], in0=p[1][:, 0:512],
                        in1=xres[:, 512:1024]),
                        reads=["pg1", "xres"], writes=["outt"])
                    P.op("vector", lambda v: v.tensor_add(
                        out=out_t[:], in0=out_t[:], in1=ffn_t[:]),
                        reads=["outt", "ffnt"], writes=["outt"])
                    P.dma("sync", out_d[r0:r0 + 128, :], out_t[:],
                          reads=["outt"], fam="otw", waw=False)

            for j in range(NJ if PH >= 5 else 0):
                c0 = j * TC
                jp = j % 2
                sz_b = sz_[jp]
                xc_b = xc_[jp]
                if j == 0:
                    for d in range(16):
                        o = d * 260
                        P.op("gpsimd", lambda g, o=o: g.memset(
                            xin_b[:, o:o + 3], 0.0), writes=["xinh"])
                for d in range(16):
                    o = d * 260
                    if j == 0:
                        P.dma("gpsimd", xin_b[:, o + 3:o + 3 + TC],
                              xz_s[d * 128:(d + 1) * 128, 0:TC],
                              reads=["Dxz"], writes=["xinb"], fam="xing",
                              waw=False)
                    else:
                        P.dma("gpsimd", xin_b[:, o:o + 3 + TC],
                              xz_s[d * 128:(d + 1) * 128, c0 - 3:c0 + TC],
                              reads=["Dxz"], writes=["xinb"], fam="xing",
                              waw=False)
                if DS >= 1:
                    P.dma("gpsimd",
                          sz_b[:].rearrange("p (d t) -> p d t", t=TC),
                          xz_s[DI:2 * DI, c0:c0 + TC]
                          .rearrange("(d p) t -> p d t", p=128),
                          reads=["Dxz"],
                          writes=[f"sz{jp}_{d}" for d in range(16)],
                          fam="ztlg")
                for d in range(16 if DS >= 1 else 0):
                    P.op("scalar", lambda s, d=d, sz_b=sz_b:
                         s.activation(
                             out=sz_b[:, d * TC:(d + 1) * TC],
                             in_=sz_b[:, d * TC:(d + 1) * TC],
                             func=AF.Silu), reads=[f"sz{jp}_{d}"],
                         writes=[f"sz{jp}_{d}"])
                for d in range(16 if DS >= 2 else 0):
                    pq = p[2] if d % 2 == 0 else p[3]
                    pcv = "P2" if d % 2 == 0 else "P3"

                    def mk_cv(t, d=d, pq=pq):
                        r = None
                        for k in range(4):
                            off = (d * 4 + k) * 128
                            r = t.matmul(
                                out=pq[:, 0:TC],
                                lhsT=diag_cv[:, off:off + 128],
                                rhs=xin_b[:, d * 260 + k:d * 260 + k + TC],
                                start=(k == 0), stop=(k == 3))
                        return r
                    P.op("tensor", mk_cv, reads=["dgc", "xinb", "xinh"],
                         writes=[pcv])
                    P.op("scalar", lambda s, d=d, pq=pq, xc_b=xc_b:
                         s.activation(
                             out=xc_b[:, d * TC:(d + 1) * TC],
                             in_=pq[:, 0:TC],
                             func=AF.Silu, bias=cb_sb[:, d:d + 1]),
                         reads=[pcv, "cb"], writes=[f"xc{jp}"])
                if DS < 3:
                    continue

                def mk_ssm(t, xc_b=xc_b, lo=0, pq=None):
                    r = None
                    for d in range(lo, lo + 8):
                        r = t.matmul(
                            out=pq[0:33, 0:TC],
                            lhsT=wxp_b[:, d * 33:(d + 1) * 33],
                            rhs=xc_b[:, d * TC:(d + 1) * TC],
                            start=(d == lo), stop=(d == lo + 7))
                    return r
                import functools
                P.op("tensor", functools.partial(mk_ssm, lo=0, pq=p[2]),
                     reads=["wxp", f"xc{jp}"], writes=["P2"])
                P.op("tensor", functools.partial(mk_ssm, lo=8, pq=p[3]),
                     reads=["wxp", f"xc{jp}"], writes=["P3"])
                P.op("vector", lambda v: v.tensor_copy(
                    out=ssm_f[:], in_=p[2][0:33, 0:TC]),
                    reads=["P2"], writes=["ssmf"])
                P.op("vector", lambda v: v.tensor_add(
                    out=ssm_f[:], in0=ssm_f[:], in1=p[3][0:33, 0:TC]),
                    reads=["ssmf", "P3"], writes=["ssmf"])
                for grp, rbase in (("B", 1), ("C", 17)):
                    P.dma("gpsimd",
                          bc_flat[0:1, :].rearrange("o (n t) -> o n t", t=TC),
                          ssm_f[rbase:rbase + NS, 0:TC].unsqueeze(0)
                          if False else
                          ssm_f[rbase:rbase + NS, 0:TC],
                          reads=["ssmf"], writes=["bcf"], fam="bcf")
                    dst = bcB if grp == "B" else bcC
                    dname = "bcB" if grp == "B" else "bcC"
                    for n in range(NS):
                        pq = p[2] if n % 2 == 0 else p[3]
                        pb = "P2" if n % 2 == 0 else "P3"
                        P.op("tensor", lambda t, n=n, pq=pq: t.matmul(
                            out=pq[:, 0:TC], lhsT=ones_row[:],
                            rhs=bc_flat[0:1, n * TC:(n + 1) * TC],
                            start=True, stop=True),
                            reads=["bcf", "ones"], writes=[pb])
                        P.op("scalar", lambda s, n=n, dst=dst, pq=pq:
                             s.copy(out=dst[:, n * TC:(n + 1) * TC],
                                    in_=pq[:, 0:TC]),
                             reads=[pb], writes=[dname])
                for d in range(16 if DS >= 4 else 0):
                    pq = p[2] if d % 2 == 0 else p[3]
                    pdl = "P2" if d % 2 == 0 else "P3"
                    base = 0 if d % 2 == 0 else TC
                    sl = slice(base, base + TC)
                    P.op("tensor", lambda t, d=d, pq=pq: t.matmul(
                        out=pq[:, 0:TC],
                        lhsT=wdt_row[0:1, d * 128:(d + 1) * 128],
                        rhs=ssm_f[0:1, 0:TC], start=True, stop=True),
                        reads=["ssmf", "wdt_row"], writes=[pdl])
                    P.op("vector", lambda v, d=d, sl=sl, pq=pq:
                         v.tensor_scalar(
                             out=arg_sp[:, sl], in0=pq[:, 0:TC],
                             scalar1=bdt_sb[:, d:d + 1], scalar2=None,
                             op0=ALU.add), reads=[pdl, "bdt"],
                         writes=[f"arg{d % 2}"])
                    P.op("scalar", lambda s, sl=sl: s.activation(
                        out=na_sp[:, sl], in_=arg_sp[:, sl], func=AF.Abs),
                        reads=[f"arg{d % 2}"], writes=[f"na{d % 2}"])
                    P.op("scalar", lambda s, sl=sl: s.activation(
                        out=na_sp[:, sl], in_=na_sp[:, sl], func=AF.Exp,
                        scale=-1.0), reads=[f"na{d % 2}"],
                        writes=[f"na{d % 2}"])
                    P.op("scalar", lambda s, sl=sl: s.activation(
                        out=na_sp[:, sl], in_=na_sp[:, sl], func=AF.Ln,
                        bias=1.0), reads=[f"na{d % 2}"],
                        writes=[f"na{d % 2}"])
                    P.op("vector", lambda v, d=d, sl=sl:
                         v.scalar_tensor_tensor(
                             out=delt[:, d * TC:(d + 1) * TC],
                             in0=arg_sp[:, sl], scalar=0.0, op0=ALU.max,
                             in1=na_sp[:, sl], op1=ALU.add),
                         reads=[f"arg{d % 2}", f"na{d % 2}"],
                         writes=[f"dl{d}"])
                while h2_pend:
                    emit_h2(h2_pend.pop(0))
                y2_pending = []
                for d in range(16 if DS >= 5 else 0):
                    dp = d % 2
                    dsl = delt[:, d * TC:(d + 1) * TC]
                    dbx = dbx_[dp]
                    P.op("gpsimd", lambda g, d=d, xc_b=xc_b:
                         g.tensor_mul(
                             out=u_b[:, d * TC:(d + 1) * TC],
                             in0=delt[:, d * TC:(d + 1) * TC],
                             in1=xc_b[:, d * TC:(d + 1) * TC]),
                         reads=[f"dl{d}", f"xc{jp}"], writes=[f"u{d}"])
                    for n4 in range(4):
                        ub4 = u_b[:, d * TC:(d + 1) * TC] \
                            .unsqueeze(1).broadcast_to([128, 4, TC])
                        P.op("gpsimd", lambda g, n4=n4, dbx=dbx,
                             ub4=ub4: g.tensor_tensor(
                                 out=dbx[:, n4 * 4 * TC:(n4 + 1) * 4 * TC]
                                 .rearrange("p (n t) -> p n t", t=TC),
                                 in0=bcB[:, n4 * 4 * TC:(n4 + 1) * 4 * TC]
                                 .rearrange("p (n t) -> p n t", t=TC),
                                 in1=ub4, op=ALU.mult),
                             reads=["bcB", f"u{d}"], writes=[f"dbx{dp}"])
                    for n in range(NS if DS >= 6 else 0):
                        np_ = n % 2
                        pa = dA_sb[:, np_ * TC:(np_ + 1) * TC]
                        paname = f"dA{np_}"
                        P.op("scalar", lambda s, d=d, n=n, pa=pa, dsl=dsl:
                             s.activation(out=pa, in_=dsl, func=AF.Exp,
                                          scale=A_sb[d][:, n:n + 1]),
                             reads=[f"dl{d}", f"A_{d}"], writes=[paname])
                        if DS < 7:
                            continue
                        P.op("vector", lambda v, d=d, n=n, pa=pa, dbx=dbx:
                             v.tensor_tensor_scan(
                                 out=hbuf[:, n * TC:(n + 1) * TC],
                                 data0=pa,
                                 data1=dbx[:, n * TC:(n + 1) * TC],
                                 initial=hstate[:, d * NS + n:
                                                d * NS + n + 1],
                                 op0=ALU.mult, op1=ALU.add),
                             reads=[paname, f"dbx{dp}", f"hst{d}"],
                             writes=["hbufS"])
                    if DS < 8:
                        continue
                    P.op("vector", lambda v, d=d: v.tensor_copy(
                        out=hstate[:, d * NS:(d + 1) * NS].unsqueeze(2),
                        in_=hbuf[:].rearrange("p (n t) -> p n t",
                                              t=TC)[:, :, TC - 1:TC]),
                        reads=["hbufS"], writes=[f"hst{d}"])
                    P.op("gpsimd", lambda g, dbx=dbx: g.tensor_mul(
                        out=dbx[:], in0=hbuf[:], in1=bcC[:]),
                        reads=["hbufS", "bcC"], writes=[f"dbx{dp}"])
                    py = p[4] if dp == 0 else p[5]
                    pyn = "P4" if dp == 0 else "P5"

                    def mk_y(t, d=d, dbx=dbx, py=py, xc_b=xc_b):
                        r = None
                        for n in range(NS):
                            r = t.matmul(out=py[:, 0:TC],
                                         lhsT=iden_b[:],
                                         rhs=dbx[:, n * TC:(n + 1) * TC],
                                         start=(n == 0), stop=False)
                        r = t.matmul(out=py[:, 0:TC],
                                     lhsT=diag_D[:, d * 128:(d + 1) * 128],
                                     rhs=xc_b[:, d * TC:(d + 1) * TC],
                                     start=False, stop=True)
                        return r
                    P.op("tensor", mk_y,
                         reads=[f"dbx{dp}", "iden_b", "dgD", f"xc{jp}"],
                         writes=[pyn])
                    y2_pending.append((d, py, pyn))
                    if len(y2_pending) > 1:
                        dd, pyd, pynd = y2_pending.pop(0)
                        P.op("vector", lambda v, d=dd, py=pyd, sz_b=sz_b:
                             v.tensor_mul(
                                 out=y2T[:, d * TC:(d + 1) * TC],
                                 in0=py[:, 0:TC],
                                 in1=sz_b[:, d * TC:(d + 1) * TC]),
                             reads=[pynd, f"sz{jp}_{dd}"],
                             writes=[f"y2_{dd}"])
                while y2_pending:
                    dd, pyd, pynd = y2_pending.pop(0)
                    P.op("vector", lambda v, d=dd, py=pyd, sz_b=sz_b:
                         v.tensor_mul(
                             out=y2T[:, d * TC:(d + 1) * TC],
                             in0=py[:, 0:TC],
                             in1=sz_b[:, d * TC:(d + 1) * TC]),
                         reads=[pynd, f"sz{jp}_{dd}"],
                         writes=[f"y2_{dd}"])
                h2_pend.append(c0)
            while h2_pend:
                emit_h2(h2_pend.pop(0))

        P.drain()

        blk = ctx.enter_context(nc.Block())

        def replay(handle, atoms):
            for a in atoms:
                if a[0] == "w":
                    handle.wait_ge(a[1], a[2])
                elif a[0] == "c":
                    instr = a[1](handle)
                    instr.then_inc(a[2], 1)
                elif a[0] == "d":
                    handle.dma_start(
                        out=a[1], in_=a[2],
                        allow_slow_non_contiguous=a[4]).then_inc(a[3], 16)
                elif a[0] == "b":
                    handle.nop().then_inc(a[1], 1)
                    handle.wait_ge(a[1], a[2])

        @blk.vector
        def _(v):
            replay(v, P.atoms["vector"])

        @blk.scalar
        def _(s):
            replay(s, P.atoms["scalar"])

        @blk.tensor
        def _(t):
            replay(t, P.atoms["tensor"])

        @blk.gpsimd
        def _(g):
            replay(g, P.atoms["gpsimd"])

        @blk.sync
        def _(sy_):
            replay(sy_, P.atoms["sync"])

    return nc


_NC_CACHE = [None]


def kernel(**inputs):
    n = 8
    x_full = np.ascontiguousarray(np.asarray(inputs["L0_attn_x"],
                                             dtype=np.float32))
    shards = [np.ascontiguousarray(x_full[i]) for i in range(n)]
    conv_w = np.ascontiguousarray(
        np.asarray(inputs["L0_attn_conv_w"], dtype=np.float32).reshape(DI, 4))
    w_dt = np.ascontiguousarray(
        np.asarray(inputs["L0_attn_w_dt"], dtype=np.float32).reshape(1, DI))
    common = {
        "W_in": np.ascontiguousarray(np.asarray(inputs["L0_attn_W_in"],
                                                dtype=np.float32)),
        "conv_w": conv_w,
        "conv_b": np.ascontiguousarray(np.asarray(inputs["L0_attn_conv_b"],
                                                  dtype=np.float32)),
        "W_xproj": np.ascontiguousarray(np.asarray(inputs["L0_attn_W_xproj"],
                                                   dtype=np.float32)),
        "w_dt": w_dt,
        "b_dt": np.ascontiguousarray(np.asarray(inputs["L0_attn_b_dt"],
                                                dtype=np.float32)),
        "A_log": np.ascontiguousarray(np.asarray(inputs["L0_attn_A_log"],
                                                 dtype=np.float32)),
        "D_param": np.ascontiguousarray(np.asarray(inputs["L0_attn_D_param"],
                                                   dtype=np.float32)),
        "W_out": np.ascontiguousarray(np.asarray(inputs["L0_attn_W_out"],
                                                 dtype=np.float32)),
        "w_gate": np.ascontiguousarray(np.asarray(inputs["L0_ffn_w_gate"],
                                                  dtype=np.float32)),
        "w_up": np.ascontiguousarray(np.asarray(inputs["L0_ffn_w_up"],
                                                dtype=np.float32)),
        "w_down": np.ascontiguousarray(np.asarray(inputs["L0_ffn_w_down"],
                                                  dtype=np.float32)),
    }
    in_maps = [dict(common, x=shards[i]) for i in range(n)]
    if _NC_CACHE[0] is None:
        _NC_CACHE[0] = build_nc()
    res = run_bass_kernel_spmd(_NC_CACHE[0], in_maps, core_ids=list(range(n)))
    out = np.stack([np.asarray(r["out"], dtype=np.float32)
                    for r in res.results], axis=0)
    return out


# version 21
# speedup vs baseline: 1.2125x; 1.2125x over previous
import os
import numpy as np
import concourse.bass as bass
import concourse.mybir as mybir
from concourse.bass_utils import run_bass_kernel_spmd

F32 = mybir.dt.float32
BF16 = mybir.dt.bfloat16
AF = mybir.ActivationFunctionType
ALU = mybir.AluOpType
AX = mybir.AxisListType

S = 2048
PH = int(os.environ.get("ANT_PH", "5"))
DS = int(os.environ.get("ANT_DS", "9"))
D = 1024
DI = 2048
NS = 16
II = 2816
TC = 256
NJ = S // TC
EPS = 1e-5
MIDS = [0.25, 0.75, 1.25, 1.75, 2.5, 3.5, 5.0]
WTS = [0.5, 0.5, 0.5, 0.5, 1.0, 1.0, 2.0]

ENGS = ("vector", "scalar", "tensor", "gpsimd", "sync")


class _Buf:
    __slots__ = ("prod", "readers")

    def __init__(self):
        self.prod = None
        self.readers = []


class _Prog:

    SELF_SYNC = {"vector", "scalar", "tensor", "gpsimd", "sync"}

    def __init__(self, sems, mk_sem):
        self.atoms = {e: [] for e in ENGS}
        self.sems = sems
        self.mk_sem = mk_sem
        self.cnt = {e: 0 for e in ENGS}
        self.marks = {e: {} for e in ENGS}
        self.bufs = {}
        self.fams = {}
        self.nbar = 0
        self.sbar = mk_sem("sbar")

    def buf(self, name):
        if name not in self.bufs:
            self.bufs[name] = _Buf()
        return self.bufs[name]

    def _wait_raw(self, e, sem, val):
        if val <= 0:
            return
        k = id(sem)
        if self.marks[e].get(k, 0) >= val:
            return
        self.atoms[e].append(("w", sem, val))
        self.marks[e][k] = val

    def _wait_tok(self, e, tok):
        if tok is None:
            return
        if tok[0] == "F":
            fam = tok[1]
            sem, cnt, _q = self.fams[fam]
            self._wait_raw(e, sem, cnt)
        else:
            sem, val = tok[1], tok[2]
            if sem is self.sems.get(e) and e not in self.SELF_SYNC:
                return
            self._wait_raw(e, sem, val)

    def _deps(self, e, reads, writes):
        for n in reads:
            b = self.buf(n)
            self._wait_tok(e, b.prod)
        for n in writes:
            b = self.buf(n)
            self._wait_tok(e, b.prod)
            for r in b.readers:
                self._wait_tok(e, r)

    def _finish(self, tok, reads, writes):
        for n in writes:
            b = self.buf(n)
            b.prod = tok
            b.readers = []
        for n in reads:
            self.buf(n).readers.append(tok)

    def op(self, e, fn, reads=(), writes=()):
        self._deps(e, reads, writes)
        self.atoms[e].append(("c", fn, self.sems[e]))
        self.cnt[e] += 1
        self._finish(("E", self.sems[e], self.cnt[e]), reads, writes)

    def dma(self, e, out_ap, in_ap, reads=(), writes=(), fam="gen",
            slow=False, waw=True):
        if fam not in self.fams:
            self.fams[fam] = [self.mk_sem("f_" + fam), 0, e]
        rec = self.fams[fam]
        assert rec[2] == e, f"fam {fam} used on two queues"
        for n in reads:
            self._wait_tok(e, self.buf(n).prod)
        for n in writes:
            b = self.buf(n)
            if waw:
                self._wait_tok(e, b.prod)
            for r in b.readers:
                self._wait_tok(e, r)
        self.atoms[e].append(("d", out_ap, in_ap, rec[0], slow))
        rec[1] += 16
        self._finish(("F", fam), reads, writes)

    def barrier(self):
        self.nbar += 1
        tgt = 5 * self.nbar
        for e in ENGS:
            for fam, (sem, cnt, q) in self.fams.items():
                if q == e and cnt:
                    self._wait_raw(e, sem, cnt)
            self.atoms[e].append(("b", self.sbar, tgt))
            self.marks[e][id(self.sbar)] = tgt
        for b in self.bufs.values():
            b.readers = []

    def drain(self):
        for e in ENGS:
            for fam, (sem, cnt, q) in self.fams.items():
                if q == e and cnt:
                    self._wait_raw(e, sem, cnt)


def build_nc():
    nc = bass.Bass("TRN2", target_bir_lowering=False)

    x_d = nc.dram_tensor("x", [S, D], F32, kind="ExternalInput").ap()
    W_in_d = nc.dram_tensor("W_in", [D, 4 * D], F32, kind="ExternalInput").ap()
    conv_w_d = nc.dram_tensor("conv_w", [DI, 4], F32, kind="ExternalInput").ap()
    conv_b_d = nc.dram_tensor("conv_b", [DI], F32, kind="ExternalInput").ap()
    W_xp_d = nc.dram_tensor("W_xproj", [DI, 33], F32, kind="ExternalInput").ap()
    w_dt_d = nc.dram_tensor("w_dt", [1, DI], F32, kind="ExternalInput").ap()
    b_dt_d = nc.dram_tensor("b_dt", [DI], F32, kind="ExternalInput").ap()
    A_log_d = nc.dram_tensor("A_log", [DI, NS], F32, kind="ExternalInput").ap()
    D_par_d = nc.dram_tensor("D_param", [DI], F32, kind="ExternalInput").ap()
    W_out_d = nc.dram_tensor("W_out", [DI, D], F32, kind="ExternalInput").ap()
    w_g_d = nc.dram_tensor("w_gate", [II, D], F32, kind="ExternalInput").ap()
    w_u_d = nc.dram_tensor("w_up", [II, D], F32, kind="ExternalInput").ap()
    w_d_d = nc.dram_tensor("w_down", [D, II], F32, kind="ExternalInput").ap()
    out_d = nc.dram_tensor("out", [S, D], F32, kind="ExternalOutput").ap()
    xz_s = nc.dram_tensor("xz_s", [4 * D, S], BF16, kind="Internal").ap()
    ffn_sc = nc.dram_tensor("ffn_sc", [S, D], F32, kind="Internal").ap()

    import contextlib
    with contextlib.ExitStack() as ctx:
        def sem(name):
            return ctx.enter_context(nc.semaphore(name))

        sv, ss, st_, sg, sy = (sem(n) for n in ("sv", "ss", "st", "sg", "sy"))

        P = _Prog(
            sems={"vector": sv, "scalar": ss, "tensor": st_, "gpsimd": sg,
                  "sync": sy},
            mk_sem=sem,
        )

        def sbt(stack, name, shape, dt=F32):
            return stack.enter_context(nc.sbuf_tensor(name, shape, dt))

        A_sb = [sbt(ctx, f"A_{d}", [128, NS]) for d in range(16)]
        cb_sb = sbt(ctx, "cb_sb", [128, 16])
        bdt_sb = sbt(ctx, "bdt_sb", [128, 16])
        dpar_sb = sbt(ctx, "dpar_sb", [128, 16])
        cw_sb = sbt(ctx, "cw_sb", [128, 64])
        ones_row = sbt(ctx, "ones_row", [1, 128], BF16)
        iden_f = sbt(ctx, "iden_f", [128, 128])
        iden_b = sbt(ctx, "iden_b", [128, 128], BF16)
        wxp_b = sbt(ctx, "wxp_b", [128, 16 * 33], BF16)
        hstate = sbt(ctx, "hstate", [128, 256])
        sm_a = sbt(ctx, "sm_a", [128, 32])
        sm_b = sbt(ctx, "sm_b", [128, 216])
        sm_c = sbt(ctx, "sm_c", [128, 216])

        p = [ctx.enter_context(nc.psum_tensor(f"p{i}", [128, 512], F32))
             for i in range(6)]
        pT_ = [ctx.enter_context(nc.psum_tensor(f"pT{w}", [128, 1024], BF16))
               for w in range(2)]

        def quantN(tag, src_name, src_ap, dst_name, dst_ap, F,
                   qt, qs, l0, l2, l3, smo):
            nb = F // 16
            smb = sm_b[:, smo:smo + nb]
            smc = sm_c[:, smo:smo + nb]
            P.op("vector", lambda v: v.tensor_reduce(
                out=smb, in_=src_ap.rearrange("p (b w) -> p b w", w=16),
                axis=AX.X, op=ALU.max, apply_absolute_value=True),
                reads=[src_name], writes=["qsc" + tag])
            P.op("vector", lambda v: v.tensor_scalar(
                out=smb, in0=smb, scalar1=1.0 / 6.0,
                scalar2=1e-12, op0=ALU.mult, op1=ALU.max),
                reads=["qsc" + tag], writes=["qsc" + tag])
            P.op("vector", lambda v: v.reciprocal(out=smc, in_=smb),
                 reads=["qsc" + tag], writes=["qrs" + tag])
            rb = smc.unsqueeze(2).broadcast_to([128, nb, 16])
            sb_ = smb.unsqueeze(2).broadcast_to([128, nb, 16])
            P.op("vector", lambda v: v.tensor_tensor(
                out=qt.rearrange("p (b w) -> p b w", w=16),
                in0=src_ap.rearrange("p (b w) -> p b w", w=16),
                in1=rb, op=ALU.mult),
                reads=[src_name, "qrs" + tag], writes=["qt" + tag])
            P.op("scalar", lambda s: s.activation(
                out=qs, in_=qt, func=AF.Square),
                reads=["qt" + tag], writes=["qsq" + tag])
            P.op("vector", lambda v: v.tensor_scalar(
                out=l0, in0=qs, scalar1=MIDS[0] * MIDS[0], scalar2=None,
                op0=ALU.is_gt), reads=["qsq" + tag], writes=["ql0" + tag])
            for i in (1, 2, 3):
                P.op("vector", lambda v, i=i: v.scalar_tensor_tensor(
                    out=l0, in0=qs, scalar=MIDS[i] * MIDS[i],
                    op0=ALU.is_gt, in1=l0, op1=ALU.add),
                    reads=["qsq" + tag, "ql0" + tag], writes=["ql0" + tag])
            P.op("vector", lambda v: v.tensor_scalar(
                out=l2, in0=qs, scalar1=MIDS[4] * MIDS[4], scalar2=None,
                op0=ALU.is_gt), reads=["qsq" + tag], writes=["ql2" + tag])
            P.op("vector", lambda v: v.scalar_tensor_tensor(
                out=l2, in0=qs, scalar=MIDS[5] * MIDS[5],
                op0=ALU.is_gt, in1=l2, op1=ALU.add),
                reads=["qsq" + tag, "ql2" + tag], writes=["ql2" + tag])
            P.op("vector", lambda v: v.tensor_scalar(
                out=l3, in0=qs, scalar1=MIDS[6] * MIDS[6], scalar2=2.0,
                op0=ALU.is_gt, op1=ALU.mult),
                reads=["qsq" + tag], writes=["ql3" + tag])
            P.op("gpsimd", lambda g: g.tensor_add(out=l2, in0=l2, in1=l3),
                 reads=["ql2" + tag, "ql3" + tag], writes=["ql2" + tag])
            P.op("vector", lambda v: v.scalar_tensor_tensor(
                out=l0, in0=l0, scalar=0.5, op0=ALU.mult,
                in1=l2, op1=ALU.add),
                reads=["ql0" + tag, "ql2" + tag], writes=["ql0" + tag])
            P.op("vector", lambda v: v.tensor_scalar(
                out=qs, in0=qt, scalar1=0.0, scalar2=2.0,
                op0=ALU.is_ge, op1=ALU.mult),
                reads=["qt" + tag], writes=["qsq" + tag])
            P.op("vector", lambda v: v.scalar_tensor_tensor(
                out=qs.rearrange("p (b w) -> p b w", w=16),
                in0=qs.rearrange("p (b w) -> p b w", w=16),
                scalar=-1.0, op0=ALU.add, in1=sb_, op1=ALU.mult),
                reads=["qsq" + tag, "qsc" + tag], writes=["qsq" + tag])
            P.op("vector", lambda v: v.tensor_tensor(
                out=dst_ap, in0=l0, in1=qs, op=ALU.mult),
                reads=["ql0" + tag, "qsq" + tag], writes=[dst_name])

        tg_par = [0]

        def transpose_group(src_name, src_ap_fn, n_tiles, dst_fn, dst_name,
                            evac_eng="scalar"):
            done = 0
            while done < n_tiles:
                cnt = min(8, n_tiles - done)
                pg = tg_par[0] & 1
                tg_par[0] += 1
                pT = pT_[pg]

                def mk_tr(t, done=done, cnt=cnt, pT=pT):
                    r = None
                    for kk in range(cnt):
                        r = t.transpose(
                            out=pT[:, kk * 128:(kk + 1) * 128],
                            in_=src_ap_fn(done + kk),
                            identity=iden_b[:])
                    return r
                P.op("tensor", mk_tr, reads=[src_name, "iden_b"],
                     writes=[f"pTx{pg}"])

                def mk_ev(s, done=done, cnt=cnt, pT=pT):
                    r = None
                    for kk in range(cnt):
                        r = s.copy(out=dst_fn(done + kk),
                                   in_=pT[:, kk * 128:(kk + 1) * 128])
                    return r
                P.op(evac_eng, mk_ev, reads=[f"pTx{pg}"], writes=[dst_name])
                done += cnt

        for d in range(16):
            P.dma("sync", A_sb[d][:], A_log_d[d * 128:(d + 1) * 128, :],
                  writes=[f"A_{d}"], fam="cst")
        P.dma("sync", cw_sb[:].rearrange("p (d k) -> p d k", k=4),
              conv_w_d.rearrange("(d p) k -> p d k", p=128), writes=["cw"],
              fam="cst")
        P.dma("sync", cb_sb[:], conv_b_d.rearrange("(d p) -> p d", p=128),
              writes=["cb"], fam="cst", slow=True)
        P.dma("sync", bdt_sb[:], b_dt_d.rearrange("(d p) -> p d", p=128),
              writes=["bdt"], fam="cst", slow=True)
        P.dma("sync", dpar_sb[:], D_par_d.rearrange("(d p) -> p d", p=128),
              writes=["dpar"], fam="cst", slow=True)
        for d in range(16):
            P.op("scalar", lambda s, d=d: s.activation(
                out=A_sb[d][:], in_=A_sb[d][:], func=AF.Exp),
                reads=[f"A_{d}"], writes=[f"A_{d}"])
            P.op("vector", lambda v, d=d: v.tensor_scalar(
                out=A_sb[d][:], in0=A_sb[d][:], scalar1=-1.0, scalar2=None,
                op0=ALU.mult), reads=[f"A_{d}"], writes=[f"A_{d}"])
        P.op("gpsimd", lambda g: g.iota(
            out=iden_f[:], pattern=[[1, 128]], base=0, channel_multiplier=-1,
            allow_small_or_imprecise_dtypes=True), writes=["idf"])
        P.op("gpsimd", lambda g: g.tensor_scalar(
            out=iden_f[:], in0=iden_f[:], scalar1=0.0, scalar2=None,
            op0=ALU.is_equal), reads=["idf"], writes=["idf"])
        P.op("gpsimd", lambda g: g.tensor_copy(out=iden_b[:], in_=iden_f[:]),
             reads=["idf"], writes=["iden_b"])
        P.op("gpsimd", lambda g: g.memset(ones_row[:], 1.0), writes=["ones"])
        P.op("gpsimd", lambda g: g.memset(hstate[:], 0.0),
             writes=[f"hst{d}" for d in range(16)])

        with contextlib.ExitStack() as sAB:
            xnqT_r = sbt(sAB, "xnqT_r", [128, 8 * S], BF16)
            with contextlib.ExitStack() as sA:
                xnT_r = sbt(sA, "xnT_r", [128, 8 * S], BF16)
                winb = sbt(sA, "winb", [128, 8 * 4 * D], BF16)
                qtA = sbt(sA, "qtA", [128, 2 * D])
                qsA = sbt(sA, "qsA", [128, 2 * D])
                l0A = sbt(sA, "l0A", [128, 2 * D], BF16)
                l2A = sbt(sA, "l2A", [128, 2 * D], BF16)
                l3A = sbt(sA, "l3A", [128, 2 * D], BF16)
                QOA = sbt(sA, "QOA", [128, 2 * D], BF16)
                xa_ = [sbt(sA, f"xa{w}", [128, D]) for w in range(2)]
                xnf_ = [sbt(sA, f"xnf{w}", [128, D]) for w in range(2)]
                xnb_ = [sbt(sA, f"xnb{w}", [128, D], BF16) for w in range(2)]
                ev_ = [sbt(sA, f"ev{w}", [128, 512], BF16) for w in range(2)]

                for d in range(16):
                    P.dma("gpsimd", wxp_b[:, d * 33:(d + 1) * 33],
                          W_xp_d[d * 128:(d + 1) * 128, :],
                          writes=["wxp"], fam="wxl", waw=False)
                for k in range(8):
                    P.dma("gpsimd", winb[:, k * 4 * D:(k + 1) * 4 * D],
                          W_in_d[k * 128:(k + 1) * 128, :],
                          writes=["winb"], fam="winl", waw=False)

                for i in range(16 if PH >= 2 else 0):
                    w = i % 2
                    xa = xa_[w]
                    xt = f"xa{w}"
                    sm = (lambda c, w=w: sm_a[:, w * 16 + c:w * 16 + c + 1])
                    P.dma("sync", xa[:], x_d[i * 128:(i + 1) * 128, :],
                          writes=[xt], fam=xt)
                    P.op("scalar", lambda s, xa=xa, w=w: s.activation(
                        out=qsA[:, w * D:(w + 1) * D], in_=xa[:],
                        func=AF.Square,
                        accum_out=sm_a[:, w * 16:w * 16 + 1]), reads=[xt],
                        writes=[f"ln_ss{w}", f"qsqA{w}"])
                    P.op("vector", lambda v, xa=xa, sm=sm: v.tensor_reduce(
                        out=sm(1), in_=xa[:], axis=AX.X, op=ALU.add),
                        reads=[xt], writes=[f"ln_sum{w}"])
                    P.op("vector", lambda v, sm=sm: v.tensor_scalar(
                        out=sm(2), in0=sm(1), scalar1=1.0 / D,
                        scalar2=None, op0=ALU.mult),
                        reads=[f"ln_sum{w}"], writes=[f"ln_mean{w}"])
                    P.op("vector", lambda v, sm=sm: v.tensor_mul(
                        out=sm(3), in0=sm(2), in1=sm(2)),
                        reads=[f"ln_mean{w}"], writes=[f"ln_m2{w}"])
                    P.op("vector", lambda v, sm=sm: v.scalar_tensor_tensor(
                        out=sm(4), in0=sm(0), scalar=1.0 / D,
                        op0=ALU.mult, in1=sm(3), op1=ALU.subtract),
                        reads=[f"ln_ss{w}", f"ln_m2{w}"],
                        writes=[f"ln_var{w}"])
                    P.op("vector", lambda v, sm=sm: v.tensor_scalar(
                        out=sm(4), in0=sm(4), scalar1=EPS,
                        scalar2=None, op0=ALU.add),
                        reads=[f"ln_var{w}"], writes=[f"ln_var{w}"])
                    P.op("scalar", lambda s, sm=sm: s.activation(
                        out=sm(5), in_=sm(4), func=AF.Ln),
                        reads=[f"ln_var{w}"], writes=[f"ln_lv{w}"])
                    P.op("vector", lambda v, sm=sm: v.tensor_scalar(
                        out=sm(5), in0=sm(5), scalar1=-0.5,
                        scalar2=None, op0=ALU.mult),
                        reads=[f"ln_lv{w}"], writes=[f"ln_lv{w}"])
                    P.op("scalar", lambda s, sm=sm: s.activation(
                        out=sm(6), in_=sm(5), func=AF.Exp),
                        reads=[f"ln_lv{w}"], writes=[f"ln_rstd{w}"])
                    P.op("vector", lambda v, xa=xa, w=w, sm=sm:
                         v.tensor_scalar(
                             out=xnf_[w][:], in0=xa[:], scalar1=sm(2),
                             scalar2=sm(6), op0=ALU.subtract, op1=ALU.mult),
                         reads=[xt, f"ln_mean{w}", f"ln_rstd{w}"],
                         writes=[f"xnf{w}"])
                    P.op("vector", lambda v, w=w: v.tensor_copy(
                        out=xnb_[w][:], in_=xnf_[w][:]),
                        reads=[f"xnf{w}"], writes=[f"xnb{w}"])
                    quantN(f"A{w}", f"xnf{w}", xnf_[w][:], f"xnq{w}",
                           QOA[:, w * D:(w + 1) * D], D,
                           qtA[:, w * D:(w + 1) * D],
                           qsA[:, w * D:(w + 1) * D],
                           l0A[:, w * D:(w + 1) * D],
                           l2A[:, w * D:(w + 1) * D],
                           l3A[:, w * D:(w + 1) * D], w * 64)
                    transpose_group(
                        f"xnb{w}",
                        lambda dj, w=w: xnb_[w][:, dj * 128:(dj + 1) * 128],
                        8,
                        lambda dj, i=i: xnT_r[:, dj * S + i * 128:
                                              dj * S + (i + 1) * 128],
                        "xnTr")
                    transpose_group(
                        f"xnq{w}",
                        lambda dj, w=w: QOA[:, w * D + dj * 128:
                                            w * D + (dj + 1) * 128], 8,
                        lambda dj, i=i: xnqT_r[:, dj * S + i * 128:
                                               dj * S + (i + 1) * 128],
                        "xnqTr")
                    if i % 4 == 3 and PH >= 4:
                        tt4 = i // 4
                        for fc in range(32):
                            pb = p[fc % 2]
                            pname = f"pg{fc % 2}"

                            def mk_mm(t, fc=fc, pb=pb, tt4=tt4):
                                r = None
                                for k in range(8):
                                    r = t.matmul(
                                        out=pb[:],
                                        lhsT=winb[:, k * 4 * D + fc * 128:
                                                  k * 4 * D +
                                                  (fc + 1) * 128],
                                        rhs=xnT_r[:, k * S + tt4 * 512:
                                                  k * S + (tt4 + 1) * 512],
                                        start=(k == 0), stop=(k == 7))
                                return r
                            P.op("tensor", mk_mm, reads=["winb", "xnTr"],
                                 writes=[pname])
                            ev = ev_[fc % 2]
                            ename = f"ev{fc % 2}"
                            P.op("scalar", lambda s, ev=ev, pb=pb: s.copy(
                                out=ev[:], in_=pb[:]), reads=[pname],
                                writes=[ename])
                            P.dma("sync",
                                  xz_s[fc * 128:(fc + 1) * 128,
                                       tt4 * 512:(tt4 + 1) * 512],
                                  ev[:], reads=[ename], writes=["Dxz"],
                                  fam="xzw", waw=False)
                P.barrier()

            W2 = 1408
            with contextlib.ExitStack() as sB:
                wgT = sbt(sB, "wgT", [128, 8 * II], BF16)
                wuT = sbt(sB, "wuT", [128, 8 * II], BF16)
                wdT = sbt(sB, "wdT", [128, 22 * D], BF16)
                qtB = sbt(sB, "qtB", [128, W2])
                qsB = sbt(sB, "qsB", [128, W2])
                l0B = sbt(sB, "l0B", [128, W2], BF16)
                l2B = sbt(sB, "l2B", [128, W2], BF16)
                l3B = sbt(sB, "l3B", [128, W2], BF16)
                QOB = sbt(sB, "QOB", [128, W2], BF16)
                wdl = [sbt(sB, "wdl0", [128, W2])] * 2
                hqT1 = sbt(sB, "hqT1", [128, W2], BF16)
                hqb2 = sbt(sB, "hqb2", [128, W2], BF16)
                dne = sbt(sB, "dne", [128, D])

                def q_half(tag, src_name, src_ap, dst_name, dst_ap, F, sl,
                           smo):
                    quantN(tag, src_name, src_ap, dst_name, dst_ap, F,
                           qtB[:, sl], qsB[:, sl], l0B[:, sl], l2B[:, sl],
                           l3B[:, sl], smo)

                for dmc in range(8 if PH >= 3 else 0):
                    for hf in range(2):
                        nm = "wdl0"
                        P.dma("sync", wdl[hf][:],
                              w_d_d[dmc * 128:(dmc + 1) * 128,
                                    hf * W2:(hf + 1) * W2],
                              writes=[nm], fam=nm)
                        q_half("Bh", nm, wdl[hf][:], "wdq",
                               QOB[:, 0:W2], W2, slice(0, W2), 128)
                        transpose_group(
                            "wdq",
                            lambda ic: QOB[:, ic * 128:(ic + 1) * 128], 11,
                            lambda ic, dmc=dmc, hf=hf:
                            wdT[:, (hf * 11 + ic) * D + dmc * 128:
                                (hf * 11 + ic) * D + (dmc + 1) * 128],
                            "wdT")
                P.barrier()
                for (src_w, dstT, nmT) in ((w_g_d, wgT, "wgT"),
                                           (w_u_d, wuT, "wuT")):
                    for it in range(22 if PH >= 3 else 0):
                        w = 0
                        nm = "wdl0"
                        P.dma("sync", wdl[w][:, 0:D],
                              src_w[it * 128:(it + 1) * 128, :],
                              writes=[nm], fam=nm)
                        q_half("Bh", nm, wdl[w][:, 0:D], "wq",
                               QOB[:, 0:D], D, slice(0, D), 128)
                        transpose_group(
                            "wq",
                            lambda dj: QOB[:, dj * 128:(dj + 1) * 128], 8,
                            lambda dj, it=it, dstT=dstT:
                            dstT[:, dj * II + it * 128:
                                 dj * II + (it + 1) * 128],
                            nmT)
                P.barrier()
                sgf = wdl[0]
                widths = (512, 512, 384)
                tails = []

                def emit_tail(rec):
                    ti, hf, par = rec
                    gbase = hf * 11
                    qsrc = QOB if par == 0 else hqb2
                    transpose_group(
                        f"hq{par}",
                        lambda ic, qsrc=qsrc:
                        qsrc[:, ic * 128:(ic + 1) * 128], 11,
                        lambda ic: hqT1[:, ic * 128:(ic + 1) * 128],
                        "hqT1")

                    def mk_down(t, gbase=gbase):
                        r = None
                        for ic in range(11):
                            lhs = hqT1[:, ic * 128:(ic + 1) * 128]
                            gic = gbase + ic
                            for mi in range(2):
                                r = t.matmul(
                                    out=p[mi][:, 0:512], lhsT=lhs,
                                    rhs=wdT[:, gic * D + mi * 512:
                                            gic * D + (mi + 1) * 512],
                                    start=(ic == 0), stop=(ic == 10))
                        return r
                    P.op("tensor", mk_down, reads=["hqT1", "wdT"],
                         writes=["pg0", "pg1"])
                    if hf == 0:
                        P.op("vector", lambda v: v.tensor_copy(
                            out=dne[:, 0:512], in_=p[0][:, 0:512]),
                            reads=["pg0"], writes=["dne"])
                        P.op("vector", lambda v: v.tensor_copy(
                            out=dne[:, 512:1024], in_=p[1][:, 0:512]),
                            reads=["pg1"], writes=["dne"])
                    else:
                        P.op("vector", lambda v: v.tensor_add(
                            out=dne[:, 0:512], in0=dne[:, 0:512],
                            in1=p[0][:, 0:512]),
                            reads=["pg0", "dne"], writes=["dne"])
                        P.op("vector", lambda v: v.tensor_add(
                            out=dne[:, 512:1024], in0=dne[:, 512:1024],
                            in1=p[1][:, 0:512]),
                            reads=["pg1", "dne"], writes=["dne"])
                        P.dma("sync", ffn_sc[ti * 128:(ti + 1) * 128, :],
                              dne[:], reads=["dne"], writes=["Dffn"],
                              fam="ffw", waw=False)

                for i in range(16 if PH >= 3 else 0):
                    for hf in range(2):
                        def mk_gu(t, i=i, hf=hf):
                            r = None
                            for kb in range(8):
                                lhs = xnqT_r[:, kb * S + i * 128:
                                             kb * S + (i + 1) * 128]
                                for gi in range(6):
                                    mi = gi % 3
                                    o0 = sum(widths[:mi])
                                    wT = wgT if gi < 3 else wuT
                                    r = t.matmul(
                                        out=p[(0 if gi < 3 else 3) + mi][
                                            :, 0:widths[mi]],
                                        lhsT=lhs,
                                        rhs=wT[:, kb * II + hf * W2 + o0:
                                               kb * II + hf * W2 + o0 +
                                               widths[mi]],
                                        start=(kb == 0), stop=(kb == 7))
                            return r
                        P.op("tensor", mk_gu,
                             reads=["xnqTr", "wgT", "wuT"],
                             writes=["pg0", "pg1", "pg2",
                                     "pu0", "pu1", "pu2"])

                        def mk_silu(s):
                            r = None
                            for mi in range(3):
                                o0 = sum(widths[:mi])
                                r = s.activation(
                                    out=sgf[:, o0:o0 + widths[mi]],
                                    in_=p[mi][:, 0:widths[mi]],
                                    func=AF.Silu)
                            return r
                        P.op("scalar", mk_silu,
                             reads=["pg0", "pg1", "pg2"], writes=["sgf"])

                        def mk_h(v):
                            r = None
                            for mi in range(3):
                                o0 = sum(widths[:mi])
                                r = v.tensor_mul(
                                    out=qsB[:, o0:o0 + widths[mi]],
                                    in0=sgf[:, o0:o0 + widths[mi]],
                                    in1=p[3 + mi][:, 0:widths[mi]])
                            return r
                        P.op("vector", mk_h,
                             reads=["sgf", "pu0", "pu1", "pu2"],
                             writes=["h_f", "qsqBh"])
                        par = (2 * i + hf) % 2
                        q_half("Bh", "h_f", qsB[:, 0:W2], f"hq{par}",
                               (QOB if par == 0 else hqb2)[:, 0:W2],
                               W2, slice(0, W2), 128)
                        if tails:
                            emit_tail(tails.pop(0))
                        tails.append((i, hf, par))
                while tails:
                    emit_tail(tails.pop(0))
                P.barrier()

        with contextlib.ExitStack() as sD:
            diag_cv = sbt(sD, "diag_cv", [128, 16 * 4 * 128], BF16)
            diag_D = sbt(sD, "diag_D", [128, 16 * 128], BF16)
            woutb = sbt(sD, "woutb", [128, 16 * D], BF16)
            wdt_row = sbt(sD, "wdt_row", [1, DI])
            xin_b = sbt(sD, "xin_b", [128, 16 * 260], BF16)
            sz_ = [sbt(sD, f"sz{w}", [128, 16 * TC], BF16) for w in range(2)]
            xc_ = [sbt(sD, f"xc{w}", [128, 16 * TC], BF16) for w in range(2)]
            delt = sbt(sD, "delt", [128, 16 * TC])
            u_b = sbt(sD, "u_b", [128, 16 * TC], BF16)
            dbx_ = [sbt(sD, f"dbx{w}", [128, NS * TC], BF16)
                    for w in range(2)]
            bcB = sbt(sD, "bcB", [128, NS * TC], BF16)
            bcC = sbt(sD, "bcC", [128, NS * TC], BF16)
            hbuf = sbt(sD, "hbuf", [128, NS * TC], BF16)
            y2T = sbt(sD, "y2T", [128, 16 * TC], BF16)
            ssm_f = sbt(sD, "ssm_f", [33, TC])
            bc_flat = sbt(sD, "bc_flat", [1, 16 * TC], BF16)
            arg_sp = sbt(sD, "arg_sp", [128, 2 * TC])
            na_sp = sbt(sD, "na_sp", [128, 2 * TC])
            dA_sb = sbt(sD, "dA_sb", [128, 4 * TC])
            xres = sbt(sD, "xres", [128, D])
            ffn_t = sbt(sD, "ffn_t", [128, D])
            out_t = sbt(sD, "out_t", [128, D])

            for d in range(16 if PH >= 5 else 0):
                for k in range(4):
                    off = (d * 4 + k) * 128
                    P.op("gpsimd", lambda g, d=d, k=k, off=off:
                         g.tensor_scalar(
                             out=diag_cv[:, off:off + 128], in0=iden_f[:],
                             scalar1=cw_sb[:, d * 4 + k:d * 4 + k + 1],
                             scalar2=None, op0=ALU.mult),
                         reads=["idf", "cw"], writes=["dgc"])
                P.op("gpsimd", lambda g, d=d: g.tensor_scalar(
                    out=diag_D[:, d * 128:(d + 1) * 128], in0=iden_f[:],
                    scalar1=dpar_sb[:, d:d + 1], scalar2=None, op0=ALU.mult),
                    reads=["idf", "dpar"], writes=["dgD"])

            P.dma("sync", wdt_row[:], w_dt_d, writes=["wdt_row"],
                  fam="wdtl")
            for k in range(16 if PH >= 5 else 0):
                P.dma("gpsimd", woutb[:, k * D:(k + 1) * D],
                      W_out_d[k * 128:(k + 1) * 128, :],
                      writes=["wor"], fam="wo", waw=False)

            for j in range(NJ if PH >= 5 else 0):
                c0 = j * TC
                jp = j % 2
                sz_b = sz_[jp]
                xc_b = xc_[jp]
                if j == 0:
                    for d in range(16):
                        o = d * 260
                        P.op("gpsimd", lambda g, o=o: g.memset(
                            xin_b[:, o:o + 3], 0.0), writes=["xinh"])
                for d in range(16):
                    o = d * 260
                    if j == 0:
                        P.dma("gpsimd", xin_b[:, o + 3:o + 3 + TC],
                              xz_s[d * 128:(d + 1) * 128, 0:TC],
                              reads=["Dxz"], writes=["xinb"], fam="xing",
                              waw=False)
                    else:
                        P.dma("gpsimd", xin_b[:, o:o + 3 + TC],
                              xz_s[d * 128:(d + 1) * 128, c0 - 3:c0 + TC],
                              reads=["Dxz"], writes=["xinb"], fam="xing",
                              waw=False)
                if DS >= 1:
                    P.dma("gpsimd",
                          sz_b[:].rearrange("p (d t) -> p d t", t=TC),
                          xz_s[DI:2 * DI, c0:c0 + TC]
                          .rearrange("(d p) t -> p d t", p=128),
                          reads=["Dxz"],
                          writes=[f"sz{jp}_{d}" for d in range(16)],
                          fam="ztlg")
                for d in range(16 if DS >= 1 else 0):
                    P.op("scalar", lambda s, d=d, sz_b=sz_b:
                         s.activation(
                             out=sz_b[:, d * TC:(d + 1) * TC],
                             in_=sz_b[:, d * TC:(d + 1) * TC],
                             func=AF.Silu), reads=[f"sz{jp}_{d}"],
                         writes=[f"sz{jp}_{d}"])
                for d in range(16 if DS >= 2 else 0):
                    pq = p[2] if d % 2 == 0 else p[3]
                    pcv = "P2" if d % 2 == 0 else "P3"

                    def mk_cv(t, d=d, pq=pq):
                        r = None
                        for k in range(4):
                            off = (d * 4 + k) * 128
                            r = t.matmul(
                                out=pq[:, 0:TC],
                                lhsT=diag_cv[:, off:off + 128],
                                rhs=xin_b[:, d * 260 + k:d * 260 + k + TC],
                                start=(k == 0), stop=(k == 3))
                        return r
                    P.op("tensor", mk_cv, reads=["dgc", "xinb", "xinh"],
                         writes=[pcv])
                    P.op("scalar", lambda s, d=d, pq=pq, xc_b=xc_b:
                         s.activation(
                             out=xc_b[:, d * TC:(d + 1) * TC],
                             in_=pq[:, 0:TC],
                             func=AF.Silu, bias=cb_sb[:, d:d + 1]),
                         reads=[pcv, "cb"], writes=[f"xc{jp}"])
                if DS < 3:
                    continue

                def mk_ssm(t, xc_b=xc_b, lo=0, pq=None):
                    r = None
                    for d in range(lo, lo + 8):
                        r = t.matmul(
                            out=pq[0:33, 0:TC],
                            lhsT=wxp_b[:, d * 33:(d + 1) * 33],
                            rhs=xc_b[:, d * TC:(d + 1) * TC],
                            start=(d == lo), stop=(d == lo + 7))
                    return r
                import functools
                P.op("tensor", functools.partial(mk_ssm, lo=0, pq=p[2]),
                     reads=["wxp", f"xc{jp}"], writes=["P2"])
                P.op("tensor", functools.partial(mk_ssm, lo=8, pq=p[3]),
                     reads=["wxp", f"xc{jp}"], writes=["P3"])
                P.op("vector", lambda v: v.tensor_copy(
                    out=ssm_f[:], in_=p[2][0:33, 0:TC]),
                    reads=["P2"], writes=["ssmf"])
                P.op("vector", lambda v: v.tensor_add(
                    out=ssm_f[:], in0=ssm_f[:], in1=p[3][0:33, 0:TC]),
                    reads=["ssmf", "P3"], writes=["ssmf"])
                for grp, rbase in (("B", 1), ("C", 17)):
                    P.dma("gpsimd",
                          bc_flat[0:1, :].rearrange("o (n t) -> o n t", t=TC),
                          ssm_f[rbase:rbase + NS, 0:TC].unsqueeze(0)
                          if False else
                          ssm_f[rbase:rbase + NS, 0:TC],
                          reads=["ssmf"], writes=["bcf"], fam="bcf")
                    dst = bcB if grp == "B" else bcC
                    dname = "bcB" if grp == "B" else "bcC"
                    for n in range(NS):
                        pq = (p[2], p[3], p[4], p[5])[n % 4]
                        pb = ("P2", "P3", "P4", "P5")[n % 4]
                        P.op("tensor", lambda t, n=n, pq=pq: t.matmul(
                            out=pq[:, 0:TC], lhsT=ones_row[:],
                            rhs=bc_flat[0:1, n * TC:(n + 1) * TC],
                            start=True, stop=True),
                            reads=["bcf", "ones"], writes=[pb])
                        P.op("scalar", lambda s, n=n, dst=dst, pq=pq:
                             s.copy(out=dst[:, n * TC:(n + 1) * TC],
                                    in_=pq[:, 0:TC]),
                             reads=[pb], writes=[dname])
                for d in range(16 if DS >= 4 else 0):
                    pq = p[2] if d % 2 == 0 else p[3]
                    pdl = "P2" if d % 2 == 0 else "P3"
                    base = 0 if d % 2 == 0 else TC
                    sl = slice(base, base + TC)
                    P.op("tensor", lambda t, d=d, pq=pq: t.matmul(
                        out=pq[:, 0:TC],
                        lhsT=wdt_row[0:1, d * 128:(d + 1) * 128],
                        rhs=ssm_f[0:1, 0:TC], start=True, stop=True),
                        reads=["ssmf", "wdt_row"], writes=[pdl])
                    P.op("vector", lambda v, d=d, sl=sl, pq=pq:
                         v.tensor_scalar(
                             out=arg_sp[:, sl], in0=pq[:, 0:TC],
                             scalar1=bdt_sb[:, d:d + 1], scalar2=None,
                             op0=ALU.add), reads=[pdl, "bdt"],
                         writes=[f"arg{d % 2}"])
                    P.op("scalar", lambda s, sl=sl: s.activation(
                        out=na_sp[:, sl], in_=arg_sp[:, sl], func=AF.Abs),
                        reads=[f"arg{d % 2}"], writes=[f"na{d % 2}"])
                    P.op("scalar", lambda s, sl=sl: s.activation(
                        out=na_sp[:, sl], in_=na_sp[:, sl], func=AF.Exp,
                        scale=-1.0), reads=[f"na{d % 2}"],
                        writes=[f"na{d % 2}"])
                    P.op("scalar", lambda s, sl=sl: s.activation(
                        out=na_sp[:, sl], in_=na_sp[:, sl], func=AF.Ln,
                        bias=1.0), reads=[f"na{d % 2}"],
                        writes=[f"na{d % 2}"])
                    P.op("vector", lambda v, d=d, sl=sl:
                         v.scalar_tensor_tensor(
                             out=delt[:, d * TC:(d + 1) * TC],
                             in0=arg_sp[:, sl], scalar=0.0, op0=ALU.max,
                             in1=na_sp[:, sl], op1=ALU.add),
                         reads=[f"arg{d % 2}", f"na{d % 2}"],
                         writes=[f"dl{d}"])
                y2_pending = []
                for d in range(16 if DS >= 5 else 0):
                    dp = d % 2
                    dsl = delt[:, d * TC:(d + 1) * TC]
                    dbx = dbx_[dp]
                    P.op("gpsimd", lambda g, d=d, xc_b=xc_b:
                         g.tensor_mul(
                             out=u_b[:, d * TC:(d + 1) * TC],
                             in0=delt[:, d * TC:(d + 1) * TC],
                             in1=xc_b[:, d * TC:(d + 1) * TC]),
                         reads=[f"dl{d}", f"xc{jp}"], writes=[f"u{d}"])
                    for n in range(NS):
                        P.op("gpsimd", lambda g, n=n, d=d, dbx=dbx:
                             g.tensor_mul(
                                 out=dbx[:, n * TC:(n + 1) * TC],
                                 in0=bcB[:, n * TC:(n + 1) * TC],
                                 in1=u_b[:, d * TC:(d + 1) * TC]),
                             reads=["bcB", f"u{d}"], writes=[f"dbx{dp}"])
                    for n in range(NS if DS >= 6 else 0):
                        np_ = n % 4
                        pa = dA_sb[:, np_ * TC:(np_ + 1) * TC]
                        paname = f"dA{np_}"
                        P.op("scalar", lambda s, d=d, n=n, pa=pa, dsl=dsl:
                             s.activation(out=pa, in_=dsl, func=AF.Exp,
                                          scale=A_sb[d][:, n:n + 1]),
                             reads=[f"dl{d}", f"A_{d}"], writes=[paname])
                        if DS < 7:
                            continue
                        P.op("vector", lambda v, d=d, n=n, pa=pa, dbx=dbx:
                             v.tensor_tensor_scan(
                                 out=hbuf[:, n * TC:(n + 1) * TC],
                                 data0=pa,
                                 data1=dbx[:, n * TC:(n + 1) * TC],
                                 initial=hstate[:, d * NS + n:
                                                d * NS + n + 1],
                                 op0=ALU.mult, op1=ALU.add),
                             reads=[paname, f"dbx{dp}", f"hst{d}"],
                             writes=["hbufS"])
                    if DS < 8:
                        continue
                    P.op("vector", lambda v, d=d: v.tensor_copy(
                        out=hstate[:, d * NS:(d + 1) * NS].unsqueeze(2),
                        in_=hbuf[:].rearrange("p (n t) -> p n t",
                                              t=TC)[:, :, TC - 1:TC]),
                        reads=["hbufS"], writes=[f"hst{d}"])
                    P.op("vector", lambda v, dbx=dbx: v.tensor_mul(
                        out=dbx[:], in0=hbuf[:], in1=bcC[:]),
                        reads=["hbufS", "bcC"], writes=[f"dbx{dp}"])
                    py = p[4] if dp == 0 else p[5]
                    pyn = "P4" if dp == 0 else "P5"

                    def mk_y(t, d=d, dbx=dbx, py=py, xc_b=xc_b):
                        r = None
                        for n in range(NS):
                            r = t.matmul(out=py[:, 0:TC],
                                         lhsT=iden_b[:],
                                         rhs=dbx[:, n * TC:(n + 1) * TC],
                                         start=(n == 0), stop=False)
                        r = t.matmul(out=py[:, 0:TC],
                                     lhsT=diag_D[:, d * 128:(d + 1) * 128],
                                     rhs=xc_b[:, d * TC:(d + 1) * TC],
                                     start=False, stop=True)
                        return r
                    P.op("tensor", mk_y,
                         reads=[f"dbx{dp}", "iden_b", "dgD", f"xc{jp}"],
                         writes=[pyn])
                    y2_pending.append((d, py, pyn))
                    if len(y2_pending) > 1:
                        dd, pyd, pynd = y2_pending.pop(0)
                        P.op("vector", lambda v, d=dd, py=pyd, sz_b=sz_b:
                             v.tensor_mul(
                                 out=y2T[:, d * TC:(d + 1) * TC],
                                 in0=py[:, 0:TC],
                                 in1=sz_b[:, d * TC:(d + 1) * TC]),
                             reads=[pynd, f"sz{jp}_{dd}"],
                             writes=[f"y2_{dd}"])
                while y2_pending:
                    dd, pyd, pynd = y2_pending.pop(0)
                    P.op("vector", lambda v, d=dd, py=pyd, sz_b=sz_b:
                         v.tensor_mul(
                             out=y2T[:, d * TC:(d + 1) * TC],
                             in0=py[:, 0:TC],
                             in1=sz_b[:, d * TC:(d + 1) * TC]),
                         reads=[pynd, f"sz{jp}_{dd}"],
                         writes=[f"y2_{dd}"])
                for h2 in range(2 if DS >= 9 else 0):
                    r0 = c0 + h2 * 128
                    op_ = h2 % 2
                    P.dma("sync", xres[:], x_d[r0:r0 + 128, :],
                          writes=["xres"], fam="xrs")
                    P.dma("sync", ffn_t[:], ffn_sc[r0:r0 + 128, :],
                          reads=["Dffn"], writes=["ffnt"], fam="fft")
                    for k in range(16):
                        def mk_wo1(t, h2=h2, k=k):
                            lhs = y2T[:, k * TC + h2 * 128:
                                      k * TC + h2 * 128 + 128]
                            r = None
                            for mi in range(2):
                                r = t.matmul(
                                    out=p[mi][:, 0:512], lhsT=lhs,
                                    rhs=woutb[:, k * D + mi * 512:
                                              k * D + (mi + 1) * 512],
                                    start=(k == 0), stop=(k == 15))
                            return r
                        P.op("tensor", mk_wo1,
                             reads=["wor"] + [f"y2_{dd}" for dd in range(16)],
                             writes=["pg0", "pg1"])
                    P.op("vector", lambda v: v.tensor_add(
                        out=out_t[:, 0:512], in0=p[0][:, 0:512],
                        in1=xres[:, 0:512]),
                        reads=["pg0", "xres"], writes=["outt"])
                    P.op("vector", lambda v: v.tensor_add(
                        out=out_t[:, 512:1024], in0=p[1][:, 0:512],
                        in1=xres[:, 512:1024]),
                        reads=["pg1", "xres"], writes=["outt"])
                    P.op("vector", lambda v: v.tensor_add(
                        out=out_t[:], in0=out_t[:], in1=ffn_t[:]),
                        reads=["outt", "ffnt"], writes=["outt"])
                    P.dma("sync", out_d[r0:r0 + 128, :], out_t[:],
                          reads=["outt"], fam="otw", waw=False)

        P.drain()

        blk = ctx.enter_context(nc.Block())

        def replay(handle, atoms):
            for a in atoms:
                if a[0] == "w":
                    handle.wait_ge(a[1], a[2])
                elif a[0] == "c":
                    instr = a[1](handle)
                    instr.then_inc(a[2], 1)
                elif a[0] == "d":
                    handle.dma_start(
                        out=a[1], in_=a[2],
                        allow_slow_non_contiguous=a[4]).then_inc(a[3], 16)
                elif a[0] == "b":
                    handle.nop().then_inc(a[1], 1)
                    handle.wait_ge(a[1], a[2])

        @blk.vector
        def _(v):
            replay(v, P.atoms["vector"])

        @blk.scalar
        def _(s):
            replay(s, P.atoms["scalar"])

        @blk.tensor
        def _(t):
            replay(t, P.atoms["tensor"])

        @blk.gpsimd
        def _(g):
            replay(g, P.atoms["gpsimd"])

        @blk.sync
        def _(sy_):
            replay(sy_, P.atoms["sync"])

    return nc


_NC_CACHE = [None]


def kernel(**inputs):
    n = 8
    x_full = np.ascontiguousarray(np.asarray(inputs["L0_attn_x"],
                                             dtype=np.float32))
    shards = [np.ascontiguousarray(x_full[i]) for i in range(n)]
    conv_w = np.ascontiguousarray(
        np.asarray(inputs["L0_attn_conv_w"], dtype=np.float32).reshape(DI, 4))
    w_dt = np.ascontiguousarray(
        np.asarray(inputs["L0_attn_w_dt"], dtype=np.float32).reshape(1, DI))
    common = {
        "W_in": np.ascontiguousarray(np.asarray(inputs["L0_attn_W_in"],
                                                dtype=np.float32)),
        "conv_w": conv_w,
        "conv_b": np.ascontiguousarray(np.asarray(inputs["L0_attn_conv_b"],
                                                  dtype=np.float32)),
        "W_xproj": np.ascontiguousarray(np.asarray(inputs["L0_attn_W_xproj"],
                                                   dtype=np.float32)),
        "w_dt": w_dt,
        "b_dt": np.ascontiguousarray(np.asarray(inputs["L0_attn_b_dt"],
                                                dtype=np.float32)),
        "A_log": np.ascontiguousarray(np.asarray(inputs["L0_attn_A_log"],
                                                 dtype=np.float32)),
        "D_param": np.ascontiguousarray(np.asarray(inputs["L0_attn_D_param"],
                                                   dtype=np.float32)),
        "W_out": np.ascontiguousarray(np.asarray(inputs["L0_attn_W_out"],
                                                 dtype=np.float32)),
        "w_gate": np.ascontiguousarray(np.asarray(inputs["L0_ffn_w_gate"],
                                                  dtype=np.float32)),
        "w_up": np.ascontiguousarray(np.asarray(inputs["L0_ffn_w_up"],
                                                dtype=np.float32)),
        "w_down": np.ascontiguousarray(np.asarray(inputs["L0_ffn_w_down"],
                                                  dtype=np.float32)),
    }
    in_maps = [dict(common, x=shards[i]) for i in range(n)]
    if _NC_CACHE[0] is None:
        _NC_CACHE[0] = build_nc()
    res = run_bass_kernel_spmd(_NC_CACHE[0], in_maps, core_ids=list(range(n)))
    out = np.stack([np.asarray(r["out"], dtype=np.float32)
                    for r in res.results], axis=0)
    return out


# version 22
# speedup vs baseline: 1.2172x; 1.0039x over previous
import os
import numpy as np
import concourse.bass as bass
import concourse.mybir as mybir
from concourse.bass_utils import run_bass_kernel_spmd

F32 = mybir.dt.float32
BF16 = mybir.dt.bfloat16
AF = mybir.ActivationFunctionType
ALU = mybir.AluOpType
AX = mybir.AxisListType

S = 2048
PH = int(os.environ.get("ANT_PH", "5"))
DS = int(os.environ.get("ANT_DS", "9"))
D = 1024
DI = 2048
NS = 16
II = 2816
TC = 256
NJ = S // TC
EPS = 1e-5
MIDS = [0.25, 0.75, 1.25, 1.75, 2.5, 3.5, 5.0]
WTS = [0.5, 0.5, 0.5, 0.5, 1.0, 1.0, 2.0]

ENGS = ("vector", "scalar", "tensor", "gpsimd", "sync")


class _Buf:
    __slots__ = ("prod", "readers")

    def __init__(self):
        self.prod = None
        self.readers = []


class _Prog:

    SELF_SYNC = {"vector", "scalar", "tensor", "gpsimd", "sync"}

    def __init__(self, sems, mk_sem):
        self.atoms = {e: [] for e in ENGS}
        self.sems = sems
        self.mk_sem = mk_sem
        self.cnt = {e: 0 for e in ENGS}
        self.marks = {e: {} for e in ENGS}
        self.bufs = {}
        self.fams = {}
        self.nbar = 0
        self.sbar = mk_sem("sbar")

    def buf(self, name):
        if name not in self.bufs:
            self.bufs[name] = _Buf()
        return self.bufs[name]

    def _wait_raw(self, e, sem, val):
        if val <= 0:
            return
        k = id(sem)
        if self.marks[e].get(k, 0) >= val:
            return
        self.atoms[e].append(("w", sem, val))
        self.marks[e][k] = val

    def _wait_tok(self, e, tok):
        if tok is None:
            return
        if tok[0] == "F":
            fam = tok[1]
            sem, cnt, _q = self.fams[fam]
            self._wait_raw(e, sem, cnt)
        else:
            sem, val = tok[1], tok[2]
            if sem is self.sems.get(e) and e not in self.SELF_SYNC:
                return
            self._wait_raw(e, sem, val)

    def _deps(self, e, reads, writes):
        for n in reads:
            b = self.buf(n)
            self._wait_tok(e, b.prod)
        for n in writes:
            b = self.buf(n)
            self._wait_tok(e, b.prod)
            for r in b.readers:
                self._wait_tok(e, r)

    def _finish(self, tok, reads, writes):
        for n in writes:
            b = self.buf(n)
            b.prod = tok
            b.readers = []
        for n in reads:
            self.buf(n).readers.append(tok)

    def op(self, e, fn, reads=(), writes=()):
        self._deps(e, reads, writes)
        self.atoms[e].append(("c", fn, self.sems[e]))
        self.cnt[e] += 1
        self._finish(("E", self.sems[e], self.cnt[e]), reads, writes)

    def dma(self, e, out_ap, in_ap, reads=(), writes=(), fam="gen",
            slow=False, waw=True):
        if fam not in self.fams:
            self.fams[fam] = [self.mk_sem("f_" + fam), 0, e]
        rec = self.fams[fam]
        assert rec[2] == e, f"fam {fam} used on two queues"
        for n in reads:
            self._wait_tok(e, self.buf(n).prod)
        for n in writes:
            b = self.buf(n)
            if waw:
                self._wait_tok(e, b.prod)
            for r in b.readers:
                self._wait_tok(e, r)
        self.atoms[e].append(("d", out_ap, in_ap, rec[0], slow))
        rec[1] += 16
        self._finish(("F", fam), reads, writes)

    def barrier(self):
        self.nbar += 1
        tgt = 5 * self.nbar
        for e in ENGS:
            for fam, (sem, cnt, q) in self.fams.items():
                if q == e and cnt:
                    self._wait_raw(e, sem, cnt)
            self.atoms[e].append(("b", self.sbar, tgt))
            self.marks[e][id(self.sbar)] = tgt
        for b in self.bufs.values():
            b.readers = []

    def drain(self):
        for e in ENGS:
            for fam, (sem, cnt, q) in self.fams.items():
                if q == e and cnt:
                    self._wait_raw(e, sem, cnt)


def build_nc():
    nc = bass.Bass("TRN2", target_bir_lowering=False)

    x_d = nc.dram_tensor("x", [S, D], F32, kind="ExternalInput").ap()
    W_in_d = nc.dram_tensor("W_in", [D, 4 * D], F32, kind="ExternalInput").ap()
    conv_w_d = nc.dram_tensor("conv_w", [DI, 4], F32, kind="ExternalInput").ap()
    conv_b_d = nc.dram_tensor("conv_b", [DI], F32, kind="ExternalInput").ap()
    W_xp_d = nc.dram_tensor("W_xproj", [DI, 33], F32, kind="ExternalInput").ap()
    w_dt_d = nc.dram_tensor("w_dt", [1, DI], F32, kind="ExternalInput").ap()
    b_dt_d = nc.dram_tensor("b_dt", [DI], F32, kind="ExternalInput").ap()
    A_log_d = nc.dram_tensor("A_log", [DI, NS], F32, kind="ExternalInput").ap()
    D_par_d = nc.dram_tensor("D_param", [DI], F32, kind="ExternalInput").ap()
    W_out_d = nc.dram_tensor("W_out", [DI, D], F32, kind="ExternalInput").ap()
    w_g_d = nc.dram_tensor("w_gate", [II, D], F32, kind="ExternalInput").ap()
    w_u_d = nc.dram_tensor("w_up", [II, D], F32, kind="ExternalInput").ap()
    w_d_d = nc.dram_tensor("w_down", [D, II], F32, kind="ExternalInput").ap()
    out_d = nc.dram_tensor("out", [S, D], F32, kind="ExternalOutput").ap()
    xz_s = nc.dram_tensor("xz_s", [4 * D, S], BF16, kind="Internal").ap()
    ffn_sc = nc.dram_tensor("ffn_sc", [S, D], F32, kind="Internal").ap()

    import contextlib
    with contextlib.ExitStack() as ctx:
        def sem(name):
            return ctx.enter_context(nc.semaphore(name))

        sv, ss, st_, sg, sy = (sem(n) for n in ("sv", "ss", "st", "sg", "sy"))

        P = _Prog(
            sems={"vector": sv, "scalar": ss, "tensor": st_, "gpsimd": sg,
                  "sync": sy},
            mk_sem=sem,
        )

        def sbt(stack, name, shape, dt=F32):
            return stack.enter_context(nc.sbuf_tensor(name, shape, dt))

        A_sb = [sbt(ctx, f"A_{d}", [128, NS]) for d in range(16)]
        cb_sb = sbt(ctx, "cb_sb", [128, 16])
        bdt_sb = sbt(ctx, "bdt_sb", [128, 16])
        dpar_sb = sbt(ctx, "dpar_sb", [128, 16])
        cw_sb = sbt(ctx, "cw_sb", [128, 64])
        ones_row = sbt(ctx, "ones_row", [1, 128], BF16)
        iden_f = sbt(ctx, "iden_f", [128, 128])
        iden_b = sbt(ctx, "iden_b", [128, 128], BF16)
        wxp_b = sbt(ctx, "wxp_b", [128, 16 * 33], BF16)
        hstate = sbt(ctx, "hstate", [128, 256])
        sm_a = sbt(ctx, "sm_a", [128, 32])
        sm_b = sbt(ctx, "sm_b", [128, 216])
        sm_c = sbt(ctx, "sm_c", [128, 216])

        p = [ctx.enter_context(nc.psum_tensor(f"p{i}", [128, 512], F32))
             for i in range(6)]
        pT_ = [ctx.enter_context(nc.psum_tensor(f"pT{w}", [128, 1024], BF16))
               for w in range(2)]

        def quantN(tag, src_name, src_ap, dst_name, dst_ap, F,
                   qt, qs, l0, l2, l3, smo):
            nb = F // 16
            smb = sm_b[:, smo:smo + nb]
            smc = sm_c[:, smo:smo + nb]
            P.op("vector", lambda v: v.tensor_reduce(
                out=smb, in_=src_ap.rearrange("p (b w) -> p b w", w=16),
                axis=AX.X, op=ALU.max, apply_absolute_value=True),
                reads=[src_name], writes=["qsc" + tag])
            P.op("vector", lambda v: v.tensor_scalar(
                out=smb, in0=smb, scalar1=1.0 / 6.0,
                scalar2=1e-12, op0=ALU.mult, op1=ALU.max),
                reads=["qsc" + tag], writes=["qsc" + tag])
            P.op("vector", lambda v: v.reciprocal(out=smc, in_=smb),
                 reads=["qsc" + tag], writes=["qrs" + tag])
            rb = smc.unsqueeze(2).broadcast_to([128, nb, 16])
            sb_ = smb.unsqueeze(2).broadcast_to([128, nb, 16])
            P.op("vector", lambda v: v.tensor_tensor(
                out=qt.rearrange("p (b w) -> p b w", w=16),
                in0=src_ap.rearrange("p (b w) -> p b w", w=16),
                in1=rb, op=ALU.mult),
                reads=[src_name, "qrs" + tag], writes=["qt" + tag])
            P.op("scalar", lambda s: s.activation(
                out=qs, in_=qt, func=AF.Square),
                reads=["qt" + tag], writes=["qsq" + tag])
            P.op("vector", lambda v: v.tensor_scalar(
                out=l0, in0=qs, scalar1=MIDS[0] * MIDS[0], scalar2=None,
                op0=ALU.is_gt), reads=["qsq" + tag], writes=["ql0" + tag])
            for i in (1, 2, 3):
                P.op("vector", lambda v, i=i: v.scalar_tensor_tensor(
                    out=l0, in0=qs, scalar=MIDS[i] * MIDS[i],
                    op0=ALU.is_gt, in1=l0, op1=ALU.add),
                    reads=["qsq" + tag, "ql0" + tag], writes=["ql0" + tag])
            P.op("vector", lambda v: v.tensor_scalar(
                out=l2, in0=qs, scalar1=MIDS[4] * MIDS[4], scalar2=None,
                op0=ALU.is_gt), reads=["qsq" + tag], writes=["ql2" + tag])
            P.op("vector", lambda v: v.scalar_tensor_tensor(
                out=l2, in0=qs, scalar=MIDS[5] * MIDS[5],
                op0=ALU.is_gt, in1=l2, op1=ALU.add),
                reads=["qsq" + tag, "ql2" + tag], writes=["ql2" + tag])
            P.op("vector", lambda v: v.tensor_scalar(
                out=l3, in0=qs, scalar1=MIDS[6] * MIDS[6], scalar2=2.0,
                op0=ALU.is_gt, op1=ALU.mult),
                reads=["qsq" + tag], writes=["ql3" + tag])
            P.op("gpsimd", lambda g: g.tensor_add(out=l2, in0=l2, in1=l3),
                 reads=["ql2" + tag, "ql3" + tag], writes=["ql2" + tag])
            P.op("vector", lambda v: v.scalar_tensor_tensor(
                out=l0, in0=l0, scalar=0.5, op0=ALU.mult,
                in1=l2, op1=ALU.add),
                reads=["ql0" + tag, "ql2" + tag], writes=["ql0" + tag])
            P.op("vector", lambda v: v.tensor_scalar(
                out=qs, in0=qt, scalar1=0.0, scalar2=2.0,
                op0=ALU.is_ge, op1=ALU.mult),
                reads=["qt" + tag], writes=["qsq" + tag])
            P.op("vector", lambda v: v.scalar_tensor_tensor(
                out=qs.rearrange("p (b w) -> p b w", w=16),
                in0=qs.rearrange("p (b w) -> p b w", w=16),
                scalar=-1.0, op0=ALU.add, in1=sb_, op1=ALU.mult),
                reads=["qsq" + tag, "qsc" + tag], writes=["qsq" + tag])
            P.op("vector", lambda v: v.tensor_tensor(
                out=dst_ap, in0=l0, in1=qs, op=ALU.mult),
                reads=["ql0" + tag, "qsq" + tag], writes=[dst_name])

        tg_par = [0]

        def transpose_group(src_name, src_ap_fn, n_tiles, dst_fn, dst_name,
                            evac_eng="scalar"):
            done = 0
            while done < n_tiles:
                cnt = min(8, n_tiles - done)
                pg = tg_par[0] & 1
                tg_par[0] += 1
                pT = pT_[pg]

                def mk_tr(t, done=done, cnt=cnt, pT=pT):
                    r = None
                    for kk in range(cnt):
                        r = t.transpose(
                            out=pT[:, kk * 128:(kk + 1) * 128],
                            in_=src_ap_fn(done + kk),
                            identity=iden_b[:])
                    return r
                P.op("tensor", mk_tr, reads=[src_name, "iden_b"],
                     writes=[f"pTx{pg}"])

                def mk_ev(s, done=done, cnt=cnt, pT=pT):
                    r = None
                    for kk in range(cnt):
                        r = s.copy(out=dst_fn(done + kk),
                                   in_=pT[:, kk * 128:(kk + 1) * 128])
                    return r
                P.op(evac_eng, mk_ev, reads=[f"pTx{pg}"], writes=[dst_name])
                done += cnt

        for d in range(16):
            P.dma("sync", A_sb[d][:], A_log_d[d * 128:(d + 1) * 128, :],
                  writes=[f"A_{d}"], fam="cst")
        P.dma("sync", cw_sb[:].rearrange("p (d k) -> p d k", k=4),
              conv_w_d.rearrange("(d p) k -> p d k", p=128), writes=["cw"],
              fam="cst")
        P.dma("sync", cb_sb[:], conv_b_d.rearrange("(d p) -> p d", p=128),
              writes=["cb"], fam="cst", slow=True)
        P.dma("sync", bdt_sb[:], b_dt_d.rearrange("(d p) -> p d", p=128),
              writes=["bdt"], fam="cst", slow=True)
        P.dma("sync", dpar_sb[:], D_par_d.rearrange("(d p) -> p d", p=128),
              writes=["dpar"], fam="cst", slow=True)
        for d in range(16):
            P.op("scalar", lambda s, d=d: s.activation(
                out=A_sb[d][:], in_=A_sb[d][:], func=AF.Exp),
                reads=[f"A_{d}"], writes=[f"A_{d}"])
            P.op("vector", lambda v, d=d: v.tensor_scalar(
                out=A_sb[d][:], in0=A_sb[d][:], scalar1=-1.0, scalar2=None,
                op0=ALU.mult), reads=[f"A_{d}"], writes=[f"A_{d}"])
        P.op("gpsimd", lambda g: g.iota(
            out=iden_f[:], pattern=[[1, 128]], base=0, channel_multiplier=-1,
            allow_small_or_imprecise_dtypes=True), writes=["idf"])
        P.op("gpsimd", lambda g: g.tensor_scalar(
            out=iden_f[:], in0=iden_f[:], scalar1=0.0, scalar2=None,
            op0=ALU.is_equal), reads=["idf"], writes=["idf"])
        P.op("gpsimd", lambda g: g.tensor_copy(out=iden_b[:], in_=iden_f[:]),
             reads=["idf"], writes=["iden_b"])
        P.op("gpsimd", lambda g: g.memset(ones_row[:], 1.0), writes=["ones"])
        P.op("gpsimd", lambda g: g.memset(hstate[:], 0.0),
             writes=[f"hst{d}" for d in range(16)])

        with contextlib.ExitStack() as sAB:
            xnqT_r = sbt(sAB, "xnqT_r", [128, 8 * S], BF16)
            with contextlib.ExitStack() as sA:
                xnT_r = sbt(sA, "xnT_r", [128, 8 * S], BF16)
                winb = sbt(sA, "winb", [128, 8 * 4 * D], BF16)
                qtA = sbt(sA, "qtA", [128, 2 * D])
                qsA = sbt(sA, "qsA", [128, 2 * D])
                l0A = sbt(sA, "l0A", [128, 2 * D], BF16)
                l2A = sbt(sA, "l2A", [128, 2 * D], BF16)
                l3A = sbt(sA, "l3A", [128, 2 * D], BF16)
                QOA = sbt(sA, "QOA", [128, 2 * D], BF16)
                xa_ = [sbt(sA, f"xa{w}", [128, D]) for w in range(2)]
                xnf_ = [sbt(sA, f"xnf{w}", [128, D]) for w in range(2)]
                xnb_ = [sbt(sA, f"xnb{w}", [128, D], BF16) for w in range(2)]
                ev_ = [sbt(sA, f"ev{w}", [128, 512], BF16) for w in range(2)]

                for d in range(16):
                    P.dma("gpsimd", wxp_b[:, d * 33:(d + 1) * 33],
                          W_xp_d[d * 128:(d + 1) * 128, :],
                          writes=["wxp"], fam="wxl", waw=False)
                for k in range(8):
                    P.dma("gpsimd", winb[:, k * 4 * D:(k + 1) * 4 * D],
                          W_in_d[k * 128:(k + 1) * 128, :],
                          writes=["winb"], fam="winl", waw=False)

                for i in range(16 if PH >= 2 else 0):
                    w = i % 2
                    xa = xa_[w]
                    xt = f"xa{w}"
                    sm = (lambda c, w=w: sm_a[:, w * 16 + c:w * 16 + c + 1])
                    P.dma("sync", xa[:], x_d[i * 128:(i + 1) * 128, :],
                          writes=[xt], fam=xt)
                    P.op("scalar", lambda s, xa=xa, w=w: s.activation(
                        out=qsA[:, w * D:(w + 1) * D], in_=xa[:],
                        func=AF.Square,
                        accum_out=sm_a[:, w * 16:w * 16 + 1]), reads=[xt],
                        writes=[f"ln_ss{w}", f"qsqA{w}"])
                    P.op("vector", lambda v, xa=xa, sm=sm: v.tensor_reduce(
                        out=sm(1), in_=xa[:], axis=AX.X, op=ALU.add),
                        reads=[xt], writes=[f"ln_sum{w}"])
                    P.op("vector", lambda v, sm=sm: v.tensor_scalar(
                        out=sm(2), in0=sm(1), scalar1=1.0 / D,
                        scalar2=None, op0=ALU.mult),
                        reads=[f"ln_sum{w}"], writes=[f"ln_mean{w}"])
                    P.op("vector", lambda v, sm=sm: v.tensor_mul(
                        out=sm(3), in0=sm(2), in1=sm(2)),
                        reads=[f"ln_mean{w}"], writes=[f"ln_m2{w}"])
                    P.op("vector", lambda v, sm=sm: v.scalar_tensor_tensor(
                        out=sm(4), in0=sm(0), scalar=1.0 / D,
                        op0=ALU.mult, in1=sm(3), op1=ALU.subtract),
                        reads=[f"ln_ss{w}", f"ln_m2{w}"],
                        writes=[f"ln_var{w}"])
                    P.op("vector", lambda v, sm=sm: v.tensor_scalar(
                        out=sm(4), in0=sm(4), scalar1=EPS,
                        scalar2=None, op0=ALU.add),
                        reads=[f"ln_var{w}"], writes=[f"ln_var{w}"])
                    P.op("scalar", lambda s, sm=sm: s.activation(
                        out=sm(5), in_=sm(4), func=AF.Ln),
                        reads=[f"ln_var{w}"], writes=[f"ln_lv{w}"])
                    P.op("vector", lambda v, sm=sm: v.tensor_scalar(
                        out=sm(5), in0=sm(5), scalar1=-0.5,
                        scalar2=None, op0=ALU.mult),
                        reads=[f"ln_lv{w}"], writes=[f"ln_lv{w}"])
                    P.op("scalar", lambda s, sm=sm: s.activation(
                        out=sm(6), in_=sm(5), func=AF.Exp),
                        reads=[f"ln_lv{w}"], writes=[f"ln_rstd{w}"])
                    P.op("vector", lambda v, xa=xa, w=w, sm=sm:
                         v.tensor_scalar(
                             out=xnf_[w][:], in0=xa[:], scalar1=sm(2),
                             scalar2=sm(6), op0=ALU.subtract, op1=ALU.mult),
                         reads=[xt, f"ln_mean{w}", f"ln_rstd{w}"],
                         writes=[f"xnf{w}"])
                    P.op("vector", lambda v, w=w: v.tensor_copy(
                        out=xnb_[w][:], in_=xnf_[w][:]),
                        reads=[f"xnf{w}"], writes=[f"xnb{w}"])
                    quantN(f"A{w}", f"xnf{w}", xnf_[w][:], f"xnq{w}",
                           QOA[:, w * D:(w + 1) * D], D,
                           qtA[:, w * D:(w + 1) * D],
                           qsA[:, w * D:(w + 1) * D],
                           l0A[:, w * D:(w + 1) * D],
                           l2A[:, w * D:(w + 1) * D],
                           l3A[:, w * D:(w + 1) * D], w * 64)
                    transpose_group(
                        f"xnb{w}",
                        lambda dj, w=w: xnb_[w][:, dj * 128:(dj + 1) * 128],
                        8,
                        lambda dj, i=i: xnT_r[:, dj * S + i * 128:
                                              dj * S + (i + 1) * 128],
                        "xnTr")
                    transpose_group(
                        f"xnq{w}",
                        lambda dj, w=w: QOA[:, w * D + dj * 128:
                                            w * D + (dj + 1) * 128], 8,
                        lambda dj, i=i: xnqT_r[:, dj * S + i * 128:
                                               dj * S + (i + 1) * 128],
                        "xnqTr")
                    if i % 4 == 3 and PH >= 4:
                        tt4 = i // 4
                        for fc in range(32):
                            pb = p[fc % 2]
                            pname = f"pg{fc % 2}"

                            def mk_mm(t, fc=fc, pb=pb, tt4=tt4):
                                r = None
                                for k in range(8):
                                    r = t.matmul(
                                        out=pb[:],
                                        lhsT=winb[:, k * 4 * D + fc * 128:
                                                  k * 4 * D +
                                                  (fc + 1) * 128],
                                        rhs=xnT_r[:, k * S + tt4 * 512:
                                                  k * S + (tt4 + 1) * 512],
                                        start=(k == 0), stop=(k == 7))
                                return r
                            P.op("tensor", mk_mm, reads=["winb", "xnTr"],
                                 writes=[pname])
                            ev = ev_[fc % 2]
                            ename = f"ev{fc % 2}"
                            P.op("scalar", lambda s, ev=ev, pb=pb: s.copy(
                                out=ev[:], in_=pb[:]), reads=[pname],
                                writes=[ename])
                            P.dma("sync",
                                  xz_s[fc * 128:(fc + 1) * 128,
                                       tt4 * 512:(tt4 + 1) * 512],
                                  ev[:], reads=[ename], writes=["Dxz"],
                                  fam="xzw", waw=False)
                P.barrier()

            W2 = 1408
            with contextlib.ExitStack() as sB:
                wgT = sbt(sB, "wgT", [128, 8 * II], BF16)
                wuT = sbt(sB, "wuT", [128, 8 * II], BF16)
                wdT = sbt(sB, "wdT", [128, 22 * D], BF16)
                qtB = sbt(sB, "qtB", [128, W2])
                qsB = sbt(sB, "qsB", [128, W2])
                l0B = sbt(sB, "l0B", [128, W2], BF16)
                l2B = sbt(sB, "l2B", [128, W2], BF16)
                l3B = sbt(sB, "l3B", [128, W2], BF16)
                QOB = sbt(sB, "QOB", [128, W2], BF16)
                wdl = [sbt(sB, "wdl0", [128, W2])] * 2
                hqT1 = sbt(sB, "hqT1", [128, W2], BF16)
                hqb2 = sbt(sB, "hqb2", [128, W2], BF16)
                dne = sbt(sB, "dne", [128, D])

                def q_half(tag, src_name, src_ap, dst_name, dst_ap, F, sl,
                           smo):
                    quantN(tag, src_name, src_ap, dst_name, dst_ap, F,
                           qtB[:, sl], qsB[:, sl], l0B[:, sl], l2B[:, sl],
                           l3B[:, sl], smo)

                for dmc in range(8 if PH >= 3 else 0):
                    for hf in range(2):
                        nm = "wdl0"
                        P.dma("sync", wdl[hf][:],
                              w_d_d[dmc * 128:(dmc + 1) * 128,
                                    hf * W2:(hf + 1) * W2],
                              writes=[nm], fam=nm)
                        q_half("Bh", nm, wdl[hf][:], "wdq",
                               QOB[:, 0:W2], W2, slice(0, W2), 128)
                        transpose_group(
                            "wdq",
                            lambda ic: QOB[:, ic * 128:(ic + 1) * 128], 11,
                            lambda ic, dmc=dmc, hf=hf:
                            wdT[:, (hf * 11 + ic) * D + dmc * 128:
                                (hf * 11 + ic) * D + (dmc + 1) * 128],
                            "wdT")
                P.barrier()
                for (src_w, dstT, nmT) in ((w_g_d, wgT, "wgT"),
                                           (w_u_d, wuT, "wuT")):
                    for it in range(22 if PH >= 3 else 0):
                        w = 0
                        nm = "wdl0"
                        P.dma("sync", wdl[w][:, 0:D],
                              src_w[it * 128:(it + 1) * 128, :],
                              writes=[nm], fam=nm)
                        q_half("Bh", nm, wdl[w][:, 0:D], "wq",
                               QOB[:, 0:D], D, slice(0, D), 128)
                        transpose_group(
                            "wq",
                            lambda dj: QOB[:, dj * 128:(dj + 1) * 128], 8,
                            lambda dj, it=it, dstT=dstT:
                            dstT[:, dj * II + it * 128:
                                 dj * II + (it + 1) * 128],
                            nmT)
                P.barrier()
                sgf = wdl[0]
                widths = (512, 512, 384)
                tails = []

                def emit_tail(rec):
                    ti, hf, par = rec
                    gbase = hf * 11
                    qsrc = QOB if par == 0 else hqb2
                    transpose_group(
                        f"hq{par}",
                        lambda ic, qsrc=qsrc:
                        qsrc[:, ic * 128:(ic + 1) * 128], 11,
                        lambda ic: hqT1[:, ic * 128:(ic + 1) * 128],
                        "hqT1")

                    def mk_down(t, gbase=gbase):
                        r = None
                        for ic in range(11):
                            lhs = hqT1[:, ic * 128:(ic + 1) * 128]
                            gic = gbase + ic
                            for mi in range(2):
                                r = t.matmul(
                                    out=p[mi][:, 0:512], lhsT=lhs,
                                    rhs=wdT[:, gic * D + mi * 512:
                                            gic * D + (mi + 1) * 512],
                                    start=(ic == 0), stop=(ic == 10))
                        return r
                    P.op("tensor", mk_down, reads=["hqT1", "wdT"],
                         writes=["pg0", "pg1"])
                    if hf == 0:
                        P.op("vector", lambda v: v.tensor_copy(
                            out=dne[:, 0:512], in_=p[0][:, 0:512]),
                            reads=["pg0"], writes=["dne"])
                        P.op("vector", lambda v: v.tensor_copy(
                            out=dne[:, 512:1024], in_=p[1][:, 0:512]),
                            reads=["pg1"], writes=["dne"])
                    else:
                        P.op("vector", lambda v: v.tensor_add(
                            out=dne[:, 0:512], in0=dne[:, 0:512],
                            in1=p[0][:, 0:512]),
                            reads=["pg0", "dne"], writes=["dne"])
                        P.op("vector", lambda v: v.tensor_add(
                            out=dne[:, 512:1024], in0=dne[:, 512:1024],
                            in1=p[1][:, 0:512]),
                            reads=["pg1", "dne"], writes=["dne"])
                        P.dma("sync", ffn_sc[ti * 128:(ti + 1) * 128, :],
                              dne[:], reads=["dne"], writes=["Dffn"],
                              fam="ffw", waw=False)

                for i in range(16 if PH >= 3 else 0):
                    for hf in range(2):
                        def mk_gu(t, i=i, hf=hf):
                            r = None
                            for kb in range(8):
                                lhs = xnqT_r[:, kb * S + i * 128:
                                             kb * S + (i + 1) * 128]
                                for gi in range(6):
                                    mi = gi % 3
                                    o0 = sum(widths[:mi])
                                    wT = wgT if gi < 3 else wuT
                                    r = t.matmul(
                                        out=p[(0 if gi < 3 else 3) + mi][
                                            :, 0:widths[mi]],
                                        lhsT=lhs,
                                        rhs=wT[:, kb * II + hf * W2 + o0:
                                               kb * II + hf * W2 + o0 +
                                               widths[mi]],
                                        start=(kb == 0), stop=(kb == 7))
                            return r
                        P.op("tensor", mk_gu,
                             reads=["xnqTr", "wgT", "wuT"],
                             writes=["pg0", "pg1", "pg2",
                                     "pu0", "pu1", "pu2"])

                        def mk_silu(s):
                            r = None
                            for mi in range(3):
                                o0 = sum(widths[:mi])
                                r = s.activation(
                                    out=sgf[:, o0:o0 + widths[mi]],
                                    in_=p[mi][:, 0:widths[mi]],
                                    func=AF.Silu)
                            return r
                        P.op("scalar", mk_silu,
                             reads=["pg0", "pg1", "pg2"], writes=["sgf"])

                        def mk_h(v):
                            r = None
                            for mi in range(3):
                                o0 = sum(widths[:mi])
                                r = v.tensor_mul(
                                    out=qsB[:, o0:o0 + widths[mi]],
                                    in0=sgf[:, o0:o0 + widths[mi]],
                                    in1=p[3 + mi][:, 0:widths[mi]])
                            return r
                        P.op("vector", mk_h,
                             reads=["sgf", "pu0", "pu1", "pu2"],
                             writes=["h_f", "qsqBh"])
                        par = (2 * i + hf) % 2
                        q_half("Bh", "h_f", qsB[:, 0:W2], f"hq{par}",
                               (QOB if par == 0 else hqb2)[:, 0:W2],
                               W2, slice(0, W2), 128)
                        if tails:
                            emit_tail(tails.pop(0))
                        tails.append((i, hf, par))
                while tails:
                    emit_tail(tails.pop(0))
                P.barrier()

        with contextlib.ExitStack() as sD:
            diag_cv = sbt(sD, "diag_cv", [128, 16 * 4 * 128], BF16)
            diag_D = sbt(sD, "diag_D", [128, 16 * 128], BF16)
            woutb = sbt(sD, "woutb", [128, 16 * D], BF16)
            wdt_row = sbt(sD, "wdt_row", [1, DI])
            xin_b = sbt(sD, "xin_b", [128, 16 * 260], BF16)
            sz_ = [sbt(sD, f"sz{w}", [128, 16 * TC], BF16) for w in range(2)]
            xc_ = [sbt(sD, f"xc{w}", [128, 16 * TC], BF16) for w in range(2)]
            delt = sbt(sD, "delt", [128, 16 * TC])
            u_b = sbt(sD, "u_b", [128, 16 * TC], BF16)
            dbx_ = [sbt(sD, f"dbx{w}", [128, NS * TC], BF16)
                    for w in range(2)]
            bcB = sbt(sD, "bcB", [128, NS * TC], BF16)
            bcC = sbt(sD, "bcC", [128, NS * TC], BF16)
            hbuf = sbt(sD, "hbuf", [128, NS * TC], BF16)
            y2T = sbt(sD, "y2T", [128, 16 * TC], BF16)
            ssm_f = sbt(sD, "ssm_f", [33, TC])
            bc_flat = sbt(sD, "bc_flat", [1, 16 * TC], BF16)
            arg_sp = sbt(sD, "arg_sp", [128, 2 * TC])
            na_sp = sbt(sD, "na_sp", [128, 2 * TC])
            dA_sb = sbt(sD, "dA_sb", [128, 4 * TC])
            xres = sbt(sD, "xres", [128, D])
            ffn_t = sbt(sD, "ffn_t", [128, D])
            out_t = sbt(sD, "out_t", [128, D])

            for d in range(16 if PH >= 5 else 0):
                for k in range(4):
                    off = (d * 4 + k) * 128
                    P.op("gpsimd", lambda g, d=d, k=k, off=off:
                         g.tensor_scalar(
                             out=diag_cv[:, off:off + 128], in0=iden_f[:],
                             scalar1=cw_sb[:, d * 4 + k:d * 4 + k + 1],
                             scalar2=None, op0=ALU.mult),
                         reads=["idf", "cw"], writes=["dgc"])
                P.op("gpsimd", lambda g, d=d: g.tensor_scalar(
                    out=diag_D[:, d * 128:(d + 1) * 128], in0=iden_f[:],
                    scalar1=dpar_sb[:, d:d + 1], scalar2=None, op0=ALU.mult),
                    reads=["idf", "dpar"], writes=["dgD"])

            P.dma("sync", wdt_row[:], w_dt_d, writes=["wdt_row"],
                  fam="wdtl")
            for k in range(16 if PH >= 5 else 0):
                P.dma("gpsimd", woutb[:, k * D:(k + 1) * D],
                      W_out_d[k * 128:(k + 1) * 128, :],
                      writes=["wor"], fam="wo", waw=False)

            for j in range(NJ if PH >= 5 else 0):
                c0 = j * TC
                jp = j % 2
                sz_b = sz_[jp]
                xc_b = xc_[jp]
                if j == 0:
                    for d in range(16):
                        o = d * 260
                        P.op("gpsimd", lambda g, o=o: g.memset(
                            xin_b[:, o:o + 3], 0.0), writes=["xinh"])
                for d in range(16):
                    o = d * 260
                    if j == 0:
                        P.dma("gpsimd", xin_b[:, o + 3:o + 3 + TC],
                              xz_s[d * 128:(d + 1) * 128, 0:TC],
                              reads=["Dxz"], writes=["xinb"], fam="xing",
                              waw=False)
                    else:
                        P.dma("gpsimd", xin_b[:, o:o + 3 + TC],
                              xz_s[d * 128:(d + 1) * 128, c0 - 3:c0 + TC],
                              reads=["Dxz"], writes=["xinb"], fam="xing",
                              waw=False)
                if DS >= 1:
                    P.dma("gpsimd",
                          sz_b[:].rearrange("p (d t) -> p d t", t=TC),
                          xz_s[DI:2 * DI, c0:c0 + TC]
                          .rearrange("(d p) t -> p d t", p=128),
                          reads=["Dxz"],
                          writes=[f"sz{jp}_{d}" for d in range(16)],
                          fam="ztlg")
                for d in range(16 if DS >= 1 else 0):
                    P.op("scalar", lambda s, d=d, sz_b=sz_b:
                         s.activation(
                             out=sz_b[:, d * TC:(d + 1) * TC],
                             in_=sz_b[:, d * TC:(d + 1) * TC],
                             func=AF.Silu), reads=[f"sz{jp}_{d}"],
                         writes=[f"sz{jp}_{d}"])
                for d in range(16 if DS >= 2 else 0):
                    pq = (p[2], p[3], p[4], p[5])[d % 4]
                    pcv = ("P2", "P3", "P4", "P5")[d % 4]

                    def mk_cv(t, d=d, pq=pq):
                        r = None
                        for k in range(4):
                            off = (d * 4 + k) * 128
                            r = t.matmul(
                                out=pq[:, 0:TC],
                                lhsT=diag_cv[:, off:off + 128],
                                rhs=xin_b[:, d * 260 + k:d * 260 + k + TC],
                                start=(k == 0), stop=(k == 3))
                        return r
                    P.op("tensor", mk_cv, reads=["dgc", "xinb", "xinh"],
                         writes=[pcv])
                    P.op("scalar", lambda s, d=d, pq=pq, xc_b=xc_b:
                         s.activation(
                             out=xc_b[:, d * TC:(d + 1) * TC],
                             in_=pq[:, 0:TC],
                             func=AF.Silu, bias=cb_sb[:, d:d + 1]),
                         reads=[pcv, "cb"], writes=[f"xc{jp}"])
                if DS < 3:
                    continue

                def mk_ssm(t, xc_b=xc_b, lo=0, pq=None):
                    r = None
                    for d in range(lo, lo + 8):
                        r = t.matmul(
                            out=pq[0:33, 0:TC],
                            lhsT=wxp_b[:, d * 33:(d + 1) * 33],
                            rhs=xc_b[:, d * TC:(d + 1) * TC],
                            start=(d == lo), stop=(d == lo + 7))
                    return r
                import functools
                P.op("tensor", functools.partial(mk_ssm, lo=0, pq=p[2]),
                     reads=["wxp", f"xc{jp}"], writes=["P2"])
                P.op("tensor", functools.partial(mk_ssm, lo=8, pq=p[3]),
                     reads=["wxp", f"xc{jp}"], writes=["P3"])
                P.op("vector", lambda v: v.tensor_copy(
                    out=ssm_f[:], in_=p[2][0:33, 0:TC]),
                    reads=["P2"], writes=["ssmf"])
                P.op("vector", lambda v: v.tensor_add(
                    out=ssm_f[:], in0=ssm_f[:], in1=p[3][0:33, 0:TC]),
                    reads=["ssmf", "P3"], writes=["ssmf"])
                for grp, rbase in (("B", 1), ("C", 17)):
                    P.dma("gpsimd",
                          bc_flat[0:1, :].rearrange("o (n t) -> o n t", t=TC),
                          ssm_f[rbase:rbase + NS, 0:TC].unsqueeze(0)
                          if False else
                          ssm_f[rbase:rbase + NS, 0:TC],
                          reads=["ssmf"], writes=["bcf"], fam="bcf")
                    dst = bcB if grp == "B" else bcC
                    dname = "bcB" if grp == "B" else "bcC"
                    for n in range(NS):
                        pq = (p[2], p[3], p[4], p[5])[n % 4]
                        pb = ("P2", "P3", "P4", "P5")[n % 4]
                        P.op("tensor", lambda t, n=n, pq=pq: t.matmul(
                            out=pq[:, 0:TC], lhsT=ones_row[:],
                            rhs=bc_flat[0:1, n * TC:(n + 1) * TC],
                            start=True, stop=True),
                            reads=["bcf", "ones"], writes=[pb])
                        P.op("scalar", lambda s, n=n, dst=dst, pq=pq:
                             s.copy(out=dst[:, n * TC:(n + 1) * TC],
                                    in_=pq[:, 0:TC]),
                             reads=[pb], writes=[dname])
                for d in range(16 if DS >= 4 else 0):
                    pq = (p[2], p[3], p[4], p[5])[d % 4]
                    pdl = ("P2", "P3", "P4", "P5")[d % 4]
                    base = 0 if d % 2 == 0 else TC
                    sl = slice(base, base + TC)
                    P.op("tensor", lambda t, d=d, pq=pq: t.matmul(
                        out=pq[:, 0:TC],
                        lhsT=wdt_row[0:1, d * 128:(d + 1) * 128],
                        rhs=ssm_f[0:1, 0:TC], start=True, stop=True),
                        reads=["ssmf", "wdt_row"], writes=[pdl])
                    P.op("vector", lambda v, d=d, sl=sl, pq=pq:
                         v.tensor_scalar(
                             out=arg_sp[:, sl], in0=pq[:, 0:TC],
                             scalar1=bdt_sb[:, d:d + 1], scalar2=None,
                             op0=ALU.add), reads=[pdl, "bdt"],
                         writes=[f"arg{d % 2}"])
                    P.op("scalar", lambda s, sl=sl: s.activation(
                        out=na_sp[:, sl], in_=arg_sp[:, sl], func=AF.Abs),
                        reads=[f"arg{d % 2}"], writes=[f"na{d % 2}"])
                    P.op("scalar", lambda s, sl=sl: s.activation(
                        out=na_sp[:, sl], in_=na_sp[:, sl], func=AF.Exp,
                        scale=-1.0), reads=[f"na{d % 2}"],
                        writes=[f"na{d % 2}"])
                    P.op("scalar", lambda s, sl=sl: s.activation(
                        out=na_sp[:, sl], in_=na_sp[:, sl], func=AF.Ln,
                        bias=1.0), reads=[f"na{d % 2}"],
                        writes=[f"na{d % 2}"])
                    P.op("vector", lambda v, d=d, sl=sl:
                         v.scalar_tensor_tensor(
                             out=delt[:, d * TC:(d + 1) * TC],
                             in0=arg_sp[:, sl], scalar=0.0, op0=ALU.max,
                             in1=na_sp[:, sl], op1=ALU.add),
                         reads=[f"arg{d % 2}", f"na{d % 2}"],
                         writes=[f"dl{d}"])
                y2_pending = []
                for d in range(16 if DS >= 5 else 0):
                    dp = d % 2
                    dsl = delt[:, d * TC:(d + 1) * TC]
                    dbx = dbx_[dp]
                    P.op("gpsimd", lambda g, d=d, xc_b=xc_b:
                         g.tensor_mul(
                             out=u_b[:, d * TC:(d + 1) * TC],
                             in0=delt[:, d * TC:(d + 1) * TC],
                             in1=xc_b[:, d * TC:(d + 1) * TC]),
                         reads=[f"dl{d}", f"xc{jp}"], writes=[f"u{d}"])
                    for n in range(NS):
                        P.op("gpsimd", lambda g, n=n, d=d, dbx=dbx:
                             g.tensor_mul(
                                 out=dbx[:, n * TC:(n + 1) * TC],
                                 in0=bcB[:, n * TC:(n + 1) * TC],
                                 in1=u_b[:, d * TC:(d + 1) * TC]),
                             reads=["bcB", f"u{d}"], writes=[f"dbx{dp}"])
                    for n in range(NS if DS >= 6 else 0):
                        np_ = n % 4
                        pa = dA_sb[:, np_ * TC:(np_ + 1) * TC]
                        paname = f"dA{np_}"
                        P.op("scalar", lambda s, d=d, n=n, pa=pa, dsl=dsl:
                             s.activation(out=pa, in_=dsl, func=AF.Exp,
                                          scale=A_sb[d][:, n:n + 1]),
                             reads=[f"dl{d}", f"A_{d}"], writes=[paname])
                        if DS < 7:
                            continue
                        P.op("vector", lambda v, d=d, n=n, pa=pa, dbx=dbx:
                             v.tensor_tensor_scan(
                                 out=hbuf[:, n * TC:(n + 1) * TC],
                                 data0=pa,
                                 data1=dbx[:, n * TC:(n + 1) * TC],
                                 initial=hstate[:, d * NS + n:
                                                d * NS + n + 1],
                                 op0=ALU.mult, op1=ALU.add),
                             reads=[paname, f"dbx{dp}", f"hst{d}"],
                             writes=["hbufS"])
                    if DS < 8:
                        continue
                    P.op("vector", lambda v, d=d: v.tensor_copy(
                        out=hstate[:, d * NS:(d + 1) * NS].unsqueeze(2),
                        in_=hbuf[:].rearrange("p (n t) -> p n t",
                                              t=TC)[:, :, TC - 1:TC]),
                        reads=["hbufS"], writes=[f"hst{d}"])
                    P.op("vector", lambda v, dbx=dbx: v.tensor_mul(
                        out=dbx[:], in0=hbuf[:], in1=bcC[:]),
                        reads=["hbufS", "bcC"], writes=[f"dbx{dp}"])
                    py = p[4] if dp == 0 else p[5]
                    pyn = "P4" if dp == 0 else "P5"

                    def mk_y(t, d=d, dbx=dbx, py=py, xc_b=xc_b):
                        r = None
                        for n in range(NS):
                            r = t.matmul(out=py[:, 0:TC],
                                         lhsT=iden_b[:],
                                         rhs=dbx[:, n * TC:(n + 1) * TC],
                                         start=(n == 0), stop=False)
                        r = t.matmul(out=py[:, 0:TC],
                                     lhsT=diag_D[:, d * 128:(d + 1) * 128],
                                     rhs=xc_b[:, d * TC:(d + 1) * TC],
                                     start=False, stop=True)
                        return r
                    P.op("tensor", mk_y,
                         reads=[f"dbx{dp}", "iden_b", "dgD", f"xc{jp}"],
                         writes=[pyn])
                    y2_pending.append((d, py, pyn))
                    if len(y2_pending) > 1:
                        dd, pyd, pynd = y2_pending.pop(0)
                        P.op("vector", lambda v, d=dd, py=pyd, sz_b=sz_b:
                             v.tensor_mul(
                                 out=y2T[:, d * TC:(d + 1) * TC],
                                 in0=py[:, 0:TC],
                                 in1=sz_b[:, d * TC:(d + 1) * TC]),
                             reads=[pynd, f"sz{jp}_{dd}"],
                             writes=[f"y2_{dd}"])
                while y2_pending:
                    dd, pyd, pynd = y2_pending.pop(0)
                    P.op("vector", lambda v, d=dd, py=pyd, sz_b=sz_b:
                         v.tensor_mul(
                             out=y2T[:, d * TC:(d + 1) * TC],
                             in0=py[:, 0:TC],
                             in1=sz_b[:, d * TC:(d + 1) * TC]),
                         reads=[pynd, f"sz{jp}_{dd}"],
                         writes=[f"y2_{dd}"])
                for h2 in range(2 if DS >= 9 else 0):
                    r0 = c0 + h2 * 128
                    op_ = h2 % 2
                    P.dma("sync", xres[:], x_d[r0:r0 + 128, :],
                          writes=["xres"], fam="xrs")
                    P.dma("sync", ffn_t[:], ffn_sc[r0:r0 + 128, :],
                          reads=["Dffn"], writes=["ffnt"], fam="fft")
                    for k in range(16):
                        def mk_wo1(t, h2=h2, k=k):
                            lhs = y2T[:, k * TC + h2 * 128:
                                      k * TC + h2 * 128 + 128]
                            r = None
                            for mi in range(2):
                                r = t.matmul(
                                    out=p[mi][:, 0:512], lhsT=lhs,
                                    rhs=woutb[:, k * D + mi * 512:
                                              k * D + (mi + 1) * 512],
                                    start=(k == 0), stop=(k == 15))
                            return r
                        P.op("tensor", mk_wo1,
                             reads=["wor"] + [f"y2_{dd}" for dd in range(16)],
                             writes=["pg0", "pg1"])
                    P.op("vector", lambda v: v.tensor_add(
                        out=out_t[:, 0:512], in0=p[0][:, 0:512],
                        in1=xres[:, 0:512]),
                        reads=["pg0", "xres"], writes=["outt"])
                    P.op("vector", lambda v: v.tensor_add(
                        out=out_t[:, 512:1024], in0=p[1][:, 0:512],
                        in1=xres[:, 512:1024]),
                        reads=["pg1", "xres"], writes=["outt"])
                    P.op("vector", lambda v: v.tensor_add(
                        out=out_t[:], in0=out_t[:], in1=ffn_t[:]),
                        reads=["outt", "ffnt"], writes=["outt"])
                    P.dma("sync", out_d[r0:r0 + 128, :], out_t[:],
                          reads=["outt"], fam="otw", waw=False)

        P.drain()

        blk = ctx.enter_context(nc.Block())

        def replay(handle, atoms):
            for a in atoms:
                if a[0] == "w":
                    handle.wait_ge(a[1], a[2])
                elif a[0] == "c":
                    instr = a[1](handle)
                    instr.then_inc(a[2], 1)
                elif a[0] == "d":
                    handle.dma_start(
                        out=a[1], in_=a[2],
                        allow_slow_non_contiguous=a[4]).then_inc(a[3], 16)
                elif a[0] == "b":
                    handle.nop().then_inc(a[1], 1)
                    handle.wait_ge(a[1], a[2])

        @blk.vector
        def _(v):
            replay(v, P.atoms["vector"])

        @blk.scalar
        def _(s):
            replay(s, P.atoms["scalar"])

        @blk.tensor
        def _(t):
            replay(t, P.atoms["tensor"])

        @blk.gpsimd
        def _(g):
            replay(g, P.atoms["gpsimd"])

        @blk.sync
        def _(sy_):
            replay(sy_, P.atoms["sync"])

    return nc


_NC_CACHE = [None]


def kernel(**inputs):
    n = 8
    x_full = np.ascontiguousarray(np.asarray(inputs["L0_attn_x"],
                                             dtype=np.float32))
    shards = [np.ascontiguousarray(x_full[i]) for i in range(n)]
    conv_w = np.ascontiguousarray(
        np.asarray(inputs["L0_attn_conv_w"], dtype=np.float32).reshape(DI, 4))
    w_dt = np.ascontiguousarray(
        np.asarray(inputs["L0_attn_w_dt"], dtype=np.float32).reshape(1, DI))
    common = {
        "W_in": np.ascontiguousarray(np.asarray(inputs["L0_attn_W_in"],
                                                dtype=np.float32)),
        "conv_w": conv_w,
        "conv_b": np.ascontiguousarray(np.asarray(inputs["L0_attn_conv_b"],
                                                  dtype=np.float32)),
        "W_xproj": np.ascontiguousarray(np.asarray(inputs["L0_attn_W_xproj"],
                                                   dtype=np.float32)),
        "w_dt": w_dt,
        "b_dt": np.ascontiguousarray(np.asarray(inputs["L0_attn_b_dt"],
                                                dtype=np.float32)),
        "A_log": np.ascontiguousarray(np.asarray(inputs["L0_attn_A_log"],
                                                 dtype=np.float32)),
        "D_param": np.ascontiguousarray(np.asarray(inputs["L0_attn_D_param"],
                                                   dtype=np.float32)),
        "W_out": np.ascontiguousarray(np.asarray(inputs["L0_attn_W_out"],
                                                 dtype=np.float32)),
        "w_gate": np.ascontiguousarray(np.asarray(inputs["L0_ffn_w_gate"],
                                                  dtype=np.float32)),
        "w_up": np.ascontiguousarray(np.asarray(inputs["L0_ffn_w_up"],
                                                dtype=np.float32)),
        "w_down": np.ascontiguousarray(np.asarray(inputs["L0_ffn_w_down"],
                                                  dtype=np.float32)),
    }
    in_maps = [dict(common, x=shards[i]) for i in range(n)]
    if _NC_CACHE[0] is None:
        _NC_CACHE[0] = build_nc()
    res = run_bass_kernel_spmd(_NC_CACHE[0], in_maps, core_ids=list(range(n)))
    out = np.stack([np.asarray(r["out"], dtype=np.float32)
                    for r in res.results], axis=0)
    return out


# version 23
# speedup vs baseline: 1.2193x; 1.0017x over previous
import os
import numpy as np
import concourse.bass as bass
import concourse.mybir as mybir
from concourse.bass_utils import run_bass_kernel_spmd

F32 = mybir.dt.float32
BF16 = mybir.dt.bfloat16
AF = mybir.ActivationFunctionType
ALU = mybir.AluOpType
AX = mybir.AxisListType

S = 2048
PH = int(os.environ.get("ANT_PH", "5"))
DS = int(os.environ.get("ANT_DS", "9"))
D = 1024
DI = 2048
NS = 16
II = 2816
TC = 256
NJ = S // TC
EPS = 1e-5
MIDS = [0.25, 0.75, 1.25, 1.75, 2.5, 3.5, 5.0]
WTS = [0.5, 0.5, 0.5, 0.5, 1.0, 1.0, 2.0]

ENGS = ("vector", "scalar", "tensor", "gpsimd", "sync")


class _Buf:
    __slots__ = ("prod", "readers")

    def __init__(self):
        self.prod = None
        self.readers = []


class _Prog:

    SELF_SYNC = {"vector", "scalar", "tensor", "gpsimd", "sync"}

    def __init__(self, sems, mk_sem):
        self.atoms = {e: [] for e in ENGS}
        self.sems = sems
        self.mk_sem = mk_sem
        self.cnt = {e: 0 for e in ENGS}
        self.marks = {e: {} for e in ENGS}
        self.bufs = {}
        self.fams = {}
        self.nbar = 0
        self.sbar = mk_sem("sbar")

    def buf(self, name):
        if name not in self.bufs:
            self.bufs[name] = _Buf()
        return self.bufs[name]

    def _wait_raw(self, e, sem, val):
        if val <= 0:
            return
        k = id(sem)
        if self.marks[e].get(k, 0) >= val:
            return
        self.atoms[e].append(("w", sem, val))
        self.marks[e][k] = val

    def _wait_tok(self, e, tok):
        if tok is None:
            return
        if tok[0] == "F":
            fam = tok[1]
            sem, cnt, _q = self.fams[fam]
            self._wait_raw(e, sem, cnt)
        else:
            sem, val = tok[1], tok[2]
            if sem is self.sems.get(e) and e not in self.SELF_SYNC:
                return
            self._wait_raw(e, sem, val)

    def _deps(self, e, reads, writes):
        for n in reads:
            b = self.buf(n)
            self._wait_tok(e, b.prod)
        for n in writes:
            b = self.buf(n)
            self._wait_tok(e, b.prod)
            for r in b.readers:
                self._wait_tok(e, r)

    def _finish(self, tok, reads, writes):
        for n in writes:
            b = self.buf(n)
            b.prod = tok
            b.readers = []
        for n in reads:
            self.buf(n).readers.append(tok)

    def op(self, e, fn, reads=(), writes=()):
        self._deps(e, reads, writes)
        self.atoms[e].append(("c", fn, self.sems[e]))
        self.cnt[e] += 1
        self._finish(("E", self.sems[e], self.cnt[e]), reads, writes)

    def dma(self, e, out_ap, in_ap, reads=(), writes=(), fam="gen",
            slow=False, waw=True):
        if fam not in self.fams:
            self.fams[fam] = [self.mk_sem("f_" + fam), 0, e]
        rec = self.fams[fam]
        assert rec[2] == e, f"fam {fam} used on two queues"
        for n in reads:
            self._wait_tok(e, self.buf(n).prod)
        for n in writes:
            b = self.buf(n)
            if waw:
                self._wait_tok(e, b.prod)
            for r in b.readers:
                self._wait_tok(e, r)
        self.atoms[e].append(("d", out_ap, in_ap, rec[0], slow))
        rec[1] += 16
        self._finish(("F", fam), reads, writes)

    def barrier(self):
        self.nbar += 1
        tgt = 5 * self.nbar
        for e in ENGS:
            for fam, (sem, cnt, q) in self.fams.items():
                if q == e and cnt:
                    self._wait_raw(e, sem, cnt)
            self.atoms[e].append(("b", self.sbar, tgt))
            self.marks[e][id(self.sbar)] = tgt
        for b in self.bufs.values():
            b.readers = []

    def drain(self):
        for e in ENGS:
            for fam, (sem, cnt, q) in self.fams.items():
                if q == e and cnt:
                    self._wait_raw(e, sem, cnt)


def build_nc():
    nc = bass.Bass("TRN2", target_bir_lowering=False)

    x_d = nc.dram_tensor("x", [S, D], F32, kind="ExternalInput").ap()
    W_in_d = nc.dram_tensor("W_in", [D, 4 * D], F32, kind="ExternalInput").ap()
    conv_w_d = nc.dram_tensor("conv_w", [DI, 4], F32, kind="ExternalInput").ap()
    conv_b_d = nc.dram_tensor("conv_b", [DI], F32, kind="ExternalInput").ap()
    W_xp_d = nc.dram_tensor("W_xproj", [DI, 33], F32, kind="ExternalInput").ap()
    w_dt_d = nc.dram_tensor("w_dt", [1, DI], F32, kind="ExternalInput").ap()
    b_dt_d = nc.dram_tensor("b_dt", [DI], F32, kind="ExternalInput").ap()
    A_log_d = nc.dram_tensor("A_log", [DI, NS], F32, kind="ExternalInput").ap()
    D_par_d = nc.dram_tensor("D_param", [DI], F32, kind="ExternalInput").ap()
    W_out_d = nc.dram_tensor("W_out", [DI, D], F32, kind="ExternalInput").ap()
    w_g_d = nc.dram_tensor("w_gate", [II, D], F32, kind="ExternalInput").ap()
    w_u_d = nc.dram_tensor("w_up", [II, D], F32, kind="ExternalInput").ap()
    w_d_d = nc.dram_tensor("w_down", [D, II], F32, kind="ExternalInput").ap()
    out_d = nc.dram_tensor("out", [S, D], F32, kind="ExternalOutput").ap()
    xz_s = nc.dram_tensor("xz_s", [4 * D, S], BF16, kind="Internal").ap()
    ffn_sc = nc.dram_tensor("ffn_sc", [S, D], F32, kind="Internal").ap()

    import contextlib
    with contextlib.ExitStack() as ctx:
        def sem(name):
            return ctx.enter_context(nc.semaphore(name))

        sv, ss, st_, sg, sy = (sem(n) for n in ("sv", "ss", "st", "sg", "sy"))

        P = _Prog(
            sems={"vector": sv, "scalar": ss, "tensor": st_, "gpsimd": sg,
                  "sync": sy},
            mk_sem=sem,
        )

        def sbt(stack, name, shape, dt=F32):
            return stack.enter_context(nc.sbuf_tensor(name, shape, dt))

        A_sb = [sbt(ctx, f"A_{d}", [128, NS]) for d in range(16)]
        cb_sb = sbt(ctx, "cb_sb", [128, 16])
        bdt_sb = sbt(ctx, "bdt_sb", [128, 16])
        dpar_sb = sbt(ctx, "dpar_sb", [128, 16])
        cw_sb = sbt(ctx, "cw_sb", [128, 64])
        ones_row = sbt(ctx, "ones_row", [1, 128], BF16)
        iden_f = sbt(ctx, "iden_f", [128, 128])
        iden_b = sbt(ctx, "iden_b", [128, 128], BF16)
        wxp_b = sbt(ctx, "wxp_b", [128, 16 * 33], BF16)
        hstate = sbt(ctx, "hstate", [128, 256])
        sm_a = sbt(ctx, "sm_a", [128, 32])
        sm_b = sbt(ctx, "sm_b", [128, 216])
        sm_c = sbt(ctx, "sm_c", [128, 216])

        p = [ctx.enter_context(nc.psum_tensor(f"p{i}", [128, 512], F32))
             for i in range(6)]
        pT_ = [ctx.enter_context(nc.psum_tensor(f"pT{w}", [128, 1024], BF16))
               for w in range(2)]

        def quantN(tag, src_name, src_ap, dst_name, dst_ap, F,
                   qt, qs, l0, l2, l3, smo):
            nb = F // 16
            smb = sm_b[:, smo:smo + nb]
            smc = sm_c[:, smo:smo + nb]
            P.op("vector", lambda v: v.tensor_reduce(
                out=smb, in_=src_ap.rearrange("p (b w) -> p b w", w=16),
                axis=AX.X, op=ALU.max, apply_absolute_value=True),
                reads=[src_name], writes=["qsc" + tag])
            P.op("vector", lambda v: v.tensor_scalar(
                out=smb, in0=smb, scalar1=1.0 / 6.0,
                scalar2=1e-12, op0=ALU.mult, op1=ALU.max),
                reads=["qsc" + tag], writes=["qsc" + tag])
            P.op("vector", lambda v: v.reciprocal(out=smc, in_=smb),
                 reads=["qsc" + tag], writes=["qrs" + tag])
            rb = smc.unsqueeze(2).broadcast_to([128, nb, 16])
            sb_ = smb.unsqueeze(2).broadcast_to([128, nb, 16])
            P.op("vector", lambda v: v.tensor_tensor(
                out=qt.rearrange("p (b w) -> p b w", w=16),
                in0=src_ap.rearrange("p (b w) -> p b w", w=16),
                in1=rb, op=ALU.mult),
                reads=[src_name, "qrs" + tag], writes=["qt" + tag])
            P.op("scalar", lambda s: s.activation(
                out=qs, in_=qt, func=AF.Square),
                reads=["qt" + tag], writes=["qsq" + tag])
            P.op("vector", lambda v: v.tensor_scalar(
                out=l0, in0=qs, scalar1=MIDS[0] * MIDS[0], scalar2=None,
                op0=ALU.is_gt), reads=["qsq" + tag], writes=["ql0" + tag])
            for i in (1, 2, 3):
                P.op("vector", lambda v, i=i: v.scalar_tensor_tensor(
                    out=l0, in0=qs, scalar=MIDS[i] * MIDS[i],
                    op0=ALU.is_gt, in1=l0, op1=ALU.add),
                    reads=["qsq" + tag, "ql0" + tag], writes=["ql0" + tag])
            P.op("vector", lambda v: v.tensor_scalar(
                out=l2, in0=qs, scalar1=MIDS[4] * MIDS[4], scalar2=None,
                op0=ALU.is_gt), reads=["qsq" + tag], writes=["ql2" + tag])
            P.op("vector", lambda v: v.scalar_tensor_tensor(
                out=l2, in0=qs, scalar=MIDS[5] * MIDS[5],
                op0=ALU.is_gt, in1=l2, op1=ALU.add),
                reads=["qsq" + tag, "ql2" + tag], writes=["ql2" + tag])
            P.op("vector", lambda v: v.tensor_scalar(
                out=l3, in0=qs, scalar1=MIDS[6] * MIDS[6], scalar2=2.0,
                op0=ALU.is_gt, op1=ALU.mult),
                reads=["qsq" + tag], writes=["ql3" + tag])
            P.op("gpsimd", lambda g: g.tensor_add(out=l2, in0=l2, in1=l3),
                 reads=["ql2" + tag, "ql3" + tag], writes=["ql2" + tag])
            P.op("vector", lambda v: v.scalar_tensor_tensor(
                out=l0, in0=l0, scalar=0.5, op0=ALU.mult,
                in1=l2, op1=ALU.add),
                reads=["ql0" + tag, "ql2" + tag], writes=["ql0" + tag])
            P.op("vector", lambda v: v.tensor_scalar(
                out=qs, in0=qt, scalar1=0.0, scalar2=2.0,
                op0=ALU.is_ge, op1=ALU.mult),
                reads=["qt" + tag], writes=["qsq" + tag])
            P.op("vector", lambda v: v.scalar_tensor_tensor(
                out=qs.rearrange("p (b w) -> p b w", w=16),
                in0=qs.rearrange("p (b w) -> p b w", w=16),
                scalar=-1.0, op0=ALU.add, in1=sb_, op1=ALU.mult),
                reads=["qsq" + tag, "qsc" + tag], writes=["qsq" + tag])
            P.op("vector", lambda v: v.tensor_tensor(
                out=dst_ap, in0=l0, in1=qs, op=ALU.mult),
                reads=["ql0" + tag, "qsq" + tag], writes=[dst_name])

        tg_par = [0]

        def transpose_group(src_name, src_ap_fn, n_tiles, dst_fn, dst_name,
                            evac_eng="scalar"):
            done = 0
            while done < n_tiles:
                cnt = min(8, n_tiles - done)
                pg = tg_par[0] & 1
                tg_par[0] += 1
                pT = pT_[pg]

                def mk_tr(t, done=done, cnt=cnt, pT=pT):
                    r = None
                    for kk in range(cnt):
                        r = t.transpose(
                            out=pT[:, kk * 128:(kk + 1) * 128],
                            in_=src_ap_fn(done + kk),
                            identity=iden_b[:])
                    return r
                P.op("tensor", mk_tr, reads=[src_name, "iden_b"],
                     writes=[f"pTx{pg}"])

                def mk_ev(s, done=done, cnt=cnt, pT=pT):
                    r = None
                    for kk in range(cnt):
                        r = s.copy(out=dst_fn(done + kk),
                                   in_=pT[:, kk * 128:(kk + 1) * 128])
                    return r
                P.op(evac_eng, mk_ev, reads=[f"pTx{pg}"], writes=[dst_name])
                done += cnt

        for d in range(16):
            P.dma("sync", A_sb[d][:], A_log_d[d * 128:(d + 1) * 128, :],
                  writes=[f"A_{d}"], fam="cst")
        P.dma("sync", cw_sb[:].rearrange("p (d k) -> p d k", k=4),
              conv_w_d.rearrange("(d p) k -> p d k", p=128), writes=["cw"],
              fam="cst")
        P.dma("sync", cb_sb[:], conv_b_d.rearrange("(d p) -> p d", p=128),
              writes=["cb"], fam="cst", slow=True)
        P.dma("sync", bdt_sb[:], b_dt_d.rearrange("(d p) -> p d", p=128),
              writes=["bdt"], fam="cst", slow=True)
        P.dma("sync", dpar_sb[:], D_par_d.rearrange("(d p) -> p d", p=128),
              writes=["dpar"], fam="cst", slow=True)
        for d in range(16):
            P.op("scalar", lambda s, d=d: s.activation(
                out=A_sb[d][:], in_=A_sb[d][:], func=AF.Exp),
                reads=[f"A_{d}"], writes=[f"A_{d}"])
            P.op("vector", lambda v, d=d: v.tensor_scalar(
                out=A_sb[d][:], in0=A_sb[d][:], scalar1=-1.0, scalar2=None,
                op0=ALU.mult), reads=[f"A_{d}"], writes=[f"A_{d}"])
        P.op("gpsimd", lambda g: g.iota(
            out=iden_f[:], pattern=[[1, 128]], base=0, channel_multiplier=-1,
            allow_small_or_imprecise_dtypes=True), writes=["idf"])
        P.op("gpsimd", lambda g: g.tensor_scalar(
            out=iden_f[:], in0=iden_f[:], scalar1=0.0, scalar2=None,
            op0=ALU.is_equal), reads=["idf"], writes=["idf"])
        P.op("gpsimd", lambda g: g.tensor_copy(out=iden_b[:], in_=iden_f[:]),
             reads=["idf"], writes=["iden_b"])
        P.op("gpsimd", lambda g: g.memset(ones_row[:], 1.0), writes=["ones"])
        P.op("gpsimd", lambda g: g.memset(hstate[:], 0.0),
             writes=[f"hst{d}" for d in range(16)])

        with contextlib.ExitStack() as sAB:
            xnqT_r = sbt(sAB, "xnqT_r", [128, 8 * S], BF16)
            with contextlib.ExitStack() as sA:
                xnT_r = sbt(sA, "xnT_r", [128, 8 * S], BF16)
                winb = sbt(sA, "winb", [128, 8 * 4 * D], BF16)
                qtA = sbt(sA, "qtA", [128, 2 * D])
                qsA = sbt(sA, "qsA", [128, 2 * D])
                l0A = sbt(sA, "l0A", [128, 2 * D], BF16)
                l2A = sbt(sA, "l2A", [128, 2 * D], BF16)
                l3A = sbt(sA, "l3A", [128, 2 * D], BF16)
                QOA = sbt(sA, "QOA", [128, 2 * D], BF16)
                xa_ = [sbt(sA, f"xa{w}", [128, D]) for w in range(2)]
                xnf_ = [sbt(sA, f"xnf{w}", [128, D]) for w in range(2)]
                xnb_ = [sbt(sA, f"xnb{w}", [128, D], BF16) for w in range(2)]
                ev_ = [sbt(sA, f"ev{w}", [128, 512], BF16) for w in range(2)]

                for d in range(16):
                    P.dma("gpsimd", wxp_b[:, d * 33:(d + 1) * 33],
                          W_xp_d[d * 128:(d + 1) * 128, :],
                          writes=["wxp"], fam="wxl", waw=False)
                for k in range(8):
                    P.dma("gpsimd", winb[:, k * 4 * D:(k + 1) * 4 * D],
                          W_in_d[k * 128:(k + 1) * 128, :],
                          writes=["winb"], fam="winl", waw=False)

                for i in range(16 if PH >= 2 else 0):
                    w = i % 2
                    xa = xa_[w]
                    xt = f"xa{w}"
                    sm = (lambda c, w=w: sm_a[:, w * 16 + c:w * 16 + c + 1])
                    P.dma("sync", xa[:], x_d[i * 128:(i + 1) * 128, :],
                          writes=[xt], fam=xt)
                    P.op("scalar", lambda s, xa=xa, w=w: s.activation(
                        out=qsA[:, w * D:(w + 1) * D], in_=xa[:],
                        func=AF.Square,
                        accum_out=sm_a[:, w * 16:w * 16 + 1]), reads=[xt],
                        writes=[f"ln_ss{w}", f"qsqA{w}"])
                    P.op("vector", lambda v, xa=xa, sm=sm: v.tensor_reduce(
                        out=sm(1), in_=xa[:], axis=AX.X, op=ALU.add),
                        reads=[xt], writes=[f"ln_sum{w}"])
                    P.op("vector", lambda v, sm=sm: v.tensor_scalar(
                        out=sm(2), in0=sm(1), scalar1=1.0 / D,
                        scalar2=None, op0=ALU.mult),
                        reads=[f"ln_sum{w}"], writes=[f"ln_mean{w}"])
                    P.op("vector", lambda v, sm=sm: v.tensor_mul(
                        out=sm(3), in0=sm(2), in1=sm(2)),
                        reads=[f"ln_mean{w}"], writes=[f"ln_m2{w}"])
                    P.op("vector", lambda v, sm=sm: v.scalar_tensor_tensor(
                        out=sm(4), in0=sm(0), scalar=1.0 / D,
                        op0=ALU.mult, in1=sm(3), op1=ALU.subtract),
                        reads=[f"ln_ss{w}", f"ln_m2{w}"],
                        writes=[f"ln_var{w}"])
                    P.op("vector", lambda v, sm=sm: v.tensor_scalar(
                        out=sm(4), in0=sm(4), scalar1=EPS,
                        scalar2=None, op0=ALU.add),
                        reads=[f"ln_var{w}"], writes=[f"ln_var{w}"])
                    P.op("scalar", lambda s, sm=sm: s.activation(
                        out=sm(5), in_=sm(4), func=AF.Ln),
                        reads=[f"ln_var{w}"], writes=[f"ln_lv{w}"])
                    P.op("vector", lambda v, sm=sm: v.tensor_scalar(
                        out=sm(5), in0=sm(5), scalar1=-0.5,
                        scalar2=None, op0=ALU.mult),
                        reads=[f"ln_lv{w}"], writes=[f"ln_lv{w}"])
                    P.op("scalar", lambda s, sm=sm: s.activation(
                        out=sm(6), in_=sm(5), func=AF.Exp),
                        reads=[f"ln_lv{w}"], writes=[f"ln_rstd{w}"])
                    P.op("vector", lambda v, xa=xa, w=w, sm=sm:
                         v.tensor_scalar(
                             out=xnf_[w][:], in0=xa[:], scalar1=sm(2),
                             scalar2=sm(6), op0=ALU.subtract, op1=ALU.mult),
                         reads=[xt, f"ln_mean{w}", f"ln_rstd{w}"],
                         writes=[f"xnf{w}"])
                    P.op("vector", lambda v, w=w: v.tensor_copy(
                        out=xnb_[w][:], in_=xnf_[w][:]),
                        reads=[f"xnf{w}"], writes=[f"xnb{w}"])
                    quantN(f"A{w}", f"xnf{w}", xnf_[w][:], f"xnq{w}",
                           QOA[:, w * D:(w + 1) * D], D,
                           qtA[:, w * D:(w + 1) * D],
                           qsA[:, w * D:(w + 1) * D],
                           l0A[:, w * D:(w + 1) * D],
                           l2A[:, w * D:(w + 1) * D],
                           l3A[:, w * D:(w + 1) * D], w * 64)
                    transpose_group(
                        f"xnb{w}",
                        lambda dj, w=w: xnb_[w][:, dj * 128:(dj + 1) * 128],
                        8,
                        lambda dj, i=i: xnT_r[:, dj * S + i * 128:
                                              dj * S + (i + 1) * 128],
                        "xnTr")
                    transpose_group(
                        f"xnq{w}",
                        lambda dj, w=w: QOA[:, w * D + dj * 128:
                                            w * D + (dj + 1) * 128], 8,
                        lambda dj, i=i: xnqT_r[:, dj * S + i * 128:
                                               dj * S + (i + 1) * 128],
                        "xnqTr")
                    if i % 4 == 3 and PH >= 4:
                        tt4 = i // 4
                        for fc in range(32):
                            pb = p[fc % 2]
                            pname = f"pg{fc % 2}"

                            def mk_mm(t, fc=fc, pb=pb, tt4=tt4):
                                r = None
                                for k in range(8):
                                    r = t.matmul(
                                        out=pb[:],
                                        lhsT=winb[:, k * 4 * D + fc * 128:
                                                  k * 4 * D +
                                                  (fc + 1) * 128],
                                        rhs=xnT_r[:, k * S + tt4 * 512:
                                                  k * S + (tt4 + 1) * 512],
                                        start=(k == 0), stop=(k == 7))
                                return r
                            P.op("tensor", mk_mm, reads=["winb", "xnTr"],
                                 writes=[pname])
                            ev = ev_[fc % 2]
                            ename = f"ev{fc % 2}"
                            P.op("scalar", lambda s, ev=ev, pb=pb: s.copy(
                                out=ev[:], in_=pb[:]), reads=[pname],
                                writes=[ename])
                            P.dma("sync",
                                  xz_s[fc * 128:(fc + 1) * 128,
                                       tt4 * 512:(tt4 + 1) * 512],
                                  ev[:], reads=[ename], writes=["Dxz"],
                                  fam="xzw", waw=False)
                P.barrier()

            W2 = 1408
            with contextlib.ExitStack() as sB:
                wgT = sbt(sB, "wgT", [128, 8 * II], BF16)
                wuT = sbt(sB, "wuT", [128, 8 * II], BF16)
                wdT = sbt(sB, "wdT", [128, 22 * D], BF16)
                qtB = sbt(sB, "qtB", [128, W2])
                qsB = sbt(sB, "qsB", [128, W2])
                l0B = sbt(sB, "l0B", [128, W2], BF16)
                l2B = sbt(sB, "l2B", [128, W2], BF16)
                l3B = sbt(sB, "l3B", [128, W2], BF16)
                QOB = sbt(sB, "QOB", [128, W2], BF16)
                wdl = [sbt(sB, "wdl0", [128, W2])] * 2
                hqT1 = sbt(sB, "hqT1", [128, W2], BF16)
                hqb2 = sbt(sB, "hqb2", [128, W2], BF16)
                dne = sbt(sB, "dne", [128, D])

                def q_half(tag, src_name, src_ap, dst_name, dst_ap, F, sl,
                           smo):
                    quantN(tag, src_name, src_ap, dst_name, dst_ap, F,
                           qtB[:, sl], qsB[:, sl], l0B[:, sl], l2B[:, sl],
                           l3B[:, sl], smo)

                for dmc in range(8 if PH >= 3 else 0):
                    for hf in range(2):
                        nm = "wdl0"
                        P.dma("sync", wdl[hf][:],
                              w_d_d[dmc * 128:(dmc + 1) * 128,
                                    hf * W2:(hf + 1) * W2],
                              writes=[nm], fam=nm)
                        q_half("Bh", nm, wdl[hf][:], "wdq",
                               QOB[:, 0:W2], W2, slice(0, W2), 128)
                        transpose_group(
                            "wdq",
                            lambda ic: QOB[:, ic * 128:(ic + 1) * 128], 11,
                            lambda ic, dmc=dmc, hf=hf:
                            wdT[:, (hf * 11 + ic) * D + dmc * 128:
                                (hf * 11 + ic) * D + (dmc + 1) * 128],
                            "wdT")
                P.barrier()
                for (src_w, dstT, nmT) in ((w_g_d, wgT, "wgT"),
                                           (w_u_d, wuT, "wuT")):
                    for it in range(22 if PH >= 3 else 0):
                        w = 0
                        nm = "wdl0"
                        P.dma("sync", wdl[w][:, 0:D],
                              src_w[it * 128:(it + 1) * 128, :],
                              writes=[nm], fam=nm)
                        q_half("Bh", nm, wdl[w][:, 0:D], "wq",
                               QOB[:, 0:D], D, slice(0, D), 128)
                        transpose_group(
                            "wq",
                            lambda dj: QOB[:, dj * 128:(dj + 1) * 128], 8,
                            lambda dj, it=it, dstT=dstT:
                            dstT[:, dj * II + it * 128:
                                 dj * II + (it + 1) * 128],
                            nmT)
                P.barrier()
                sgf = wdl[0]
                widths = (512, 512, 384)
                tails = []

                def emit_tail(rec):
                    ti, hf, par = rec
                    gbase = hf * 11
                    qsrc = QOB if par == 0 else hqb2
                    transpose_group(
                        f"hq{par}",
                        lambda ic, qsrc=qsrc:
                        qsrc[:, ic * 128:(ic + 1) * 128], 11,
                        lambda ic: hqT1[:, ic * 128:(ic + 1) * 128],
                        "hqT1")

                    def mk_down(t, gbase=gbase):
                        r = None
                        for ic in range(11):
                            lhs = hqT1[:, ic * 128:(ic + 1) * 128]
                            gic = gbase + ic
                            for mi in range(2):
                                r = t.matmul(
                                    out=p[mi][:, 0:512], lhsT=lhs,
                                    rhs=wdT[:, gic * D + mi * 512:
                                            gic * D + (mi + 1) * 512],
                                    start=(ic == 0), stop=(ic == 10))
                        return r
                    P.op("tensor", mk_down, reads=["hqT1", "wdT"],
                         writes=["pg0", "pg1"])
                    if hf == 0:
                        P.op("vector", lambda v: v.tensor_copy(
                            out=dne[:, 0:512], in_=p[0][:, 0:512]),
                            reads=["pg0"], writes=["dne"])
                        P.op("vector", lambda v: v.tensor_copy(
                            out=dne[:, 512:1024], in_=p[1][:, 0:512]),
                            reads=["pg1"], writes=["dne"])
                    else:
                        P.op("vector", lambda v: v.tensor_add(
                            out=dne[:, 0:512], in0=dne[:, 0:512],
                            in1=p[0][:, 0:512]),
                            reads=["pg0", "dne"], writes=["dne"])
                        P.op("vector", lambda v: v.tensor_add(
                            out=dne[:, 512:1024], in0=dne[:, 512:1024],
                            in1=p[1][:, 0:512]),
                            reads=["pg1", "dne"], writes=["dne"])
                        P.dma("sync", ffn_sc[ti * 128:(ti + 1) * 128, :],
                              dne[:], reads=["dne"], writes=["Dffn"],
                              fam="ffw", waw=False)

                for i in range(16 if PH >= 3 else 0):
                    for hf in range(2):
                        def mk_gu(t, i=i, hf=hf):
                            r = None
                            for kb in range(8):
                                lhs = xnqT_r[:, kb * S + i * 128:
                                             kb * S + (i + 1) * 128]
                                for gi in range(6):
                                    mi = gi % 3
                                    o0 = sum(widths[:mi])
                                    wT = wgT if gi < 3 else wuT
                                    r = t.matmul(
                                        out=p[(0 if gi < 3 else 3) + mi][
                                            :, 0:widths[mi]],
                                        lhsT=lhs,
                                        rhs=wT[:, kb * II + hf * W2 + o0:
                                               kb * II + hf * W2 + o0 +
                                               widths[mi]],
                                        start=(kb == 0), stop=(kb == 7))
                            return r
                        P.op("tensor", mk_gu,
                             reads=["xnqTr", "wgT", "wuT"],
                             writes=["pg0", "pg1", "pg2",
                                     "pu0", "pu1", "pu2"])

                        def mk_silu(s):
                            r = None
                            for mi in range(3):
                                o0 = sum(widths[:mi])
                                r = s.activation(
                                    out=sgf[:, o0:o0 + widths[mi]],
                                    in_=p[mi][:, 0:widths[mi]],
                                    func=AF.Silu)
                            return r
                        P.op("scalar", mk_silu,
                             reads=["pg0", "pg1", "pg2"], writes=["sgf"])

                        def mk_h(v):
                            r = None
                            for mi in range(3):
                                o0 = sum(widths[:mi])
                                r = v.tensor_mul(
                                    out=qsB[:, o0:o0 + widths[mi]],
                                    in0=sgf[:, o0:o0 + widths[mi]],
                                    in1=p[3 + mi][:, 0:widths[mi]])
                            return r
                        P.op("vector", mk_h,
                             reads=["sgf", "pu0", "pu1", "pu2"],
                             writes=["h_f", "qsqBh"])
                        par = (2 * i + hf) % 2
                        q_half("Bh", "h_f", qsB[:, 0:W2], f"hq{par}",
                               (QOB if par == 0 else hqb2)[:, 0:W2],
                               W2, slice(0, W2), 128)
                        if tails:
                            emit_tail(tails.pop(0))
                        tails.append((i, hf, par))
                while tails:
                    emit_tail(tails.pop(0))
                P.barrier()

        with contextlib.ExitStack() as sD:
            diag_cv = sbt(sD, "diag_cv", [128, 16 * 4 * 128], BF16)
            diag_D = sbt(sD, "diag_D", [128, 16 * 128], BF16)
            woutb = sbt(sD, "woutb", [128, 16 * D], BF16)
            wdt_row = sbt(sD, "wdt_row", [1, DI])
            xin_b = sbt(sD, "xin_b", [128, 16 * 260], BF16)
            sz_ = [sbt(sD, f"sz{w}", [128, 16 * TC], BF16) for w in range(2)]
            xc_ = [sbt(sD, f"xc{w}", [128, 16 * TC], BF16) for w in range(2)]
            delt = sbt(sD, "delt", [128, 16 * TC])
            u_b = sbt(sD, "u_b", [128, 16 * TC], BF16)
            dbx_ = [sbt(sD, f"dbx{w}", [128, NS * TC], BF16)
                    for w in range(2)]
            bcB = sbt(sD, "bcB", [128, NS * TC], BF16)
            bcC = sbt(sD, "bcC", [128, NS * TC], BF16)
            hbuf = sbt(sD, "hbuf", [128, NS * TC], BF16)
            y2T = sbt(sD, "y2T", [128, 16 * TC], BF16)
            ssm_f = sbt(sD, "ssm_f", [33, TC])
            bc_flat = sbt(sD, "bc_flat", [1, 16 * TC], BF16)
            arg_sp = sbt(sD, "arg_sp", [128, 2 * TC])
            na_sp = sbt(sD, "na_sp", [128, 2 * TC])
            dA_sb = sbt(sD, "dA_sb", [128, 4 * TC])
            xres = sbt(sD, "xres", [128, D])
            ffn_t = sbt(sD, "ffn_t", [128, D])
            out_t = sbt(sD, "out_t", [128, D])

            for d in range(16 if PH >= 5 else 0):
                for k in range(4):
                    off = (d * 4 + k) * 128
                    P.op("gpsimd", lambda g, d=d, k=k, off=off:
                         g.tensor_scalar(
                             out=diag_cv[:, off:off + 128], in0=iden_f[:],
                             scalar1=cw_sb[:, d * 4 + k:d * 4 + k + 1],
                             scalar2=None, op0=ALU.mult),
                         reads=["idf", "cw"], writes=["dgc"])
                P.op("gpsimd", lambda g, d=d: g.tensor_scalar(
                    out=diag_D[:, d * 128:(d + 1) * 128], in0=iden_f[:],
                    scalar1=dpar_sb[:, d:d + 1], scalar2=None, op0=ALU.mult),
                    reads=["idf", "dpar"], writes=["dgD"])

            P.dma("sync", wdt_row[:], w_dt_d, writes=["wdt_row"],
                  fam="wdtl")
            for k in range(16 if PH >= 5 else 0):
                P.dma("gpsimd", woutb[:, k * D:(k + 1) * D],
                      W_out_d[k * 128:(k + 1) * 128, :],
                      writes=["wor"], fam="wo", waw=False)

            for j in range(NJ if PH >= 5 else 0):
                c0 = j * TC
                jp = j % 2
                sz_b = sz_[jp]
                xc_b = xc_[jp]
                if j == 0:
                    for d in range(16):
                        o = d * 260
                        P.op("gpsimd", lambda g, o=o: g.memset(
                            xin_b[:, o:o + 3], 0.0), writes=["xinh"])
                for d in range(16):
                    o = d * 260
                    if j == 0:
                        P.dma("gpsimd", xin_b[:, o + 3:o + 3 + TC],
                              xz_s[d * 128:(d + 1) * 128, 0:TC],
                              reads=["Dxz"], writes=["xinb"], fam="xing",
                              waw=False)
                    else:
                        P.dma("gpsimd", xin_b[:, o:o + 3 + TC],
                              xz_s[d * 128:(d + 1) * 128, c0 - 3:c0 + TC],
                              reads=["Dxz"], writes=["xinb"], fam="xing",
                              waw=False)
                if DS >= 1:
                    P.dma("gpsimd",
                          sz_b[:].rearrange("p (d t) -> p d t", t=TC),
                          xz_s[DI:2 * DI, c0:c0 + TC]
                          .rearrange("(d p) t -> p d t", p=128),
                          reads=["Dxz"],
                          writes=[f"sz{jp}_{d}" for d in range(16)],
                          fam="ztlg")
                for d in range(16 if DS >= 1 else 0):
                    P.op("scalar", lambda s, d=d, sz_b=sz_b:
                         s.activation(
                             out=sz_b[:, d * TC:(d + 1) * TC],
                             in_=sz_b[:, d * TC:(d + 1) * TC],
                             func=AF.Silu), reads=[f"sz{jp}_{d}"],
                         writes=[f"sz{jp}_{d}"])
                for d in range(16 if DS >= 2 else 0):
                    pq = p[2] if d % 2 == 0 else p[3]
                    pcv = "P2" if d % 2 == 0 else "P3"

                    def mk_cv(t, d=d, pq=pq):
                        r = None
                        for k in range(4):
                            off = (d * 4 + k) * 128
                            r = t.matmul(
                                out=pq[:, 0:TC],
                                lhsT=diag_cv[:, off:off + 128],
                                rhs=xin_b[:, d * 260 + k:d * 260 + k + TC],
                                start=(k == 0), stop=(k == 3))
                        return r
                    P.op("tensor", mk_cv, reads=["dgc", "xinb", "xinh"],
                         writes=[pcv])
                    P.op("scalar", lambda s, d=d, pq=pq, xc_b=xc_b:
                         s.activation(
                             out=xc_b[:, d * TC:(d + 1) * TC],
                             in_=pq[:, 0:TC],
                             func=AF.Silu, bias=cb_sb[:, d:d + 1]),
                         reads=[pcv, "cb"], writes=[f"xc{jp}"])
                if DS < 3:
                    continue

                def mk_ssm(t, xc_b=xc_b, lo=0, pq=None):
                    r = None
                    for d in range(lo, lo + 8):
                        r = t.matmul(
                            out=pq[0:33, 0:TC],
                            lhsT=wxp_b[:, d * 33:(d + 1) * 33],
                            rhs=xc_b[:, d * TC:(d + 1) * TC],
                            start=(d == lo), stop=(d == lo + 7))
                    return r
                import functools
                P.op("tensor", functools.partial(mk_ssm, lo=0, pq=p[2]),
                     reads=["wxp", f"xc{jp}"], writes=["P2"])
                P.op("tensor", functools.partial(mk_ssm, lo=8, pq=p[3]),
                     reads=["wxp", f"xc{jp}"], writes=["P3"])
                P.op("vector", lambda v: v.tensor_copy(
                    out=ssm_f[:], in_=p[2][0:33, 0:TC]),
                    reads=["P2"], writes=["ssmf"])
                P.op("vector", lambda v: v.tensor_add(
                    out=ssm_f[:], in0=ssm_f[:], in1=p[3][0:33, 0:TC]),
                    reads=["ssmf", "P3"], writes=["ssmf"])
                for grp, rbase in (("B", 1), ("C", 17)):
                    P.dma("gpsimd",
                          bc_flat[0:1, :].rearrange("o (n t) -> o n t", t=TC),
                          ssm_f[rbase:rbase + NS, 0:TC].unsqueeze(0)
                          if False else
                          ssm_f[rbase:rbase + NS, 0:TC],
                          reads=["ssmf"], writes=["bcf"], fam="bcf")
                    dst = bcB if grp == "B" else bcC
                    dname = "bcB" if grp == "B" else "bcC"
                    for n in range(NS):
                        pq = (p[2], p[3], p[4], p[5])[n % 4]
                        pb = ("P2", "P3", "P4", "P5")[n % 4]
                        P.op("tensor", lambda t, n=n, pq=pq: t.matmul(
                            out=pq[:, 0:TC], lhsT=ones_row[:],
                            rhs=bc_flat[0:1, n * TC:(n + 1) * TC],
                            start=True, stop=True),
                            reads=["bcf", "ones"], writes=[pb])
                        P.op("scalar", lambda s, n=n, dst=dst, pq=pq:
                             s.copy(out=dst[:, n * TC:(n + 1) * TC],
                                    in_=pq[:, 0:TC]),
                             reads=[pb], writes=[dname])
                for d in range(16 if DS >= 4 else 0):
                    pq = p[2] if d % 2 == 0 else p[3]
                    pdl = "P2" if d % 2 == 0 else "P3"
                    base = 0 if d % 2 == 0 else TC
                    sl = slice(base, base + TC)
                    P.op("tensor", lambda t, d=d, pq=pq: t.matmul(
                        out=pq[:, 0:TC],
                        lhsT=wdt_row[0:1, d * 128:(d + 1) * 128],
                        rhs=ssm_f[0:1, 0:TC], start=True, stop=True),
                        reads=["ssmf", "wdt_row"], writes=[pdl])
                    P.op("vector", lambda v, d=d, sl=sl, pq=pq:
                         v.tensor_scalar(
                             out=arg_sp[:, sl], in0=pq[:, 0:TC],
                             scalar1=bdt_sb[:, d:d + 1], scalar2=None,
                             op0=ALU.add), reads=[pdl, "bdt"],
                         writes=[f"arg{d % 2}"])
                    P.op("scalar", lambda s, sl=sl: s.activation(
                        out=na_sp[:, sl], in_=arg_sp[:, sl], func=AF.Abs),
                        reads=[f"arg{d % 2}"], writes=[f"na{d % 2}"])
                    P.op("scalar", lambda s, sl=sl: s.activation(
                        out=na_sp[:, sl], in_=na_sp[:, sl], func=AF.Exp,
                        scale=-1.0), reads=[f"na{d % 2}"],
                        writes=[f"na{d % 2}"])
                    P.op("scalar", lambda s, sl=sl: s.activation(
                        out=na_sp[:, sl], in_=na_sp[:, sl], func=AF.Ln,
                        bias=1.0), reads=[f"na{d % 2}"],
                        writes=[f"na{d % 2}"])
                    P.op("vector", lambda v, d=d, sl=sl:
                         v.scalar_tensor_tensor(
                             out=delt[:, d * TC:(d + 1) * TC],
                             in0=arg_sp[:, sl], scalar=0.0, op0=ALU.max,
                             in1=na_sp[:, sl], op1=ALU.add),
                         reads=[f"arg{d % 2}", f"na{d % 2}"],
                         writes=[f"dl{d}"])
                y2_pending = []
                for d in range(16 if DS >= 5 else 0):
                    dp = d % 2
                    dsl = delt[:, d * TC:(d + 1) * TC]
                    dbx = dbx_[dp]
                    P.op("gpsimd", lambda g, d=d, xc_b=xc_b:
                         g.tensor_mul(
                             out=u_b[:, d * TC:(d + 1) * TC],
                             in0=delt[:, d * TC:(d + 1) * TC],
                             in1=xc_b[:, d * TC:(d + 1) * TC]),
                         reads=[f"dl{d}", f"xc{jp}"], writes=[f"u{d}"])
                    for n in range(NS):
                        P.op("gpsimd", lambda g, n=n, d=d, dbx=dbx:
                             g.tensor_mul(
                                 out=dbx[:, n * TC:(n + 1) * TC],
                                 in0=bcB[:, n * TC:(n + 1) * TC],
                                 in1=u_b[:, d * TC:(d + 1) * TC]),
                             reads=["bcB", f"u{d}"], writes=[f"dbx{dp}"])
                    for n in range(NS if DS >= 6 else 0):
                        np_ = n % 4
                        pa = dA_sb[:, np_ * TC:(np_ + 1) * TC]
                        paname = f"dA{np_}"
                        if n % 2 == 0:
                            np1 = (n + 1) % 4
                            pa1 = dA_sb[:, np1 * TC:(np1 + 1) * TC]

                            def mk_da(s, d=d, n=n, pa=pa, pa1=pa1, dsl=dsl):
                                s.activation(out=pa, in_=dsl, func=AF.Exp,
                                             scale=A_sb[d][:, n:n + 1])
                                return s.activation(
                                    out=pa1, in_=dsl, func=AF.Exp,
                                    scale=A_sb[d][:, n + 1:n + 2])
                            P.op("scalar", mk_da,
                                 reads=[f"dl{d}", f"A_{d}"],
                                 writes=[paname, f"dA{np1}"])
                        if DS < 7:
                            continue
                        P.op("vector", lambda v, d=d, n=n, pa=pa, dbx=dbx:
                             v.tensor_tensor_scan(
                                 out=hbuf[:, n * TC:(n + 1) * TC],
                                 data0=pa,
                                 data1=dbx[:, n * TC:(n + 1) * TC],
                                 initial=hstate[:, d * NS + n:
                                                d * NS + n + 1],
                                 op0=ALU.mult, op1=ALU.add),
                             reads=[paname, f"dbx{dp}", f"hst{d}"],
                             writes=["hbufS"])
                    if DS < 8:
                        continue
                    P.op("vector", lambda v, d=d: v.tensor_copy(
                        out=hstate[:, d * NS:(d + 1) * NS].unsqueeze(2),
                        in_=hbuf[:].rearrange("p (n t) -> p n t",
                                              t=TC)[:, :, TC - 1:TC]),
                        reads=["hbufS"], writes=[f"hst{d}"])
                    P.op("vector", lambda v, dbx=dbx: v.tensor_mul(
                        out=dbx[:], in0=hbuf[:], in1=bcC[:]),
                        reads=["hbufS", "bcC"], writes=[f"dbx{dp}"])
                    py = p[4] if dp == 0 else p[5]
                    pyn = "P4" if dp == 0 else "P5"

                    def mk_y(t, d=d, dbx=dbx, py=py, xc_b=xc_b):
                        r = None
                        for n in range(NS):
                            r = t.matmul(out=py[:, 0:TC],
                                         lhsT=iden_b[:],
                                         rhs=dbx[:, n * TC:(n + 1) * TC],
                                         start=(n == 0), stop=False)
                        r = t.matmul(out=py[:, 0:TC],
                                     lhsT=diag_D[:, d * 128:(d + 1) * 128],
                                     rhs=xc_b[:, d * TC:(d + 1) * TC],
                                     start=False, stop=True)
                        return r
                    P.op("tensor", mk_y,
                         reads=[f"dbx{dp}", "iden_b", "dgD", f"xc{jp}"],
                         writes=[pyn])
                    y2_pending.append((d, py, pyn))
                    if len(y2_pending) > 1:
                        dd, pyd, pynd = y2_pending.pop(0)
                        P.op("vector", lambda v, d=dd, py=pyd, sz_b=sz_b:
                             v.tensor_mul(
                                 out=y2T[:, d * TC:(d + 1) * TC],
                                 in0=py[:, 0:TC],
                                 in1=sz_b[:, d * TC:(d + 1) * TC]),
                             reads=[pynd, f"sz{jp}_{dd}"],
                             writes=[f"y2_{dd}"])
                while y2_pending:
                    dd, pyd, pynd = y2_pending.pop(0)
                    P.op("vector", lambda v, d=dd, py=pyd, sz_b=sz_b:
                         v.tensor_mul(
                             out=y2T[:, d * TC:(d + 1) * TC],
                             in0=py[:, 0:TC],
                             in1=sz_b[:, d * TC:(d + 1) * TC]),
                         reads=[pynd, f"sz{jp}_{dd}"],
                         writes=[f"y2_{dd}"])
                for h2 in range(2 if DS >= 9 else 0):
                    r0 = c0 + h2 * 128
                    op_ = h2 % 2
                    P.dma("sync", xres[:], x_d[r0:r0 + 128, :],
                          writes=["xres"], fam="xrs")
                    P.dma("sync", ffn_t[:], ffn_sc[r0:r0 + 128, :],
                          reads=["Dffn"], writes=["ffnt"], fam="fft")
                    for k in range(16):
                        def mk_wo1(t, h2=h2, k=k):
                            lhs = y2T[:, k * TC + h2 * 128:
                                      k * TC + h2 * 128 + 128]
                            r = None
                            for mi in range(2):
                                r = t.matmul(
                                    out=p[mi][:, 0:512], lhsT=lhs,
                                    rhs=woutb[:, k * D + mi * 512:
                                              k * D + (mi + 1) * 512],
                                    start=(k == 0), stop=(k == 15))
                            return r
                        P.op("tensor", mk_wo1,
                             reads=["wor"] + [f"y2_{dd}" for dd in range(16)],
                             writes=["pg0", "pg1"])
                    P.op("vector", lambda v: v.tensor_add(
                        out=out_t[:, 0:512], in0=p[0][:, 0:512],
                        in1=xres[:, 0:512]),
                        reads=["pg0", "xres"], writes=["outt"])
                    P.op("vector", lambda v: v.tensor_add(
                        out=out_t[:, 512:1024], in0=p[1][:, 0:512],
                        in1=xres[:, 512:1024]),
                        reads=["pg1", "xres"], writes=["outt"])
                    P.op("vector", lambda v: v.tensor_add(
                        out=out_t[:], in0=out_t[:], in1=ffn_t[:]),
                        reads=["outt", "ffnt"], writes=["outt"])
                    P.dma("sync", out_d[r0:r0 + 128, :], out_t[:],
                          reads=["outt"], fam="otw", waw=False)

        P.drain()

        blk = ctx.enter_context(nc.Block())

        def replay(handle, atoms):
            for a in atoms:
                if a[0] == "w":
                    handle.wait_ge(a[1], a[2])
                elif a[0] == "c":
                    instr = a[1](handle)
                    instr.then_inc(a[2], 1)
                elif a[0] == "d":
                    handle.dma_start(
                        out=a[1], in_=a[2],
                        allow_slow_non_contiguous=a[4]).then_inc(a[3], 16)
                elif a[0] == "b":
                    handle.nop().then_inc(a[1], 1)
                    handle.wait_ge(a[1], a[2])

        @blk.vector
        def _(v):
            replay(v, P.atoms["vector"])

        @blk.scalar
        def _(s):
            replay(s, P.atoms["scalar"])

        @blk.tensor
        def _(t):
            replay(t, P.atoms["tensor"])

        @blk.gpsimd
        def _(g):
            replay(g, P.atoms["gpsimd"])

        @blk.sync
        def _(sy_):
            replay(sy_, P.atoms["sync"])

    return nc


_NC_CACHE = [None]


def kernel(**inputs):
    n = 8
    x_full = np.ascontiguousarray(np.asarray(inputs["L0_attn_x"],
                                             dtype=np.float32))
    shards = [np.ascontiguousarray(x_full[i]) for i in range(n)]
    conv_w = np.ascontiguousarray(
        np.asarray(inputs["L0_attn_conv_w"], dtype=np.float32).reshape(DI, 4))
    w_dt = np.ascontiguousarray(
        np.asarray(inputs["L0_attn_w_dt"], dtype=np.float32).reshape(1, DI))
    common = {
        "W_in": np.ascontiguousarray(np.asarray(inputs["L0_attn_W_in"],
                                                dtype=np.float32)),
        "conv_w": conv_w,
        "conv_b": np.ascontiguousarray(np.asarray(inputs["L0_attn_conv_b"],
                                                  dtype=np.float32)),
        "W_xproj": np.ascontiguousarray(np.asarray(inputs["L0_attn_W_xproj"],
                                                   dtype=np.float32)),
        "w_dt": w_dt,
        "b_dt": np.ascontiguousarray(np.asarray(inputs["L0_attn_b_dt"],
                                                dtype=np.float32)),
        "A_log": np.ascontiguousarray(np.asarray(inputs["L0_attn_A_log"],
                                                 dtype=np.float32)),
        "D_param": np.ascontiguousarray(np.asarray(inputs["L0_attn_D_param"],
                                                   dtype=np.float32)),
        "W_out": np.ascontiguousarray(np.asarray(inputs["L0_attn_W_out"],
                                                 dtype=np.float32)),
        "w_gate": np.ascontiguousarray(np.asarray(inputs["L0_ffn_w_gate"],
                                                  dtype=np.float32)),
        "w_up": np.ascontiguousarray(np.asarray(inputs["L0_ffn_w_up"],
                                                dtype=np.float32)),
        "w_down": np.ascontiguousarray(np.asarray(inputs["L0_ffn_w_down"],
                                                  dtype=np.float32)),
    }
    in_maps = [dict(common, x=shards[i]) for i in range(n)]
    if _NC_CACHE[0] is None:
        _NC_CACHE[0] = build_nc()
    res = run_bass_kernel_spmd(_NC_CACHE[0], in_maps, core_ids=list(range(n)))
    out = np.stack([np.asarray(r["out"], dtype=np.float32)
                    for r in res.results], axis=0)
    return out


# version 24
# speedup vs baseline: 1.2292x; 1.0082x over previous
import os
import numpy as np
import concourse.bass as bass
import concourse.mybir as mybir
from concourse.bass_utils import run_bass_kernel_spmd

F32 = mybir.dt.float32
BF16 = mybir.dt.bfloat16
AF = mybir.ActivationFunctionType
ALU = mybir.AluOpType
AX = mybir.AxisListType

S = 2048
PH = int(os.environ.get("ANT_PH", "5"))
DS = int(os.environ.get("ANT_DS", "9"))
D = 1024
DI = 2048
NS = 16
II = 2816
TC = 256
NJ = S // TC
EPS = 1e-5
MIDS = [0.25, 0.75, 1.25, 1.75, 2.5, 3.5, 5.0]
WTS = [0.5, 0.5, 0.5, 0.5, 1.0, 1.0, 2.0]

ENGS = ("vector", "scalar", "tensor", "gpsimd", "sync")


class _Buf:
    __slots__ = ("prod", "readers")

    def __init__(self):
        self.prod = None
        self.readers = []


class _Prog:

    SELF_SYNC = {"vector", "scalar", "tensor", "gpsimd", "sync"}

    def __init__(self, sems, mk_sem):
        self.atoms = {e: [] for e in ENGS}
        self.sems = sems
        self.mk_sem = mk_sem
        self.cnt = {e: 0 for e in ENGS}
        self.marks = {e: {} for e in ENGS}
        self.bufs = {}
        self.fams = {}
        self.nbar = 0
        self.sbar = mk_sem("sbar")

    def buf(self, name):
        if name not in self.bufs:
            self.bufs[name] = _Buf()
        return self.bufs[name]

    def _wait_raw(self, e, sem, val):
        if val <= 0:
            return
        k = id(sem)
        if self.marks[e].get(k, 0) >= val:
            return
        self.atoms[e].append(("w", sem, val))
        self.marks[e][k] = val

    def _wait_tok(self, e, tok):
        if tok is None:
            return
        if tok[0] == "F":
            fam = tok[1]
            sem, cnt, _q = self.fams[fam]
            self._wait_raw(e, sem, cnt)
        else:
            sem, val = tok[1], tok[2]
            if sem is self.sems.get(e) and e not in self.SELF_SYNC:
                return
            self._wait_raw(e, sem, val)

    def _deps(self, e, reads, writes):
        for n in reads:
            b = self.buf(n)
            self._wait_tok(e, b.prod)
        for n in writes:
            b = self.buf(n)
            self._wait_tok(e, b.prod)
            for r in b.readers:
                self._wait_tok(e, r)

    def _finish(self, tok, reads, writes):
        for n in writes:
            b = self.buf(n)
            b.prod = tok
            b.readers = []
        for n in reads:
            self.buf(n).readers.append(tok)

    def op(self, e, fn, reads=(), writes=()):
        self._deps(e, reads, writes)
        self.atoms[e].append(("c", fn, self.sems[e]))
        self.cnt[e] += 1
        self._finish(("E", self.sems[e], self.cnt[e]), reads, writes)

    def dma(self, e, out_ap, in_ap, reads=(), writes=(), fam="gen",
            slow=False, waw=True):
        if fam not in self.fams:
            self.fams[fam] = [self.mk_sem("f_" + fam), 0, e]
        rec = self.fams[fam]
        assert rec[2] == e, f"fam {fam} used on two queues"
        for n in reads:
            self._wait_tok(e, self.buf(n).prod)
        for n in writes:
            b = self.buf(n)
            if waw:
                self._wait_tok(e, b.prod)
            for r in b.readers:
                self._wait_tok(e, r)
        self.atoms[e].append(("d", out_ap, in_ap, rec[0], slow))
        rec[1] += 16
        self._finish(("F", fam), reads, writes)

    def barrier(self):
        self.nbar += 1
        tgt = 5 * self.nbar
        for e in ENGS:
            for fam, (sem, cnt, q) in self.fams.items():
                if q == e and cnt:
                    self._wait_raw(e, sem, cnt)
            self.atoms[e].append(("b", self.sbar, tgt))
            self.marks[e][id(self.sbar)] = tgt
        for b in self.bufs.values():
            b.readers = []

    def drain(self):
        for e in ENGS:
            for fam, (sem, cnt, q) in self.fams.items():
                if q == e and cnt:
                    self._wait_raw(e, sem, cnt)


def build_nc():
    nc = bass.Bass("TRN2", target_bir_lowering=False)

    x_d = nc.dram_tensor("x", [S, D], F32, kind="ExternalInput").ap()
    W_in_d = nc.dram_tensor("W_in", [D, 4 * D], F32, kind="ExternalInput").ap()
    conv_w_d = nc.dram_tensor("conv_w", [DI, 4], F32, kind="ExternalInput").ap()
    conv_b_d = nc.dram_tensor("conv_b", [DI], F32, kind="ExternalInput").ap()
    W_xp_d = nc.dram_tensor("W_xproj", [DI, 33], F32, kind="ExternalInput").ap()
    w_dt_d = nc.dram_tensor("w_dt", [1, DI], F32, kind="ExternalInput").ap()
    b_dt_d = nc.dram_tensor("b_dt", [DI], F32, kind="ExternalInput").ap()
    A_log_d = nc.dram_tensor("A_log", [DI, NS], F32, kind="ExternalInput").ap()
    D_par_d = nc.dram_tensor("D_param", [DI], F32, kind="ExternalInput").ap()
    W_out_d = nc.dram_tensor("W_out", [DI, D], F32, kind="ExternalInput").ap()
    w_g_d = nc.dram_tensor("w_gate", [II, D], F32, kind="ExternalInput").ap()
    w_u_d = nc.dram_tensor("w_up", [II, D], F32, kind="ExternalInput").ap()
    w_d_d = nc.dram_tensor("w_down", [D, II], F32, kind="ExternalInput").ap()
    out_d = nc.dram_tensor("out", [S, D], F32, kind="ExternalOutput").ap()
    xz_s = nc.dram_tensor("xz_s", [4 * D, S], BF16, kind="Internal").ap()
    ffn_sc = nc.dram_tensor("ffn_sc", [S, D], F32, kind="Internal").ap()

    import contextlib
    with contextlib.ExitStack() as ctx:
        def sem(name):
            return ctx.enter_context(nc.semaphore(name))

        sv, ss, st_, sg, sy = (sem(n) for n in ("sv", "ss", "st", "sg", "sy"))

        P = _Prog(
            sems={"vector": sv, "scalar": ss, "tensor": st_, "gpsimd": sg,
                  "sync": sy},
            mk_sem=sem,
        )

        def sbt(stack, name, shape, dt=F32):
            return stack.enter_context(nc.sbuf_tensor(name, shape, dt))

        A_sb = [sbt(ctx, f"A_{d}", [128, NS]) for d in range(16)]
        cb_sb = sbt(ctx, "cb_sb", [128, 16])
        bdt_sb = sbt(ctx, "bdt_sb", [128, 16])
        dpar_sb = sbt(ctx, "dpar_sb", [128, 16])
        cw_sb = sbt(ctx, "cw_sb", [128, 64])
        ones_row = sbt(ctx, "ones_row", [1, 128], BF16)
        iden_f = sbt(ctx, "iden_f", [128, 128])
        iden_b = sbt(ctx, "iden_b", [128, 128], BF16)
        wxp_b = sbt(ctx, "wxp_b", [128, 16 * 33], BF16)
        hstate = sbt(ctx, "hstate", [128, 256])
        sm_a = sbt(ctx, "sm_a", [128, 32])
        sm_b = sbt(ctx, "sm_b", [128, 216])
        sm_c = sbt(ctx, "sm_c", [128, 216])

        p = [ctx.enter_context(nc.psum_tensor(f"p{i}", [128, 512], F32))
             for i in range(6)]
        pT_ = [ctx.enter_context(nc.psum_tensor(f"pT{w}", [128, 1024], BF16))
               for w in range(2)]

        def quantN(tag, src_name, src_ap, dst_name, dst_ap, F,
                   qt, qs, l0, l2, l3, smo):
            nb = F // 16
            smb = sm_b[:, smo:smo + nb]
            smc = sm_c[:, smo:smo + nb]
            P.op("vector", lambda v: v.tensor_reduce(
                out=smb, in_=src_ap.rearrange("p (b w) -> p b w", w=16),
                axis=AX.X, op=ALU.max, apply_absolute_value=True),
                reads=[src_name], writes=["qsc" + tag])
            P.op("vector", lambda v: v.tensor_scalar(
                out=smb, in0=smb, scalar1=1.0 / 6.0,
                scalar2=1e-12, op0=ALU.mult, op1=ALU.max),
                reads=["qsc" + tag], writes=["qsc" + tag])
            P.op("vector", lambda v: v.reciprocal(out=smc, in_=smb),
                 reads=["qsc" + tag], writes=["qrs" + tag])
            rb = smc.unsqueeze(2).broadcast_to([128, nb, 16])
            sb_ = smb.unsqueeze(2).broadcast_to([128, nb, 16])
            P.op("vector", lambda v: v.tensor_tensor(
                out=qt.rearrange("p (b w) -> p b w", w=16),
                in0=src_ap.rearrange("p (b w) -> p b w", w=16),
                in1=rb, op=ALU.mult),
                reads=[src_name, "qrs" + tag], writes=["qt" + tag])
            P.op("scalar", lambda s: s.activation(
                out=qs, in_=qt, func=AF.Square),
                reads=["qt" + tag], writes=["qsq" + tag])
            P.op("vector", lambda v: v.tensor_scalar(
                out=l0, in0=qs, scalar1=MIDS[0] * MIDS[0], scalar2=None,
                op0=ALU.is_gt), reads=["qsq" + tag], writes=["ql0" + tag])
            for i in (1, 2, 3):
                P.op("vector", lambda v, i=i: v.scalar_tensor_tensor(
                    out=l0, in0=qs, scalar=MIDS[i] * MIDS[i],
                    op0=ALU.is_gt, in1=l0, op1=ALU.add),
                    reads=["qsq" + tag, "ql0" + tag], writes=["ql0" + tag])
            P.op("vector", lambda v: v.tensor_scalar(
                out=l2, in0=qs, scalar1=MIDS[4] * MIDS[4], scalar2=None,
                op0=ALU.is_gt), reads=["qsq" + tag], writes=["ql2" + tag])
            P.op("vector", lambda v: v.scalar_tensor_tensor(
                out=l2, in0=qs, scalar=MIDS[5] * MIDS[5],
                op0=ALU.is_gt, in1=l2, op1=ALU.add),
                reads=["qsq" + tag, "ql2" + tag], writes=["ql2" + tag])
            P.op("vector", lambda v: v.tensor_scalar(
                out=l3, in0=qs, scalar1=MIDS[6] * MIDS[6], scalar2=2.0,
                op0=ALU.is_gt, op1=ALU.mult),
                reads=["qsq" + tag], writes=["ql3" + tag])
            P.op("gpsimd", lambda g: g.tensor_add(out=l2, in0=l2, in1=l3),
                 reads=["ql2" + tag, "ql3" + tag], writes=["ql2" + tag])
            P.op("vector", lambda v: v.scalar_tensor_tensor(
                out=l0, in0=l0, scalar=0.5, op0=ALU.mult,
                in1=l2, op1=ALU.add),
                reads=["ql0" + tag, "ql2" + tag], writes=["ql0" + tag])
            P.op("vector", lambda v: v.tensor_scalar(
                out=qs, in0=qt, scalar1=0.0, scalar2=2.0,
                op0=ALU.is_ge, op1=ALU.mult),
                reads=["qt" + tag], writes=["qsq" + tag])
            P.op("vector", lambda v: v.scalar_tensor_tensor(
                out=qs.rearrange("p (b w) -> p b w", w=16),
                in0=qs.rearrange("p (b w) -> p b w", w=16),
                scalar=-1.0, op0=ALU.add, in1=sb_, op1=ALU.mult),
                reads=["qsq" + tag, "qsc" + tag], writes=["qsq" + tag])
            P.op("vector", lambda v: v.tensor_tensor(
                out=dst_ap, in0=l0, in1=qs, op=ALU.mult),
                reads=["ql0" + tag, "qsq" + tag], writes=[dst_name])

        tg_par = [0]

        def transpose_group(src_name, src_ap_fn, n_tiles, dst_fn, dst_name,
                            evac_eng="scalar"):
            done = 0
            while done < n_tiles:
                cnt = min(8, n_tiles - done)
                pg = tg_par[0] & 1
                tg_par[0] += 1
                pT = pT_[pg]

                def mk_tr(t, done=done, cnt=cnt, pT=pT):
                    r = None
                    for kk in range(cnt):
                        r = t.transpose(
                            out=pT[:, kk * 128:(kk + 1) * 128],
                            in_=src_ap_fn(done + kk),
                            identity=iden_b[:])
                    return r
                P.op("tensor", mk_tr, reads=[src_name, "iden_b"],
                     writes=[f"pTx{pg}"])

                def mk_ev(s, done=done, cnt=cnt, pT=pT):
                    r = None
                    for kk in range(cnt):
                        r = s.copy(out=dst_fn(done + kk),
                                   in_=pT[:, kk * 128:(kk + 1) * 128])
                    return r
                P.op(evac_eng, mk_ev, reads=[f"pTx{pg}"], writes=[dst_name])
                done += cnt

        for d in range(16):
            P.dma("sync", A_sb[d][:], A_log_d[d * 128:(d + 1) * 128, :],
                  writes=[f"A_{d}"], fam="cst")
        P.dma("sync", cw_sb[:].rearrange("p (d k) -> p d k", k=4),
              conv_w_d.rearrange("(d p) k -> p d k", p=128), writes=["cw"],
              fam="cst")
        P.dma("sync", cb_sb[:], conv_b_d.rearrange("(d p) -> p d", p=128),
              writes=["cb"], fam="cst", slow=True)
        P.dma("sync", bdt_sb[:], b_dt_d.rearrange("(d p) -> p d", p=128),
              writes=["bdt"], fam="cst", slow=True)
        P.dma("sync", dpar_sb[:], D_par_d.rearrange("(d p) -> p d", p=128),
              writes=["dpar"], fam="cst", slow=True)
        for d in range(16):
            P.op("scalar", lambda s, d=d: s.activation(
                out=A_sb[d][:], in_=A_sb[d][:], func=AF.Exp),
                reads=[f"A_{d}"], writes=[f"A_{d}"])
            P.op("vector", lambda v, d=d: v.tensor_scalar(
                out=A_sb[d][:], in0=A_sb[d][:], scalar1=-1.0, scalar2=None,
                op0=ALU.mult), reads=[f"A_{d}"], writes=[f"A_{d}"])
        P.op("gpsimd", lambda g: g.iota(
            out=iden_f[:], pattern=[[1, 128]], base=0, channel_multiplier=-1,
            allow_small_or_imprecise_dtypes=True), writes=["idf"])
        P.op("gpsimd", lambda g: g.tensor_scalar(
            out=iden_f[:], in0=iden_f[:], scalar1=0.0, scalar2=None,
            op0=ALU.is_equal), reads=["idf"], writes=["idf"])
        P.op("gpsimd", lambda g: g.tensor_copy(out=iden_b[:], in_=iden_f[:]),
             reads=["idf"], writes=["iden_b"])
        P.op("gpsimd", lambda g: g.memset(ones_row[:], 1.0), writes=["ones"])
        P.op("gpsimd", lambda g: g.memset(hstate[:], 0.0),
             writes=[f"hst{d}" for d in range(16)])

        with contextlib.ExitStack() as sAB:
            xnqT_r = sbt(sAB, "xnqT_r", [128, 8 * S], BF16)
            with contextlib.ExitStack() as sA:
                xnT_r = sbt(sA, "xnT_r", [128, 8 * S], BF16)
                winb = sbt(sA, "winb", [128, 8 * 4 * D], BF16)
                qtA = sbt(sA, "qtA", [128, 2 * D])
                qsA = sbt(sA, "qsA", [128, 2 * D])
                l0A = sbt(sA, "l0A", [128, 2 * D], BF16)
                l2A = sbt(sA, "l2A", [128, 2 * D], BF16)
                l3A = sbt(sA, "l3A", [128, 2 * D], BF16)
                QOA = sbt(sA, "QOA", [128, 2 * D], BF16)
                xa_ = [sbt(sA, f"xa{w}", [128, D]) for w in range(2)]
                xnf_ = [sbt(sA, f"xnf{w}", [128, D]) for w in range(2)]
                xnb_ = [sbt(sA, f"xnb{w}", [128, D], BF16) for w in range(2)]
                ev_ = [sbt(sA, f"ev{w}", [128, 512], BF16) for w in range(2)]

                for d in range(16):
                    P.dma("gpsimd", wxp_b[:, d * 33:(d + 1) * 33],
                          W_xp_d[d * 128:(d + 1) * 128, :],
                          writes=["wxp"], fam="wxl", waw=False)
                for k in range(8):
                    P.dma("gpsimd", winb[:, k * 4 * D:(k + 1) * 4 * D],
                          W_in_d[k * 128:(k + 1) * 128, :],
                          writes=["winb"], fam="winl", waw=False)

                for i in range(16 if PH >= 2 else 0):
                    w = i % 2
                    xa = xa_[w]
                    xt = f"xa{w}"
                    sm = (lambda c, w=w: sm_a[:, w * 16 + c:w * 16 + c + 1])
                    P.dma("sync", xa[:], x_d[i * 128:(i + 1) * 128, :],
                          writes=[xt], fam=xt)
                    P.op("scalar", lambda s, xa=xa, w=w: s.activation(
                        out=qsA[:, w * D:(w + 1) * D], in_=xa[:],
                        func=AF.Square,
                        accum_out=sm_a[:, w * 16:w * 16 + 1]), reads=[xt],
                        writes=[f"ln_ss{w}", f"qsqA{w}"])
                    P.op("vector", lambda v, xa=xa, sm=sm: v.tensor_reduce(
                        out=sm(1), in_=xa[:], axis=AX.X, op=ALU.add),
                        reads=[xt], writes=[f"ln_sum{w}"])
                    P.op("vector", lambda v, sm=sm: v.tensor_scalar(
                        out=sm(2), in0=sm(1), scalar1=1.0 / D,
                        scalar2=None, op0=ALU.mult),
                        reads=[f"ln_sum{w}"], writes=[f"ln_mean{w}"])
                    P.op("vector", lambda v, sm=sm: v.tensor_mul(
                        out=sm(3), in0=sm(2), in1=sm(2)),
                        reads=[f"ln_mean{w}"], writes=[f"ln_m2{w}"])
                    P.op("vector", lambda v, sm=sm: v.scalar_tensor_tensor(
                        out=sm(4), in0=sm(0), scalar=1.0 / D,
                        op0=ALU.mult, in1=sm(3), op1=ALU.subtract),
                        reads=[f"ln_ss{w}", f"ln_m2{w}"],
                        writes=[f"ln_var{w}"])
                    P.op("vector", lambda v, sm=sm: v.tensor_scalar(
                        out=sm(4), in0=sm(4), scalar1=EPS,
                        scalar2=None, op0=ALU.add),
                        reads=[f"ln_var{w}"], writes=[f"ln_var{w}"])
                    P.op("scalar", lambda s, sm=sm: s.activation(
                        out=sm(5), in_=sm(4), func=AF.Ln),
                        reads=[f"ln_var{w}"], writes=[f"ln_lv{w}"])
                    P.op("vector", lambda v, sm=sm: v.tensor_scalar(
                        out=sm(5), in0=sm(5), scalar1=-0.5,
                        scalar2=None, op0=ALU.mult),
                        reads=[f"ln_lv{w}"], writes=[f"ln_lv{w}"])
                    P.op("scalar", lambda s, sm=sm: s.activation(
                        out=sm(6), in_=sm(5), func=AF.Exp),
                        reads=[f"ln_lv{w}"], writes=[f"ln_rstd{w}"])
                    P.op("vector", lambda v, xa=xa, w=w, sm=sm:
                         v.tensor_scalar(
                             out=xnf_[w][:], in0=xa[:], scalar1=sm(2),
                             scalar2=sm(6), op0=ALU.subtract, op1=ALU.mult),
                         reads=[xt, f"ln_mean{w}", f"ln_rstd{w}"],
                         writes=[f"xnf{w}"])
                    P.op("vector", lambda v, w=w: v.tensor_copy(
                        out=xnb_[w][:], in_=xnf_[w][:]),
                        reads=[f"xnf{w}"], writes=[f"xnb{w}"])
                    quantN(f"A{w}", f"xnf{w}", xnf_[w][:], f"xnq{w}",
                           QOA[:, w * D:(w + 1) * D], D,
                           qtA[:, w * D:(w + 1) * D],
                           qsA[:, w * D:(w + 1) * D],
                           l0A[:, w * D:(w + 1) * D],
                           l2A[:, w * D:(w + 1) * D],
                           l3A[:, w * D:(w + 1) * D], w * 64)
                    transpose_group(
                        f"xnb{w}",
                        lambda dj, w=w: xnb_[w][:, dj * 128:(dj + 1) * 128],
                        8,
                        lambda dj, i=i: xnT_r[:, dj * S + i * 128:
                                              dj * S + (i + 1) * 128],
                        "xnTr")
                    transpose_group(
                        f"xnq{w}",
                        lambda dj, w=w: QOA[:, w * D + dj * 128:
                                            w * D + (dj + 1) * 128], 8,
                        lambda dj, i=i: xnqT_r[:, dj * S + i * 128:
                                               dj * S + (i + 1) * 128],
                        "xnqTr")
                    if i % 4 == 3 and PH >= 4:
                        tt4 = i // 4
                        for fc in range(32):
                            pb = p[fc % 2]
                            pname = f"pg{fc % 2}"

                            def mk_mm(t, fc=fc, pb=pb, tt4=tt4):
                                r = None
                                for k in range(8):
                                    r = t.matmul(
                                        out=pb[:],
                                        lhsT=winb[:, k * 4 * D + fc * 128:
                                                  k * 4 * D +
                                                  (fc + 1) * 128],
                                        rhs=xnT_r[:, k * S + tt4 * 512:
                                                  k * S + (tt4 + 1) * 512],
                                        start=(k == 0), stop=(k == 7))
                                return r
                            P.op("tensor", mk_mm, reads=["winb", "xnTr"],
                                 writes=[pname])
                            ev = ev_[fc % 2]
                            ename = f"ev{fc % 2}"
                            P.op("scalar", lambda s, ev=ev, pb=pb: s.copy(
                                out=ev[:], in_=pb[:]), reads=[pname],
                                writes=[ename])
                            P.dma("sync",
                                  xz_s[fc * 128:(fc + 1) * 128,
                                       tt4 * 512:(tt4 + 1) * 512],
                                  ev[:], reads=[ename], writes=["Dxz"],
                                  fam="xzw", waw=False)
                P.barrier()

            W2 = 1408
            with contextlib.ExitStack() as sB:
                wgT = sbt(sB, "wgT", [128, 8 * II], BF16)
                wuT = sbt(sB, "wuT", [128, 8 * II], BF16)
                wdT = sbt(sB, "wdT", [128, 22 * D], BF16)
                qtB = sbt(sB, "qtB", [128, W2])
                qsB = sbt(sB, "qsB", [128, W2])
                l0B = sbt(sB, "l0B", [128, W2], BF16)
                l2B = sbt(sB, "l2B", [128, W2], BF16)
                l3B = sbt(sB, "l3B", [128, W2], BF16)
                QOB = sbt(sB, "QOB", [128, W2], BF16)
                wdl = [sbt(sB, "wdl0", [128, W2])] * 2
                hqT1 = sbt(sB, "hqT1", [128, W2], BF16)
                hqb2 = sbt(sB, "hqb2", [128, W2], BF16)
                dne = sbt(sB, "dne", [128, D])

                def q_half(tag, src_name, src_ap, dst_name, dst_ap, F, sl,
                           smo):
                    quantN(tag, src_name, src_ap, dst_name, dst_ap, F,
                           qtB[:, sl], qsB[:, sl], l0B[:, sl], l2B[:, sl],
                           l3B[:, sl], smo)

                for dmc in range(8 if PH >= 3 else 0):
                    for hf in range(2):
                        nm = "wdl0"
                        P.dma("sync", wdl[hf][:],
                              w_d_d[dmc * 128:(dmc + 1) * 128,
                                    hf * W2:(hf + 1) * W2],
                              writes=[nm], fam=nm)
                        q_half("Bh", nm, wdl[hf][:], "wdq",
                               QOB[:, 0:W2], W2, slice(0, W2), 128)
                        transpose_group(
                            "wdq",
                            lambda ic: QOB[:, ic * 128:(ic + 1) * 128], 11,
                            lambda ic, dmc=dmc, hf=hf:
                            wdT[:, (hf * 11 + ic) * D + dmc * 128:
                                (hf * 11 + ic) * D + (dmc + 1) * 128],
                            "wdT")
                P.barrier()
                for (src_w, dstT, nmT) in ((w_g_d, wgT, "wgT"),
                                           (w_u_d, wuT, "wuT")):
                    for it in range(22 if PH >= 3 else 0):
                        w = 0
                        nm = "wdl0"
                        P.dma("sync", wdl[w][:, 0:D],
                              src_w[it * 128:(it + 1) * 128, :],
                              writes=[nm], fam=nm)
                        q_half("Bh", nm, wdl[w][:, 0:D], "wq",
                               QOB[:, 0:D], D, slice(0, D), 128)
                        transpose_group(
                            "wq",
                            lambda dj: QOB[:, dj * 128:(dj + 1) * 128], 8,
                            lambda dj, it=it, dstT=dstT:
                            dstT[:, dj * II + it * 128:
                                 dj * II + (it + 1) * 128],
                            nmT)
                P.barrier()
                sgf = wdl[0]
                widths = (512, 512, 384)
                tails = []

                def emit_tail(rec):
                    ti, hf, par = rec
                    gbase = hf * 11
                    qsrc = QOB if par == 0 else hqb2
                    transpose_group(
                        f"hq{par}",
                        lambda ic, qsrc=qsrc:
                        qsrc[:, ic * 128:(ic + 1) * 128], 11,
                        lambda ic: hqT1[:, ic * 128:(ic + 1) * 128],
                        "hqT1")

                    def mk_down(t, gbase=gbase):
                        r = None
                        for ic in range(11):
                            lhs = hqT1[:, ic * 128:(ic + 1) * 128]
                            gic = gbase + ic
                            for mi in range(2):
                                r = t.matmul(
                                    out=p[mi][:, 0:512], lhsT=lhs,
                                    rhs=wdT[:, gic * D + mi * 512:
                                            gic * D + (mi + 1) * 512],
                                    start=(ic == 0), stop=(ic == 10))
                        return r
                    P.op("tensor", mk_down, reads=["hqT1", "wdT"],
                         writes=["pg0", "pg1"])
                    if hf == 0:
                        P.op("vector", lambda v: v.tensor_copy(
                            out=dne[:, 0:512], in_=p[0][:, 0:512]),
                            reads=["pg0"], writes=["dne"])
                        P.op("vector", lambda v: v.tensor_copy(
                            out=dne[:, 512:1024], in_=p[1][:, 0:512]),
                            reads=["pg1"], writes=["dne"])
                    else:
                        P.op("vector", lambda v: v.tensor_add(
                            out=dne[:, 0:512], in0=dne[:, 0:512],
                            in1=p[0][:, 0:512]),
                            reads=["pg0", "dne"], writes=["dne"])
                        P.op("vector", lambda v: v.tensor_add(
                            out=dne[:, 512:1024], in0=dne[:, 512:1024],
                            in1=p[1][:, 0:512]),
                            reads=["pg1", "dne"], writes=["dne"])
                        P.dma("sync", ffn_sc[ti * 128:(ti + 1) * 128, :],
                              dne[:], reads=["dne"], writes=["Dffn"],
                              fam="ffw", waw=False)

                for i in range(16 if PH >= 3 else 0):
                    for hf in range(2):
                        def mk_gu(t, i=i, hf=hf):
                            r = None
                            for kb in range(8):
                                lhs = xnqT_r[:, kb * S + i * 128:
                                             kb * S + (i + 1) * 128]
                                for gi in range(6):
                                    mi = gi % 3
                                    o0 = sum(widths[:mi])
                                    wT = wgT if gi < 3 else wuT
                                    r = t.matmul(
                                        out=p[(0 if gi < 3 else 3) + mi][
                                            :, 0:widths[mi]],
                                        lhsT=lhs,
                                        rhs=wT[:, kb * II + hf * W2 + o0:
                                               kb * II + hf * W2 + o0 +
                                               widths[mi]],
                                        start=(kb == 0), stop=(kb == 7))
                            return r
                        P.op("tensor", mk_gu,
                             reads=["xnqTr", "wgT", "wuT"],
                             writes=["pg0", "pg1", "pg2",
                                     "pu0", "pu1", "pu2"])

                        def mk_silu(s):
                            r = None
                            for mi in range(3):
                                o0 = sum(widths[:mi])
                                r = s.activation(
                                    out=sgf[:, o0:o0 + widths[mi]],
                                    in_=p[mi][:, 0:widths[mi]],
                                    func=AF.Silu)
                            return r
                        P.op("scalar", mk_silu,
                             reads=["pg0", "pg1", "pg2"], writes=["sgf"])

                        def mk_h(v):
                            r = None
                            for mi in range(3):
                                o0 = sum(widths[:mi])
                                r = v.tensor_mul(
                                    out=qsB[:, o0:o0 + widths[mi]],
                                    in0=sgf[:, o0:o0 + widths[mi]],
                                    in1=p[3 + mi][:, 0:widths[mi]])
                            return r
                        P.op("vector", mk_h,
                             reads=["sgf", "pu0", "pu1", "pu2"],
                             writes=["h_f", "qsqBh"])
                        par = (2 * i + hf) % 2
                        q_half("Bh", "h_f", qsB[:, 0:W2], f"hq{par}",
                               (QOB if par == 0 else hqb2)[:, 0:W2],
                               W2, slice(0, W2), 128)
                        if tails:
                            emit_tail(tails.pop(0))
                        tails.append((i, hf, par))
                while tails:
                    emit_tail(tails.pop(0))
                P.barrier()

        with contextlib.ExitStack() as sD:
            diag_cv = sbt(sD, "diag_cv", [128, 16 * 4 * 128], BF16)
            diag_D = sbt(sD, "diag_D", [128, 16 * 128], BF16)
            woutb = sbt(sD, "woutb", [128, 16 * D], BF16)
            wdt_row = sbt(sD, "wdt_row", [1, DI])
            xin_b = sbt(sD, "xin_b", [128, 16 * 260], BF16)
            sz_ = [sbt(sD, f"sz{w}", [128, 16 * TC], BF16) for w in range(2)]
            xc_ = [sbt(sD, f"xc{w}", [128, 16 * TC], BF16) for w in range(2)]
            delt = sbt(sD, "delt", [128, 16 * TC])
            u_b = sbt(sD, "u_b", [128, 16 * TC], BF16)
            dbx_ = [sbt(sD, f"dbx{w}", [128, NS * TC], BF16)
                    for w in range(2)]
            bcB = sbt(sD, "bcB", [128, NS * TC], BF16)
            bcC = sbt(sD, "bcC", [128, NS * TC], BF16)
            hbuf = sbt(sD, "hbuf", [128, NS * TC], BF16)
            y2T = sbt(sD, "y2T", [128, 16 * TC], BF16)
            ssm_f = sbt(sD, "ssm_f", [33, TC])
            bc_flat = sbt(sD, "bc_flat", [1, 16 * TC], BF16)
            arg_sp = sbt(sD, "arg_sp", [128, 2 * TC])
            na_sp = sbt(sD, "na_sp", [128, 2 * TC])
            dA_sb = sbt(sD, "dA_sb", [128, 4 * TC])
            xres = sbt(sD, "xres", [128, D])
            ffn_t = sbt(sD, "ffn_t", [128, D])
            out_t = sbt(sD, "out_t", [128, D])

            for d in range(16 if PH >= 5 else 0):
                for k in range(4):
                    off = (d * 4 + k) * 128
                    P.op("gpsimd", lambda g, d=d, k=k, off=off:
                         g.tensor_scalar(
                             out=diag_cv[:, off:off + 128], in0=iden_f[:],
                             scalar1=cw_sb[:, d * 4 + k:d * 4 + k + 1],
                             scalar2=None, op0=ALU.mult),
                         reads=["idf", "cw"], writes=["dgc"])
                P.op("gpsimd", lambda g, d=d: g.tensor_scalar(
                    out=diag_D[:, d * 128:(d + 1) * 128], in0=iden_f[:],
                    scalar1=dpar_sb[:, d:d + 1], scalar2=None, op0=ALU.mult),
                    reads=["idf", "dpar"], writes=["dgD"])

            P.dma("sync", wdt_row[:], w_dt_d, writes=["wdt_row"],
                  fam="wdtl")
            for k in range(16 if PH >= 5 else 0):
                P.dma("gpsimd", woutb[:, k * D:(k + 1) * D],
                      W_out_d[k * 128:(k + 1) * 128, :],
                      writes=["wor"], fam="wo", waw=False)

            for j in range(NJ if PH >= 5 else 0):
                c0 = j * TC
                jp = j % 2
                sz_b = sz_[jp]
                xc_b = xc_[jp]
                if j == 0:
                    for d in range(16):
                        o = d * 260
                        P.op("gpsimd", lambda g, o=o: g.memset(
                            xin_b[:, o:o + 3], 0.0), writes=["xinh"])
                for d in range(16):
                    o = d * 260
                    if j == 0:
                        P.dma("gpsimd", xin_b[:, o + 3:o + 3 + TC],
                              xz_s[d * 128:(d + 1) * 128, 0:TC],
                              reads=["Dxz"], writes=["xinb"], fam="xing",
                              waw=False)
                    else:
                        P.dma("gpsimd", xin_b[:, o:o + 3 + TC],
                              xz_s[d * 128:(d + 1) * 128, c0 - 3:c0 + TC],
                              reads=["Dxz"], writes=["xinb"], fam="xing",
                              waw=False)
                if DS >= 1:
                    P.dma("gpsimd",
                          sz_b[:].rearrange("p (d t) -> p d t", t=TC),
                          xz_s[DI:2 * DI, c0:c0 + TC]
                          .rearrange("(d p) t -> p d t", p=128),
                          reads=["Dxz"],
                          writes=[f"sz{jp}_{d}" for d in range(16)],
                          fam="ztlg")
                for d in range(16 if DS >= 1 else 0):
                    P.op("scalar", lambda s, d=d, sz_b=sz_b:
                         s.activation(
                             out=sz_b[:, d * TC:(d + 1) * TC],
                             in_=sz_b[:, d * TC:(d + 1) * TC],
                             func=AF.Silu), reads=[f"sz{jp}_{d}"],
                         writes=[f"sz{jp}_{d}"])
                for d in range(16 if DS >= 2 else 0):
                    pq = p[2] if d % 2 == 0 else p[3]
                    pcv = "P2" if d % 2 == 0 else "P3"

                    def mk_cv(t, d=d, pq=pq):
                        r = None
                        for k in range(4):
                            off = (d * 4 + k) * 128
                            r = t.matmul(
                                out=pq[:, 0:TC],
                                lhsT=diag_cv[:, off:off + 128],
                                rhs=xin_b[:, d * 260 + k:d * 260 + k + TC],
                                start=(k == 0), stop=(k == 3))
                        return r
                    P.op("tensor", mk_cv, reads=["dgc", "xinb", "xinh"],
                         writes=[pcv])
                    P.op("scalar", lambda s, d=d, pq=pq, xc_b=xc_b:
                         s.activation(
                             out=xc_b[:, d * TC:(d + 1) * TC],
                             in_=pq[:, 0:TC],
                             func=AF.Silu, bias=cb_sb[:, d:d + 1]),
                         reads=[pcv, "cb"], writes=[f"xc{jp}"])
                if DS < 3:
                    continue

                def mk_ssm(t, xc_b=xc_b, lo=0, pq=None):
                    r = None
                    for d in range(lo, lo + 8):
                        r = t.matmul(
                            out=pq[0:33, 0:TC],
                            lhsT=wxp_b[:, d * 33:(d + 1) * 33],
                            rhs=xc_b[:, d * TC:(d + 1) * TC],
                            start=(d == lo), stop=(d == lo + 7))
                    return r
                import functools
                P.op("tensor", functools.partial(mk_ssm, lo=0, pq=p[2]),
                     reads=["wxp", f"xc{jp}"], writes=["P2"])
                P.op("tensor", functools.partial(mk_ssm, lo=8, pq=p[3]),
                     reads=["wxp", f"xc{jp}"], writes=["P3"])
                P.op("vector", lambda v: v.tensor_copy(
                    out=ssm_f[:], in_=p[2][0:33, 0:TC]),
                    reads=["P2"], writes=["ssmf"])
                P.op("vector", lambda v: v.tensor_add(
                    out=ssm_f[:], in0=ssm_f[:], in1=p[3][0:33, 0:TC]),
                    reads=["ssmf", "P3"], writes=["ssmf"])
                for grp, rbase in (("B", 1), ("C", 17)):
                    P.dma("gpsimd",
                          bc_flat[0:1, :].rearrange("o (n t) -> o n t", t=TC),
                          ssm_f[rbase:rbase + NS, 0:TC].unsqueeze(0)
                          if False else
                          ssm_f[rbase:rbase + NS, 0:TC],
                          reads=["ssmf"], writes=["bcf"], fam="bcf")
                    dst = bcB if grp == "B" else bcC
                    dname = "bcB" if grp == "B" else "bcC"
                    for n in range(NS):
                        pq = (p[2], p[3], p[4], p[5])[n % 4]
                        pb = ("P2", "P3", "P4", "P5")[n % 4]
                        P.op("tensor", lambda t, n=n, pq=pq: t.matmul(
                            out=pq[:, 0:TC], lhsT=ones_row[:],
                            rhs=bc_flat[0:1, n * TC:(n + 1) * TC],
                            start=True, stop=True),
                            reads=["bcf", "ones"], writes=[pb])
                        P.op("scalar", lambda s, n=n, dst=dst, pq=pq:
                             s.copy(out=dst[:, n * TC:(n + 1) * TC],
                                    in_=pq[:, 0:TC]),
                             reads=[pb], writes=[dname])
                for d in range(16 if DS >= 4 else 0):
                    pq = p[2] if d % 2 == 0 else p[3]
                    pdl = "P2" if d % 2 == 0 else "P3"
                    base = 0 if d % 2 == 0 else TC
                    sl = slice(base, base + TC)
                    P.op("tensor", lambda t, d=d, pq=pq: t.matmul(
                        out=pq[:, 0:TC],
                        lhsT=wdt_row[0:1, d * 128:(d + 1) * 128],
                        rhs=ssm_f[0:1, 0:TC], start=True, stop=True),
                        reads=["ssmf", "wdt_row"], writes=[pdl])
                    P.op("vector", lambda v, d=d, sl=sl, pq=pq:
                         v.tensor_scalar(
                             out=arg_sp[:, sl], in0=pq[:, 0:TC],
                             scalar1=bdt_sb[:, d:d + 1], scalar2=None,
                             op0=ALU.add), reads=[pdl, "bdt"],
                         writes=[f"arg{d % 2}"])
                    P.op("scalar", lambda s, sl=sl: s.activation(
                        out=na_sp[:, sl], in_=arg_sp[:, sl], func=AF.Abs),
                        reads=[f"arg{d % 2}"], writes=[f"na{d % 2}"])
                    P.op("scalar", lambda s, sl=sl: s.activation(
                        out=na_sp[:, sl], in_=na_sp[:, sl], func=AF.Exp,
                        scale=-1.0), reads=[f"na{d % 2}"],
                        writes=[f"na{d % 2}"])
                    P.op("scalar", lambda s, sl=sl: s.activation(
                        out=na_sp[:, sl], in_=na_sp[:, sl], func=AF.Ln,
                        bias=1.0), reads=[f"na{d % 2}"],
                        writes=[f"na{d % 2}"])
                    P.op("vector", lambda v, d=d, sl=sl:
                         v.scalar_tensor_tensor(
                             out=delt[:, d * TC:(d + 1) * TC],
                             in0=arg_sp[:, sl], scalar=0.0, op0=ALU.max,
                             in1=na_sp[:, sl], op1=ALU.add),
                         reads=[f"arg{d % 2}", f"na{d % 2}"],
                         writes=[f"dl{d}"])
                y2_pending = []
                for d in range(16 if DS >= 5 else 0):
                    dp = d % 2
                    dsl = delt[:, d * TC:(d + 1) * TC]
                    dbx = dbx_[dp]
                    P.op("gpsimd", lambda g, d=d, xc_b=xc_b:
                         g.tensor_mul(
                             out=u_b[:, d * TC:(d + 1) * TC],
                             in0=delt[:, d * TC:(d + 1) * TC],
                             in1=xc_b[:, d * TC:(d + 1) * TC]),
                         reads=[f"dl{d}", f"xc{jp}"], writes=[f"u{d}"])
                    for n4 in range(4):
                        def mk_dbx(g, n4=n4, d=d, dbx=dbx):
                            r = None
                            for n in range(n4 * 4, n4 * 4 + 4):
                                r = g.tensor_mul(
                                    out=dbx[:, n * TC:(n + 1) * TC],
                                    in0=bcB[:, n * TC:(n + 1) * TC],
                                    in1=u_b[:, d * TC:(d + 1) * TC])
                            return r
                        P.op("gpsimd", mk_dbx,
                             reads=["bcB", f"u{d}"], writes=[f"dbx{dp}"])
                    for n in range(NS if DS >= 6 else 0):
                        np_ = n % 4
                        pa = dA_sb[:, np_ * TC:(np_ + 1) * TC]
                        paname = f"dA{np_}"
                        if n % 2 == 0:
                            np1 = (n + 1) % 4
                            pa1 = dA_sb[:, np1 * TC:(np1 + 1) * TC]

                            def mk_da(s, d=d, n=n, pa=pa, pa1=pa1, dsl=dsl):
                                s.activation(out=pa, in_=dsl, func=AF.Exp,
                                             scale=A_sb[d][:, n:n + 1])
                                return s.activation(
                                    out=pa1, in_=dsl, func=AF.Exp,
                                    scale=A_sb[d][:, n + 1:n + 2])
                            P.op("scalar", mk_da,
                                 reads=[f"dl{d}", f"A_{d}"],
                                 writes=[paname, f"dA{np1}"])
                        if DS < 7:
                            continue
                        P.op("vector", lambda v, d=d, n=n, pa=pa, dbx=dbx:
                             v.tensor_tensor_scan(
                                 out=hbuf[:, n * TC:(n + 1) * TC],
                                 data0=pa,
                                 data1=dbx[:, n * TC:(n + 1) * TC],
                                 initial=hstate[:, d * NS + n:
                                                d * NS + n + 1],
                                 op0=ALU.mult, op1=ALU.add),
                             reads=[paname, f"dbx{dp}", f"hst{d}"],
                             writes=["hbufS"])
                    if DS < 8:
                        continue
                    P.op("vector", lambda v, d=d: v.tensor_copy(
                        out=hstate[:, d * NS:(d + 1) * NS].unsqueeze(2),
                        in_=hbuf[:].rearrange("p (n t) -> p n t",
                                              t=TC)[:, :, TC - 1:TC]),
                        reads=["hbufS"], writes=[f"hst{d}"])
                    P.op("vector", lambda v, dbx=dbx: v.tensor_mul(
                        out=dbx[:], in0=hbuf[:], in1=bcC[:]),
                        reads=["hbufS", "bcC"], writes=[f"dbx{dp}"])
                    py = p[4] if dp == 0 else p[5]
                    pyn = "P4" if dp == 0 else "P5"

                    def mk_y(t, d=d, dbx=dbx, py=py, xc_b=xc_b):
                        r = None
                        for n in range(NS):
                            r = t.matmul(out=py[:, 0:TC],
                                         lhsT=iden_b[:],
                                         rhs=dbx[:, n * TC:(n + 1) * TC],
                                         start=(n == 0), stop=False)
                        r = t.matmul(out=py[:, 0:TC],
                                     lhsT=diag_D[:, d * 128:(d + 1) * 128],
                                     rhs=xc_b[:, d * TC:(d + 1) * TC],
                                     start=False, stop=True)
                        return r
                    P.op("tensor", mk_y,
                         reads=[f"dbx{dp}", "iden_b", "dgD", f"xc{jp}"],
                         writes=[pyn])
                    y2_pending.append((d, py, pyn))
                    if len(y2_pending) > 1:
                        dd, pyd, pynd = y2_pending.pop(0)
                        P.op("vector", lambda v, d=dd, py=pyd, sz_b=sz_b:
                             v.tensor_mul(
                                 out=y2T[:, d * TC:(d + 1) * TC],
                                 in0=py[:, 0:TC],
                                 in1=sz_b[:, d * TC:(d + 1) * TC]),
                             reads=[pynd, f"sz{jp}_{dd}"],
                             writes=[f"y2_{dd}"])
                while y2_pending:
                    dd, pyd, pynd = y2_pending.pop(0)
                    P.op("vector", lambda v, d=dd, py=pyd, sz_b=sz_b:
                         v.tensor_mul(
                             out=y2T[:, d * TC:(d + 1) * TC],
                             in0=py[:, 0:TC],
                             in1=sz_b[:, d * TC:(d + 1) * TC]),
                         reads=[pynd, f"sz{jp}_{dd}"],
                         writes=[f"y2_{dd}"])
                for h2 in range(2 if DS >= 9 else 0):
                    r0 = c0 + h2 * 128
                    op_ = h2 % 2
                    P.dma("sync", xres[:], x_d[r0:r0 + 128, :],
                          writes=["xres"], fam="xrs")
                    P.dma("sync", ffn_t[:], ffn_sc[r0:r0 + 128, :],
                          reads=["Dffn"], writes=["ffnt"], fam="fft")
                    for k in range(16):
                        def mk_wo1(t, h2=h2, k=k):
                            lhs = y2T[:, k * TC + h2 * 128:
                                      k * TC + h2 * 128 + 128]
                            r = None
                            for mi in range(2):
                                r = t.matmul(
                                    out=p[mi][:, 0:512], lhsT=lhs,
                                    rhs=woutb[:, k * D + mi * 512:
                                              k * D + (mi + 1) * 512],
                                    start=(k == 0), stop=(k == 15))
                            return r
                        P.op("tensor", mk_wo1,
                             reads=["wor"] + [f"y2_{dd}" for dd in range(16)],
                             writes=["pg0", "pg1"])
                    P.op("vector", lambda v: v.tensor_add(
                        out=out_t[:, 0:512], in0=p[0][:, 0:512],
                        in1=xres[:, 0:512]),
                        reads=["pg0", "xres"], writes=["outt"])
                    P.op("vector", lambda v: v.tensor_add(
                        out=out_t[:, 512:1024], in0=p[1][:, 0:512],
                        in1=xres[:, 512:1024]),
                        reads=["pg1", "xres"], writes=["outt"])
                    P.op("vector", lambda v: v.tensor_add(
                        out=out_t[:], in0=out_t[:], in1=ffn_t[:]),
                        reads=["outt", "ffnt"], writes=["outt"])
                    P.dma("sync", out_d[r0:r0 + 128, :], out_t[:],
                          reads=["outt"], fam="otw", waw=False)

        P.drain()

        blk = ctx.enter_context(nc.Block())

        def replay(handle, atoms):
            for a in atoms:
                if a[0] == "w":
                    handle.wait_ge(a[1], a[2])
                elif a[0] == "c":
                    instr = a[1](handle)
                    instr.then_inc(a[2], 1)
                elif a[0] == "d":
                    handle.dma_start(
                        out=a[1], in_=a[2],
                        allow_slow_non_contiguous=a[4]).then_inc(a[3], 16)
                elif a[0] == "b":
                    handle.nop().then_inc(a[1], 1)
                    handle.wait_ge(a[1], a[2])

        @blk.vector
        def _(v):
            replay(v, P.atoms["vector"])

        @blk.scalar
        def _(s):
            replay(s, P.atoms["scalar"])

        @blk.tensor
        def _(t):
            replay(t, P.atoms["tensor"])

        @blk.gpsimd
        def _(g):
            replay(g, P.atoms["gpsimd"])

        @blk.sync
        def _(sy_):
            replay(sy_, P.atoms["sync"])

    return nc


_NC_CACHE = [None]


def kernel(**inputs):
    n = 8
    x_full = np.ascontiguousarray(np.asarray(inputs["L0_attn_x"],
                                             dtype=np.float32))
    shards = [np.ascontiguousarray(x_full[i]) for i in range(n)]
    conv_w = np.ascontiguousarray(
        np.asarray(inputs["L0_attn_conv_w"], dtype=np.float32).reshape(DI, 4))
    w_dt = np.ascontiguousarray(
        np.asarray(inputs["L0_attn_w_dt"], dtype=np.float32).reshape(1, DI))
    common = {
        "W_in": np.ascontiguousarray(np.asarray(inputs["L0_attn_W_in"],
                                                dtype=np.float32)),
        "conv_w": conv_w,
        "conv_b": np.ascontiguousarray(np.asarray(inputs["L0_attn_conv_b"],
                                                  dtype=np.float32)),
        "W_xproj": np.ascontiguousarray(np.asarray(inputs["L0_attn_W_xproj"],
                                                   dtype=np.float32)),
        "w_dt": w_dt,
        "b_dt": np.ascontiguousarray(np.asarray(inputs["L0_attn_b_dt"],
                                                dtype=np.float32)),
        "A_log": np.ascontiguousarray(np.asarray(inputs["L0_attn_A_log"],
                                                 dtype=np.float32)),
        "D_param": np.ascontiguousarray(np.asarray(inputs["L0_attn_D_param"],
                                                   dtype=np.float32)),
        "W_out": np.ascontiguousarray(np.asarray(inputs["L0_attn_W_out"],
                                                 dtype=np.float32)),
        "w_gate": np.ascontiguousarray(np.asarray(inputs["L0_ffn_w_gate"],
                                                  dtype=np.float32)),
        "w_up": np.ascontiguousarray(np.asarray(inputs["L0_ffn_w_up"],
                                                dtype=np.float32)),
        "w_down": np.ascontiguousarray(np.asarray(inputs["L0_ffn_w_down"],
                                                  dtype=np.float32)),
    }
    in_maps = [dict(common, x=shards[i]) for i in range(n)]
    if _NC_CACHE[0] is None:
        _NC_CACHE[0] = build_nc()
    res = run_bass_kernel_spmd(_NC_CACHE[0], in_maps, core_ids=list(range(n)))
    out = np.stack([np.asarray(r["out"], dtype=np.float32)
                    for r in res.results], axis=0)
    return out


# version 25
# speedup vs baseline: 1.2486x; 1.0157x over previous
import os
import numpy as np
import concourse.bass as bass
import concourse.mybir as mybir
from concourse.bass_utils import run_bass_kernel_spmd

F32 = mybir.dt.float32
BF16 = mybir.dt.bfloat16
AF = mybir.ActivationFunctionType
ALU = mybir.AluOpType
AX = mybir.AxisListType

S = 2048
PH = int(os.environ.get("ANT_PH", "5"))
DS = int(os.environ.get("ANT_DS", "9"))
D = 1024
DI = 2048
NS = 16
II = 2816
TC = 256
NJ = S // TC
EPS = 1e-5
MIDS = [0.25, 0.75, 1.25, 1.75, 2.5, 3.5, 5.0]
WTS = [0.5, 0.5, 0.5, 0.5, 1.0, 1.0, 2.0]

ENGS = ("vector", "scalar", "tensor", "gpsimd", "sync")


class _Buf:
    __slots__ = ("prod", "readers")

    def __init__(self):
        self.prod = None
        self.readers = []


class _Prog:

    SELF_SYNC = {"vector", "scalar", "tensor", "gpsimd", "sync"}

    def __init__(self, sems, mk_sem):
        self.atoms = {e: [] for e in ENGS}
        self.sems = sems
        self.mk_sem = mk_sem
        self.cnt = {e: 0 for e in ENGS}
        self.marks = {e: {} for e in ENGS}
        self.bufs = {}
        self.fams = {}
        self.nbar = 0
        self.sbar = mk_sem("sbar")

    def buf(self, name):
        if name not in self.bufs:
            self.bufs[name] = _Buf()
        return self.bufs[name]

    def _wait_raw(self, e, sem, val):
        if val <= 0:
            return
        k = id(sem)
        if self.marks[e].get(k, 0) >= val:
            return
        self.atoms[e].append(("w", sem, val))
        self.marks[e][k] = val

    def _wait_tok(self, e, tok):
        if tok is None:
            return
        if tok[0] == "F":
            fam = tok[1]
            sem, cnt, _q = self.fams[fam]
            self._wait_raw(e, sem, cnt)
        else:
            sem, val = tok[1], tok[2]
            if sem is self.sems.get(e) and e not in self.SELF_SYNC:
                return
            self._wait_raw(e, sem, val)

    def _deps(self, e, reads, writes):
        for n in reads:
            b = self.buf(n)
            self._wait_tok(e, b.prod)
        for n in writes:
            b = self.buf(n)
            self._wait_tok(e, b.prod)
            for r in b.readers:
                self._wait_tok(e, r)

    def _finish(self, tok, reads, writes):
        for n in writes:
            b = self.buf(n)
            b.prod = tok
            b.readers = []
        for n in reads:
            self.buf(n).readers.append(tok)

    def op(self, e, fn, reads=(), writes=()):
        self._deps(e, reads, writes)
        self.atoms[e].append(("c", fn, self.sems[e]))
        self.cnt[e] += 1
        self._finish(("E", self.sems[e], self.cnt[e]), reads, writes)

    def dma(self, e, out_ap, in_ap, reads=(), writes=(), fam="gen",
            slow=False, waw=True):
        if fam not in self.fams:
            self.fams[fam] = [self.mk_sem("f_" + fam), 0, e]
        rec = self.fams[fam]
        assert rec[2] == e, f"fam {fam} used on two queues"
        for n in reads:
            self._wait_tok(e, self.buf(n).prod)
        for n in writes:
            b = self.buf(n)
            if waw:
                self._wait_tok(e, b.prod)
            for r in b.readers:
                self._wait_tok(e, r)
        self.atoms[e].append(("d", out_ap, in_ap, rec[0], slow))
        rec[1] += 16
        self._finish(("F", fam), reads, writes)

    def barrier(self):
        self.nbar += 1
        tgt = 5 * self.nbar
        for e in ENGS:
            for fam, (sem, cnt, q) in self.fams.items():
                if q == e and cnt:
                    self._wait_raw(e, sem, cnt)
            self.atoms[e].append(("b", self.sbar, tgt))
            self.marks[e][id(self.sbar)] = tgt
        for b in self.bufs.values():
            b.readers = []

    def drain(self):
        for e in ENGS:
            for fam, (sem, cnt, q) in self.fams.items():
                if q == e and cnt:
                    self._wait_raw(e, sem, cnt)


def build_nc():
    nc = bass.Bass("TRN2", target_bir_lowering=False)

    x_d = nc.dram_tensor("x", [S, D], F32, kind="ExternalInput").ap()
    W_in_d = nc.dram_tensor("W_in", [D, 4 * D], F32, kind="ExternalInput").ap()
    conv_w_d = nc.dram_tensor("conv_w", [DI, 4], F32, kind="ExternalInput").ap()
    conv_b_d = nc.dram_tensor("conv_b", [DI], F32, kind="ExternalInput").ap()
    W_xp_d = nc.dram_tensor("W_xproj", [DI, 33], F32, kind="ExternalInput").ap()
    w_dt_d = nc.dram_tensor("w_dt", [1, DI], F32, kind="ExternalInput").ap()
    b_dt_d = nc.dram_tensor("b_dt", [DI], F32, kind="ExternalInput").ap()
    A_log_d = nc.dram_tensor("A_log", [DI, NS], F32, kind="ExternalInput").ap()
    D_par_d = nc.dram_tensor("D_param", [DI], F32, kind="ExternalInput").ap()
    W_out_d = nc.dram_tensor("W_out", [DI, D], F32, kind="ExternalInput").ap()
    w_g_d = nc.dram_tensor("w_gate", [II, D], F32, kind="ExternalInput").ap()
    w_u_d = nc.dram_tensor("w_up", [II, D], F32, kind="ExternalInput").ap()
    w_d_d = nc.dram_tensor("w_down", [D, II], F32, kind="ExternalInput").ap()
    out_d = nc.dram_tensor("out", [S, D], F32, kind="ExternalOutput").ap()
    xz_s = nc.dram_tensor("xz_s", [4 * D, S], BF16, kind="Internal").ap()
    ffn_sc = nc.dram_tensor("ffn_sc", [S, D], F32, kind="Internal").ap()

    import contextlib
    with contextlib.ExitStack() as ctx:
        def sem(name):
            return ctx.enter_context(nc.semaphore(name))

        sv, ss, st_, sg, sy = (sem(n) for n in ("sv", "ss", "st", "sg", "sy"))

        P = _Prog(
            sems={"vector": sv, "scalar": ss, "tensor": st_, "gpsimd": sg,
                  "sync": sy},
            mk_sem=sem,
        )

        def sbt(stack, name, shape, dt=F32):
            return stack.enter_context(nc.sbuf_tensor(name, shape, dt))

        A_sb = [sbt(ctx, f"A_{d}", [128, NS]) for d in range(16)]
        cb_sb = sbt(ctx, "cb_sb", [128, 16])
        bdt_sb = sbt(ctx, "bdt_sb", [128, 16])
        dpar_sb = sbt(ctx, "dpar_sb", [128, 16])
        cw_sb = sbt(ctx, "cw_sb", [128, 64])
        ones_row = sbt(ctx, "ones_row", [1, 128], BF16)
        iden_f = sbt(ctx, "iden_f", [128, 128])
        iden_b = sbt(ctx, "iden_b", [128, 128], BF16)
        wxp_b = sbt(ctx, "wxp_b", [128, 16 * 33], BF16)
        hstate = sbt(ctx, "hstate", [128, 256])
        sm_a = sbt(ctx, "sm_a", [128, 32])
        sm_b = sbt(ctx, "sm_b", [128, 216])
        sm_c = sbt(ctx, "sm_c", [128, 216])

        p = [ctx.enter_context(nc.psum_tensor(f"p{i}", [128, 512], F32))
             for i in range(6)]
        pT_ = [ctx.enter_context(nc.psum_tensor(f"pT{w}", [128, 1024], BF16))
               for w in range(2)]

        def quantN(tag, src_name, src_ap, dst_name, dst_ap, F,
                   qt, qs, l0, l2, l3, smo):
            nb = F // 16
            smb = sm_b[:, smo:smo + nb]
            smc = sm_c[:, smo:smo + nb]
            P.op("vector", lambda v: v.tensor_reduce(
                out=smb, in_=src_ap.rearrange("p (b w) -> p b w", w=16),
                axis=AX.X, op=ALU.max, apply_absolute_value=True),
                reads=[src_name], writes=["qsc" + tag])
            P.op("vector", lambda v: v.tensor_scalar(
                out=smb, in0=smb, scalar1=1.0 / 6.0,
                scalar2=1e-12, op0=ALU.mult, op1=ALU.max),
                reads=["qsc" + tag], writes=["qsc" + tag])
            P.op("vector", lambda v: v.reciprocal(out=smc, in_=smb),
                 reads=["qsc" + tag], writes=["qrs" + tag])
            rb = smc.unsqueeze(2).broadcast_to([128, nb, 16])
            sb_ = smb.unsqueeze(2).broadcast_to([128, nb, 16])
            P.op("vector", lambda v: v.tensor_tensor(
                out=qt.rearrange("p (b w) -> p b w", w=16),
                in0=src_ap.rearrange("p (b w) -> p b w", w=16),
                in1=rb, op=ALU.mult),
                reads=[src_name, "qrs" + tag], writes=["qt" + tag])
            P.op("scalar", lambda s: s.activation(
                out=qs, in_=qt, func=AF.Square),
                reads=["qt" + tag], writes=["qsq" + tag])
            P.op("vector", lambda v: v.tensor_scalar(
                out=l0, in0=qs, scalar1=MIDS[0] * MIDS[0], scalar2=None,
                op0=ALU.is_gt), reads=["qsq" + tag], writes=["ql0" + tag])
            for i in (1, 2, 3):
                P.op("vector", lambda v, i=i: v.scalar_tensor_tensor(
                    out=l0, in0=qs, scalar=MIDS[i] * MIDS[i],
                    op0=ALU.is_gt, in1=l0, op1=ALU.add),
                    reads=["qsq" + tag, "ql0" + tag], writes=["ql0" + tag])
            P.op("vector", lambda v: v.tensor_scalar(
                out=l2, in0=qs, scalar1=MIDS[4] * MIDS[4], scalar2=None,
                op0=ALU.is_gt), reads=["qsq" + tag], writes=["ql2" + tag])
            P.op("vector", lambda v: v.scalar_tensor_tensor(
                out=l2, in0=qs, scalar=MIDS[5] * MIDS[5],
                op0=ALU.is_gt, in1=l2, op1=ALU.add),
                reads=["qsq" + tag, "ql2" + tag], writes=["ql2" + tag])
            P.op("vector", lambda v: v.tensor_scalar(
                out=l3, in0=qs, scalar1=MIDS[6] * MIDS[6], scalar2=2.0,
                op0=ALU.is_gt, op1=ALU.mult),
                reads=["qsq" + tag], writes=["ql3" + tag])
            P.op("gpsimd", lambda g: g.tensor_add(out=l2, in0=l2, in1=l3),
                 reads=["ql2" + tag, "ql3" + tag], writes=["ql2" + tag])
            P.op("vector", lambda v: v.scalar_tensor_tensor(
                out=l0, in0=l0, scalar=0.5, op0=ALU.mult,
                in1=l2, op1=ALU.add),
                reads=["ql0" + tag, "ql2" + tag], writes=["ql0" + tag])
            P.op("vector", lambda v: v.tensor_scalar(
                out=qs, in0=qt, scalar1=0.0, scalar2=2.0,
                op0=ALU.is_ge, op1=ALU.mult),
                reads=["qt" + tag], writes=["qsq" + tag])
            P.op("vector", lambda v: v.scalar_tensor_tensor(
                out=qs.rearrange("p (b w) -> p b w", w=16),
                in0=qs.rearrange("p (b w) -> p b w", w=16),
                scalar=-1.0, op0=ALU.add, in1=sb_, op1=ALU.mult),
                reads=["qsq" + tag, "qsc" + tag], writes=["qsq" + tag])
            P.op("vector", lambda v: v.tensor_tensor(
                out=dst_ap, in0=l0, in1=qs, op=ALU.mult),
                reads=["ql0" + tag, "qsq" + tag], writes=[dst_name])

        tg_par = [0]

        def transpose_group(src_name, src_ap_fn, n_tiles, dst_fn, dst_name,
                            evac_eng="scalar"):
            done = 0
            while done < n_tiles:
                cnt = min(8, n_tiles - done)
                pg = tg_par[0] & 1
                tg_par[0] += 1
                pT = pT_[pg]

                def mk_tr(t, done=done, cnt=cnt, pT=pT):
                    r = None
                    for kk in range(cnt):
                        r = t.transpose(
                            out=pT[:, kk * 128:(kk + 1) * 128],
                            in_=src_ap_fn(done + kk),
                            identity=iden_b[:])
                    return r
                P.op("tensor", mk_tr, reads=[src_name, "iden_b"],
                     writes=[f"pTx{pg}"])

                def mk_ev(s, done=done, cnt=cnt, pT=pT):
                    r = None
                    for kk in range(cnt):
                        r = s.copy(out=dst_fn(done + kk),
                                   in_=pT[:, kk * 128:(kk + 1) * 128])
                    return r
                P.op(evac_eng, mk_ev, reads=[f"pTx{pg}"], writes=[dst_name])
                done += cnt

        for d in range(16):
            P.dma("sync", A_sb[d][:], A_log_d[d * 128:(d + 1) * 128, :],
                  writes=[f"A_{d}"], fam="cst")
        P.dma("sync", cw_sb[:].rearrange("p (d k) -> p d k", k=4),
              conv_w_d.rearrange("(d p) k -> p d k", p=128), writes=["cw"],
              fam="cst")
        P.dma("sync", cb_sb[:], conv_b_d.rearrange("(d p) -> p d", p=128),
              writes=["cb"], fam="cst", slow=True)
        P.dma("sync", bdt_sb[:], b_dt_d.rearrange("(d p) -> p d", p=128),
              writes=["bdt"], fam="cst", slow=True)
        P.dma("sync", dpar_sb[:], D_par_d.rearrange("(d p) -> p d", p=128),
              writes=["dpar"], fam="cst", slow=True)
        for d in range(16):
            P.op("scalar", lambda s, d=d: s.activation(
                out=A_sb[d][:], in_=A_sb[d][:], func=AF.Exp),
                reads=[f"A_{d}"], writes=[f"A_{d}"])
            P.op("vector", lambda v, d=d: v.tensor_scalar(
                out=A_sb[d][:], in0=A_sb[d][:], scalar1=-1.0, scalar2=None,
                op0=ALU.mult), reads=[f"A_{d}"], writes=[f"A_{d}"])
        P.op("gpsimd", lambda g: g.iota(
            out=iden_f[:], pattern=[[1, 128]], base=0, channel_multiplier=-1,
            allow_small_or_imprecise_dtypes=True), writes=["idf"])
        P.op("gpsimd", lambda g: g.tensor_scalar(
            out=iden_f[:], in0=iden_f[:], scalar1=0.0, scalar2=None,
            op0=ALU.is_equal), reads=["idf"], writes=["idf"])
        P.op("gpsimd", lambda g: g.tensor_copy(out=iden_b[:], in_=iden_f[:]),
             reads=["idf"], writes=["iden_b"])
        P.op("gpsimd", lambda g: g.memset(ones_row[:], 1.0), writes=["ones"])
        P.op("gpsimd", lambda g: g.memset(hstate[:], 0.0),
             writes=[f"hst{d}" for d in range(16)])

        with contextlib.ExitStack() as sAB:
            xnqT_r = sbt(sAB, "xnqT_r", [128, 8 * S], BF16)
            with contextlib.ExitStack() as sA:
                xnT_r = sbt(sA, "xnT_r", [128, 8 * S], BF16)
                winb = sbt(sA, "winb", [128, 8 * 4 * D], BF16)
                qtA = sbt(sA, "qtA", [128, 2 * D])
                qsA = sbt(sA, "qsA", [128, 2 * D])
                l0A = sbt(sA, "l0A", [128, 2 * D], BF16)
                l2A = sbt(sA, "l2A", [128, 2 * D], BF16)
                l3A = sbt(sA, "l3A", [128, 2 * D], BF16)
                QOA = sbt(sA, "QOA", [128, 2 * D], BF16)
                xa_ = [sbt(sA, f"xa{w}", [128, D]) for w in range(2)]
                xnf_ = [sbt(sA, f"xnf{w}", [128, D]) for w in range(2)]
                xnb_ = [sbt(sA, f"xnb{w}", [128, D], BF16) for w in range(2)]
                ev_ = [sbt(sA, f"ev{w}", [128, 512], BF16) for w in range(2)]

                for d in range(16):
                    P.dma("gpsimd", wxp_b[:, d * 33:(d + 1) * 33],
                          W_xp_d[d * 128:(d + 1) * 128, :],
                          writes=["wxp"], fam="wxl", waw=False)
                for k in range(8):
                    P.dma("gpsimd", winb[:, k * 4 * D:(k + 1) * 4 * D],
                          W_in_d[k * 128:(k + 1) * 128, :],
                          writes=["winb"], fam="winl", waw=False)

                for i in range(16 if PH >= 2 else 0):
                    w = i % 2
                    xa = xa_[w]
                    xt = f"xa{w}"
                    sm = (lambda c, w=w: sm_a[:, w * 16 + c:w * 16 + c + 1])
                    P.dma("sync", xa[:], x_d[i * 128:(i + 1) * 128, :],
                          writes=[xt], fam=xt)
                    P.op("scalar", lambda s, xa=xa, w=w: s.activation(
                        out=qsA[:, w * D:(w + 1) * D], in_=xa[:],
                        func=AF.Square,
                        accum_out=sm_a[:, w * 16:w * 16 + 1]), reads=[xt],
                        writes=[f"ln_ss{w}", f"qsqA{w}"])
                    P.op("vector", lambda v, xa=xa, sm=sm: v.tensor_reduce(
                        out=sm(1), in_=xa[:], axis=AX.X, op=ALU.add),
                        reads=[xt], writes=[f"ln_sum{w}"])
                    P.op("vector", lambda v, sm=sm: v.tensor_scalar(
                        out=sm(2), in0=sm(1), scalar1=1.0 / D,
                        scalar2=None, op0=ALU.mult),
                        reads=[f"ln_sum{w}"], writes=[f"ln_mean{w}"])
                    P.op("vector", lambda v, sm=sm: v.tensor_mul(
                        out=sm(3), in0=sm(2), in1=sm(2)),
                        reads=[f"ln_mean{w}"], writes=[f"ln_m2{w}"])
                    P.op("vector", lambda v, sm=sm: v.scalar_tensor_tensor(
                        out=sm(4), in0=sm(0), scalar=1.0 / D,
                        op0=ALU.mult, in1=sm(3), op1=ALU.subtract),
                        reads=[f"ln_ss{w}", f"ln_m2{w}"],
                        writes=[f"ln_var{w}"])
                    P.op("vector", lambda v, sm=sm: v.tensor_scalar(
                        out=sm(4), in0=sm(4), scalar1=EPS,
                        scalar2=None, op0=ALU.add),
                        reads=[f"ln_var{w}"], writes=[f"ln_var{w}"])
                    P.op("scalar", lambda s, sm=sm: s.activation(
                        out=sm(5), in_=sm(4), func=AF.Ln),
                        reads=[f"ln_var{w}"], writes=[f"ln_lv{w}"])
                    P.op("vector", lambda v, sm=sm: v.tensor_scalar(
                        out=sm(5), in0=sm(5), scalar1=-0.5,
                        scalar2=None, op0=ALU.mult),
                        reads=[f"ln_lv{w}"], writes=[f"ln_lv{w}"])
                    P.op("scalar", lambda s, sm=sm: s.activation(
                        out=sm(6), in_=sm(5), func=AF.Exp),
                        reads=[f"ln_lv{w}"], writes=[f"ln_rstd{w}"])
                    P.op("vector", lambda v, xa=xa, w=w, sm=sm:
                         v.tensor_scalar(
                             out=xnf_[w][:], in0=xa[:], scalar1=sm(2),
                             scalar2=sm(6), op0=ALU.subtract, op1=ALU.mult),
                         reads=[xt, f"ln_mean{w}", f"ln_rstd{w}"],
                         writes=[f"xnf{w}"])
                    P.op("vector", lambda v, w=w: v.tensor_copy(
                        out=xnb_[w][:], in_=xnf_[w][:]),
                        reads=[f"xnf{w}"], writes=[f"xnb{w}"])
                    quantN(f"A{w}", f"xnf{w}", xnf_[w][:], f"xnq{w}",
                           QOA[:, w * D:(w + 1) * D], D,
                           qtA[:, w * D:(w + 1) * D],
                           qsA[:, w * D:(w + 1) * D],
                           l0A[:, w * D:(w + 1) * D],
                           l2A[:, w * D:(w + 1) * D],
                           l3A[:, w * D:(w + 1) * D], w * 64)
                    transpose_group(
                        f"xnb{w}",
                        lambda dj, w=w: xnb_[w][:, dj * 128:(dj + 1) * 128],
                        8,
                        lambda dj, i=i: xnT_r[:, dj * S + i * 128:
                                              dj * S + (i + 1) * 128],
                        "xnTr")
                    transpose_group(
                        f"xnq{w}",
                        lambda dj, w=w: QOA[:, w * D + dj * 128:
                                            w * D + (dj + 1) * 128], 8,
                        lambda dj, i=i: xnqT_r[:, dj * S + i * 128:
                                               dj * S + (i + 1) * 128],
                        "xnqTr")
                    if i % 4 == 3 and PH >= 4:
                        tt4 = i // 4
                        for fc in range(32):
                            pb = p[fc % 2]
                            pname = f"pg{fc % 2}"

                            def mk_mm(t, fc=fc, pb=pb, tt4=tt4):
                                r = None
                                for k in range(8):
                                    r = t.matmul(
                                        out=pb[:],
                                        lhsT=winb[:, k * 4 * D + fc * 128:
                                                  k * 4 * D +
                                                  (fc + 1) * 128],
                                        rhs=xnT_r[:, k * S + tt4 * 512:
                                                  k * S + (tt4 + 1) * 512],
                                        start=(k == 0), stop=(k == 7))
                                return r
                            P.op("tensor", mk_mm, reads=["winb", "xnTr"],
                                 writes=[pname])
                            ev = ev_[fc % 2]
                            ename = f"ev{fc % 2}"
                            P.op("scalar", lambda s, ev=ev, pb=pb: s.copy(
                                out=ev[:], in_=pb[:]), reads=[pname],
                                writes=[ename])
                            P.dma("sync",
                                  xz_s[fc * 128:(fc + 1) * 128,
                                       tt4 * 512:(tt4 + 1) * 512],
                                  ev[:], reads=[ename], writes=["Dxz"],
                                  fam="xzw", waw=False)
                P.barrier()

            W2 = 1408
            with contextlib.ExitStack() as sB:
                wgT = sbt(sB, "wgT", [128, 8 * II], BF16)
                wuT = sbt(sB, "wuT", [128, 8 * II], BF16)
                wdT = sbt(sB, "wdT", [128, 22 * D], BF16)
                qtB = sbt(sB, "qtB", [128, W2])
                qsB = sbt(sB, "qsB", [128, W2])
                l0B = sbt(sB, "l0B", [128, W2], BF16)
                l2B = sbt(sB, "l2B", [128, W2], BF16)
                l3B = sbt(sB, "l3B", [128, W2], BF16)
                QOB = sbt(sB, "QOB", [128, W2], BF16)
                wdl = [sbt(sB, "wdl0", [128, W2])] * 2
                hqT1 = sbt(sB, "hqT1", [128, W2], BF16)
                hqb2 = sbt(sB, "hqb2", [128, W2], BF16)
                dne = sbt(sB, "dne", [128, D])

                def q_half(tag, src_name, src_ap, dst_name, dst_ap, F, sl,
                           smo):
                    quantN(tag, src_name, src_ap, dst_name, dst_ap, F,
                           qtB[:, sl], qsB[:, sl], l0B[:, sl], l2B[:, sl],
                           l3B[:, sl], smo)

                for dmc in range(8 if PH >= 3 else 0):
                    for hf in range(2):
                        nm = "wdl0"
                        P.dma("sync", wdl[hf][:],
                              w_d_d[dmc * 128:(dmc + 1) * 128,
                                    hf * W2:(hf + 1) * W2],
                              writes=[nm], fam=nm)
                        q_half("Bh", nm, wdl[hf][:], "wdq",
                               QOB[:, 0:W2], W2, slice(0, W2), 128)
                        transpose_group(
                            "wdq",
                            lambda ic: QOB[:, ic * 128:(ic + 1) * 128], 11,
                            lambda ic, dmc=dmc, hf=hf:
                            wdT[:, (hf * 11 + ic) * D + dmc * 128:
                                (hf * 11 + ic) * D + (dmc + 1) * 128],
                            "wdT")
                P.barrier()
                for (src_w, dstT, nmT) in ((w_g_d, wgT, "wgT"),
                                           (w_u_d, wuT, "wuT")):
                    for it in range(22 if PH >= 3 else 0):
                        w = 0
                        nm = "wdl0"
                        P.dma("sync", wdl[w][:, 0:D],
                              src_w[it * 128:(it + 1) * 128, :],
                              writes=[nm], fam=nm)
                        q_half("Bh", nm, wdl[w][:, 0:D], "wq",
                               QOB[:, 0:D], D, slice(0, D), 128)
                        transpose_group(
                            "wq",
                            lambda dj: QOB[:, dj * 128:(dj + 1) * 128], 8,
                            lambda dj, it=it, dstT=dstT:
                            dstT[:, dj * II + it * 128:
                                 dj * II + (it + 1) * 128],
                            nmT)
                P.barrier()
                sgf = wdl[0]
                widths = (512, 512, 384)
                tails = []

                def emit_tail(rec):
                    ti, hf, par = rec
                    gbase = hf * 11
                    qsrc = QOB if par == 0 else hqb2
                    transpose_group(
                        f"hq{par}",
                        lambda ic, qsrc=qsrc:
                        qsrc[:, ic * 128:(ic + 1) * 128], 11,
                        lambda ic: hqT1[:, ic * 128:(ic + 1) * 128],
                        "hqT1")

                    def mk_down(t, gbase=gbase):
                        r = None
                        for ic in range(11):
                            lhs = hqT1[:, ic * 128:(ic + 1) * 128]
                            gic = gbase + ic
                            for mi in range(2):
                                r = t.matmul(
                                    out=p[mi][:, 0:512], lhsT=lhs,
                                    rhs=wdT[:, gic * D + mi * 512:
                                            gic * D + (mi + 1) * 512],
                                    start=(ic == 0), stop=(ic == 10))
                        return r
                    P.op("tensor", mk_down, reads=["hqT1", "wdT"],
                         writes=["pg0", "pg1"])
                    if hf == 0:
                        P.op("vector", lambda v: v.tensor_copy(
                            out=dne[:, 0:512], in_=p[0][:, 0:512]),
                            reads=["pg0"], writes=["dne"])
                        P.op("vector", lambda v: v.tensor_copy(
                            out=dne[:, 512:1024], in_=p[1][:, 0:512]),
                            reads=["pg1"], writes=["dne"])
                    else:
                        P.op("vector", lambda v: v.tensor_add(
                            out=dne[:, 0:512], in0=dne[:, 0:512],
                            in1=p[0][:, 0:512]),
                            reads=["pg0", "dne"], writes=["dne"])
                        P.op("vector", lambda v: v.tensor_add(
                            out=dne[:, 512:1024], in0=dne[:, 512:1024],
                            in1=p[1][:, 0:512]),
                            reads=["pg1", "dne"], writes=["dne"])
                        P.dma("sync", ffn_sc[ti * 128:(ti + 1) * 128, :],
                              dne[:], reads=["dne"], writes=["Dffn"],
                              fam="ffw", waw=False)

                for i in range(16 if PH >= 3 else 0):
                    for hf in range(2):
                        def mk_gu(t, i=i, hf=hf):
                            r = None
                            for kb in range(8):
                                lhs = xnqT_r[:, kb * S + i * 128:
                                             kb * S + (i + 1) * 128]
                                for gi in range(6):
                                    mi = gi % 3
                                    o0 = sum(widths[:mi])
                                    wT = wgT if gi < 3 else wuT
                                    r = t.matmul(
                                        out=p[(0 if gi < 3 else 3) + mi][
                                            :, 0:widths[mi]],
                                        lhsT=lhs,
                                        rhs=wT[:, kb * II + hf * W2 + o0:
                                               kb * II + hf * W2 + o0 +
                                               widths[mi]],
                                        start=(kb == 0), stop=(kb == 7))
                            return r
                        P.op("tensor", mk_gu,
                             reads=["xnqTr", "wgT", "wuT"],
                             writes=["pg0", "pg1", "pg2",
                                     "pu0", "pu1", "pu2"])

                        def mk_silu(s):
                            r = None
                            for mi in range(3):
                                o0 = sum(widths[:mi])
                                r = s.activation(
                                    out=sgf[:, o0:o0 + widths[mi]],
                                    in_=p[mi][:, 0:widths[mi]],
                                    func=AF.Silu)
                            return r
                        P.op("scalar", mk_silu,
                             reads=["pg0", "pg1", "pg2"], writes=["sgf"])

                        def mk_h(v):
                            r = None
                            for mi in range(3):
                                o0 = sum(widths[:mi])
                                r = v.tensor_mul(
                                    out=qsB[:, o0:o0 + widths[mi]],
                                    in0=sgf[:, o0:o0 + widths[mi]],
                                    in1=p[3 + mi][:, 0:widths[mi]])
                            return r
                        P.op("vector", mk_h,
                             reads=["sgf", "pu0", "pu1", "pu2"],
                             writes=["h_f", "qsqBh"])
                        par = (2 * i + hf) % 2
                        q_half("Bh", "h_f", qsB[:, 0:W2], f"hq{par}",
                               (QOB if par == 0 else hqb2)[:, 0:W2],
                               W2, slice(0, W2), 128)
                        if tails:
                            emit_tail(tails.pop(0))
                        tails.append((i, hf, par))
                while tails:
                    emit_tail(tails.pop(0))
                P.barrier()

        with contextlib.ExitStack() as sD:
            diag_cv = sbt(sD, "diag_cv", [128, 16 * 4 * 128], BF16)
            diag_D = sbt(sD, "diag_D", [128, 16 * 128], BF16)
            woutb = sbt(sD, "woutb", [128, 16 * D], BF16)
            wdt_row = sbt(sD, "wdt_row", [1, DI])
            xin_b = sbt(sD, "xin_b", [128, 16 * 260], BF16)
            sz_ = [sbt(sD, f"sz{w}", [128, 16 * TC], BF16) for w in range(2)]
            xc_ = [sbt(sD, f"xc{w}", [128, 16 * TC], BF16) for w in range(2)]
            delt = sbt(sD, "delt", [128, 16 * TC])
            u_b = sbt(sD, "u_b", [128, 16 * TC], BF16)
            dbx_ = [sbt(sD, f"dbx{w}", [128, NS * TC], BF16)
                    for w in range(2)]
            bcB = sbt(sD, "bcB", [128, NS * TC], BF16)
            bcC = sbt(sD, "bcC", [128, NS * TC], BF16)
            hbuf = sbt(sD, "hbuf", [128, NS * TC], BF16)
            y2T = sbt(sD, "y2T", [128, 16 * TC], BF16)
            ssm_f = sbt(sD, "ssm_f", [33, TC])
            bc_flat = sbt(sD, "bc_flat", [1, 16 * TC], BF16)
            arg_sp = sbt(sD, "arg_sp", [128, 2 * TC])
            na_sp = sbt(sD, "na_sp", [128, 2 * TC])
            dA_sb = sbt(sD, "dA_sb", [128, 4 * TC])
            xres = sbt(sD, "xres", [128, D])
            ffn_t = sbt(sD, "ffn_t", [128, D])
            out_t = sbt(sD, "out_t", [128, D])

            for d in range(16 if PH >= 5 else 0):
                for k in range(4):
                    off = (d * 4 + k) * 128
                    P.op("gpsimd", lambda g, d=d, k=k, off=off:
                         g.tensor_scalar(
                             out=diag_cv[:, off:off + 128], in0=iden_f[:],
                             scalar1=cw_sb[:, d * 4 + k:d * 4 + k + 1],
                             scalar2=None, op0=ALU.mult),
                         reads=["idf", "cw"], writes=["dgc"])
                P.op("gpsimd", lambda g, d=d: g.tensor_scalar(
                    out=diag_D[:, d * 128:(d + 1) * 128], in0=iden_f[:],
                    scalar1=dpar_sb[:, d:d + 1], scalar2=None, op0=ALU.mult),
                    reads=["idf", "dpar"], writes=["dgD"])

            P.dma("sync", wdt_row[:], w_dt_d, writes=["wdt_row"],
                  fam="wdtl")
            for k in range(16 if PH >= 5 else 0):
                P.dma("gpsimd", woutb[:, k * D:(k + 1) * D],
                      W_out_d[k * 128:(k + 1) * 128, :],
                      writes=["wor"], fam="wo", waw=False)

            for j in range(NJ if PH >= 5 else 0):
                c0 = j * TC
                jp = j % 2
                sz_b = sz_[jp]
                xc_b = xc_[jp]
                if j == 0:
                    for d in range(16):
                        o = d * 260
                        P.op("gpsimd", lambda g, o=o: g.memset(
                            xin_b[:, o:o + 3], 0.0), writes=["xinh"])
                for d in range(16):
                    o = d * 260
                    if j == 0:
                        P.dma("gpsimd", xin_b[:, o + 3:o + 3 + TC],
                              xz_s[d * 128:(d + 1) * 128, 0:TC],
                              reads=["Dxz"], writes=["xinb"], fam="xing",
                              waw=False)
                    else:
                        P.dma("gpsimd", xin_b[:, o:o + 3 + TC],
                              xz_s[d * 128:(d + 1) * 128, c0 - 3:c0 + TC],
                              reads=["Dxz"], writes=["xinb"], fam="xing",
                              waw=False)
                if DS >= 1:
                    P.dma("gpsimd",
                          sz_b[:].rearrange("p (d t) -> p d t", t=TC),
                          xz_s[DI:2 * DI, c0:c0 + TC]
                          .rearrange("(d p) t -> p d t", p=128),
                          reads=["Dxz"],
                          writes=[f"sz{jp}_{d}" for d in range(16)],
                          fam="ztlg")
                for d in range(16 if DS >= 1 else 0):
                    P.op("scalar", lambda s, d=d, sz_b=sz_b:
                         s.activation(
                             out=sz_b[:, d * TC:(d + 1) * TC],
                             in_=sz_b[:, d * TC:(d + 1) * TC],
                             func=AF.Silu), reads=[f"sz{jp}_{d}"],
                         writes=[f"sz{jp}_{d}"])
                for d in range(16 if DS >= 2 else 0):
                    pq = p[2] if d % 2 == 0 else p[3]
                    pcv = "P2" if d % 2 == 0 else "P3"

                    def mk_cv(t, d=d, pq=pq):
                        r = None
                        for k in range(4):
                            off = (d * 4 + k) * 128
                            r = t.matmul(
                                out=pq[:, 0:TC],
                                lhsT=diag_cv[:, off:off + 128],
                                rhs=xin_b[:, d * 260 + k:d * 260 + k + TC],
                                start=(k == 0), stop=(k == 3))
                        return r
                    P.op("tensor", mk_cv, reads=["dgc", "xinb", "xinh"],
                         writes=[pcv])
                    P.op("scalar", lambda s, d=d, pq=pq, xc_b=xc_b:
                         s.activation(
                             out=xc_b[:, d * TC:(d + 1) * TC],
                             in_=pq[:, 0:TC],
                             func=AF.Silu, bias=cb_sb[:, d:d + 1]),
                         reads=[pcv, "cb"], writes=[f"xc{jp}"])
                if DS < 3:
                    continue

                def mk_ssm(t, xc_b=xc_b, lo=0, pq=None):
                    r = None
                    for d in range(lo, lo + 8):
                        r = t.matmul(
                            out=pq[0:33, 0:TC],
                            lhsT=wxp_b[:, d * 33:(d + 1) * 33],
                            rhs=xc_b[:, d * TC:(d + 1) * TC],
                            start=(d == lo), stop=(d == lo + 7))
                    return r
                import functools
                P.op("tensor", functools.partial(mk_ssm, lo=0, pq=p[2]),
                     reads=["wxp", f"xc{jp}"], writes=["P2"])
                P.op("tensor", functools.partial(mk_ssm, lo=8, pq=p[3]),
                     reads=["wxp", f"xc{jp}"], writes=["P3"])
                P.op("vector", lambda v: v.tensor_copy(
                    out=ssm_f[:], in_=p[2][0:33, 0:TC]),
                    reads=["P2"], writes=["ssmf"])
                P.op("vector", lambda v: v.tensor_add(
                    out=ssm_f[:], in0=ssm_f[:], in1=p[3][0:33, 0:TC]),
                    reads=["ssmf", "P3"], writes=["ssmf"])
                for grp, rbase in (("B", 1), ("C", 17)):
                    P.dma("gpsimd",
                          bc_flat[0:1, :].rearrange("o (n t) -> o n t", t=TC),
                          ssm_f[rbase:rbase + NS, 0:TC].unsqueeze(0)
                          if False else
                          ssm_f[rbase:rbase + NS, 0:TC],
                          reads=["ssmf"], writes=["bcf"], fam="bcf")
                    dst = bcB if grp == "B" else bcC
                    dname = "bcB" if grp == "B" else "bcC"
                    for n in range(NS):
                        pq = (p[2], p[3], p[4], p[5])[n % 4]
                        pb = ("P2", "P3", "P4", "P5")[n % 4]
                        P.op("tensor", lambda t, n=n, pq=pq: t.matmul(
                            out=pq[:, 0:TC], lhsT=ones_row[:],
                            rhs=bc_flat[0:1, n * TC:(n + 1) * TC],
                            start=True, stop=True),
                            reads=["bcf", "ones"], writes=[pb])
                        P.op("scalar", lambda s, n=n, dst=dst, pq=pq:
                             s.copy(out=dst[:, n * TC:(n + 1) * TC],
                                    in_=pq[:, 0:TC]),
                             reads=[pb], writes=[dname])
                for d in range(16 if DS >= 4 else 0):
                    pq = p[2] if d % 2 == 0 else p[3]
                    pdl = "P2" if d % 2 == 0 else "P3"
                    base = 0 if d % 2 == 0 else TC
                    sl = slice(base, base + TC)
                    P.op("tensor", lambda t, d=d, pq=pq: t.matmul(
                        out=pq[:, 0:TC],
                        lhsT=wdt_row[0:1, d * 128:(d + 1) * 128],
                        rhs=ssm_f[0:1, 0:TC], start=True, stop=True),
                        reads=["ssmf", "wdt_row"], writes=[pdl])
                    P.op("vector", lambda v, d=d, sl=sl, pq=pq:
                         v.tensor_scalar(
                             out=arg_sp[:, sl], in0=pq[:, 0:TC],
                             scalar1=bdt_sb[:, d:d + 1], scalar2=None,
                             op0=ALU.add), reads=[pdl, "bdt"],
                         writes=[f"arg{d % 2}"])
                    P.op("scalar", lambda s, sl=sl: s.activation(
                        out=na_sp[:, sl], in_=arg_sp[:, sl], func=AF.Abs),
                        reads=[f"arg{d % 2}"], writes=[f"na{d % 2}"])
                    P.op("scalar", lambda s, sl=sl: s.activation(
                        out=na_sp[:, sl], in_=na_sp[:, sl], func=AF.Exp,
                        scale=-1.0), reads=[f"na{d % 2}"],
                        writes=[f"na{d % 2}"])
                    P.op("scalar", lambda s, sl=sl: s.activation(
                        out=na_sp[:, sl], in_=na_sp[:, sl], func=AF.Ln,
                        bias=1.0), reads=[f"na{d % 2}"],
                        writes=[f"na{d % 2}"])
                    P.op("vector", lambda v, d=d, sl=sl:
                         v.scalar_tensor_tensor(
                             out=delt[:, d * TC:(d + 1) * TC],
                             in0=arg_sp[:, sl], scalar=0.0, op0=ALU.max,
                             in1=na_sp[:, sl], op1=ALU.add),
                         reads=[f"arg{d % 2}", f"na{d % 2}"],
                         writes=[f"dl{d}"])
                y2_pending = []
                for d in range(16 if DS >= 5 else 0):
                    dp = d % 2
                    dsl = delt[:, d * TC:(d + 1) * TC]
                    dbx = dbx_[dp]
                    P.op("gpsimd", lambda g, d=d, xc_b=xc_b:
                         g.tensor_mul(
                             out=u_b[:, d * TC:(d + 1) * TC],
                             in0=delt[:, d * TC:(d + 1) * TC],
                             in1=xc_b[:, d * TC:(d + 1) * TC]),
                         reads=[f"dl{d}", f"xc{jp}"], writes=[f"u{d}"])
                    for n4 in range(4):
                        def mk_dbx(g, n4=n4, d=d, dbx=dbx):
                            r = None
                            for n in range(n4 * 4, n4 * 4 + 4):
                                r = g.tensor_mul(
                                    out=dbx[:, n * TC:(n + 1) * TC],
                                    in0=bcB[:, n * TC:(n + 1) * TC],
                                    in1=u_b[:, d * TC:(d + 1) * TC])
                            return r
                        P.op("gpsimd", mk_dbx,
                             reads=["bcB", f"u{d}"], writes=[f"dbx{dp}"])
                    for n in range(NS if DS >= 6 else 0):
                        np_ = n % 4
                        pa = dA_sb[:, np_ * TC:(np_ + 1) * TC]
                        paname = f"dA{np_}"
                        if n % 2 == 0:
                            np1 = (n + 1) % 4
                            pa1 = dA_sb[:, np1 * TC:(np1 + 1) * TC]

                            def mk_da(s, d=d, n=n, pa=pa, pa1=pa1, dsl=dsl):
                                s.activation(out=pa, in_=dsl, func=AF.Exp,
                                             scale=A_sb[d][:, n:n + 1])
                                return s.activation(
                                    out=pa1, in_=dsl, func=AF.Exp,
                                    scale=A_sb[d][:, n + 1:n + 2])
                            P.op("scalar", mk_da,
                                 reads=[f"dl{d}", f"A_{d}"],
                                 writes=[paname, f"dA{np1}"])
                        if DS < 7:
                            continue
                        if n % 2 == 1:
                            continue

                        def mk_scan2(v, d=d, n=n, dbx=dbx):
                            r = None
                            for nn in (n, n + 1):
                                pax = dA_sb[:, (nn % 4) * TC:
                                            (nn % 4 + 1) * TC]
                                r = v.tensor_tensor_scan(
                                    out=hbuf[:, nn * TC:(nn + 1) * TC],
                                    data0=pax,
                                    data1=dbx[:, nn * TC:(nn + 1) * TC],
                                    initial=hstate[:, d * NS + nn:
                                                   d * NS + nn + 1],
                                    op0=ALU.mult, op1=ALU.add)
                            return r
                        P.op("vector", mk_scan2,
                             reads=[paname, f"dA{(n + 1) % 4}",
                                    f"dbx{dp}", f"hst{d}"],
                             writes=["hbufS"])
                    if DS < 8:
                        continue
                    P.op("vector", lambda v, d=d: v.tensor_copy(
                        out=hstate[:, d * NS:(d + 1) * NS].unsqueeze(2),
                        in_=hbuf[:].rearrange("p (n t) -> p n t",
                                              t=TC)[:, :, TC - 1:TC]),
                        reads=["hbufS"], writes=[f"hst{d}"])
                    P.op("vector", lambda v, dbx=dbx: v.tensor_mul(
                        out=dbx[:], in0=hbuf[:], in1=bcC[:]),
                        reads=["hbufS", "bcC"], writes=[f"dbx{dp}"])
                    py = p[4] if dp == 0 else p[5]
                    pyn = "P4" if dp == 0 else "P5"

                    def mk_y(t, d=d, dbx=dbx, py=py, xc_b=xc_b):
                        r = None
                        for n in range(NS):
                            r = t.matmul(out=py[:, 0:TC],
                                         lhsT=iden_b[:],
                                         rhs=dbx[:, n * TC:(n + 1) * TC],
                                         start=(n == 0), stop=False)
                        r = t.matmul(out=py[:, 0:TC],
                                     lhsT=diag_D[:, d * 128:(d + 1) * 128],
                                     rhs=xc_b[:, d * TC:(d + 1) * TC],
                                     start=False, stop=True)
                        return r
                    P.op("tensor", mk_y,
                         reads=[f"dbx{dp}", "iden_b", "dgD", f"xc{jp}"],
                         writes=[pyn])
                    y2_pending.append((d, py, pyn))
                    if len(y2_pending) > 1:
                        dd, pyd, pynd = y2_pending.pop(0)
                        P.op("vector", lambda v, d=dd, py=pyd, sz_b=sz_b:
                             v.tensor_mul(
                                 out=y2T[:, d * TC:(d + 1) * TC],
                                 in0=py[:, 0:TC],
                                 in1=sz_b[:, d * TC:(d + 1) * TC]),
                             reads=[pynd, f"sz{jp}_{dd}"],
                             writes=[f"y2_{dd}"])
                while y2_pending:
                    dd, pyd, pynd = y2_pending.pop(0)
                    P.op("vector", lambda v, d=dd, py=pyd, sz_b=sz_b:
                         v.tensor_mul(
                             out=y2T[:, d * TC:(d + 1) * TC],
                             in0=py[:, 0:TC],
                             in1=sz_b[:, d * TC:(d + 1) * TC]),
                         reads=[pynd, f"sz{jp}_{dd}"],
                         writes=[f"y2_{dd}"])
                for h2 in range(2 if DS >= 9 else 0):
                    r0 = c0 + h2 * 128
                    op_ = h2 % 2
                    P.dma("sync", xres[:], x_d[r0:r0 + 128, :],
                          writes=["xres"], fam="xrs")
                    P.dma("sync", ffn_t[:], ffn_sc[r0:r0 + 128, :],
                          reads=["Dffn"], writes=["ffnt"], fam="fft")
                    for k in range(16):
                        def mk_wo1(t, h2=h2, k=k):
                            lhs = y2T[:, k * TC + h2 * 128:
                                      k * TC + h2 * 128 + 128]
                            r = None
                            for mi in range(2):
                                r = t.matmul(
                                    out=p[mi][:, 0:512], lhsT=lhs,
                                    rhs=woutb[:, k * D + mi * 512:
                                              k * D + (mi + 1) * 512],
                                    start=(k == 0), stop=(k == 15))
                            return r
                        P.op("tensor", mk_wo1,
                             reads=["wor"] + [f"y2_{dd}" for dd in range(16)],
                             writes=["pg0", "pg1"])
                    P.op("vector", lambda v: v.tensor_add(
                        out=out_t[:, 0:512], in0=p[0][:, 0:512],
                        in1=xres[:, 0:512]),
                        reads=["pg0", "xres"], writes=["outt"])
                    P.op("vector", lambda v: v.tensor_add(
                        out=out_t[:, 512:1024], in0=p[1][:, 0:512],
                        in1=xres[:, 512:1024]),
                        reads=["pg1", "xres"], writes=["outt"])
                    P.op("vector", lambda v: v.tensor_add(
                        out=out_t[:], in0=out_t[:], in1=ffn_t[:]),
                        reads=["outt", "ffnt"], writes=["outt"])
                    P.dma("sync", out_d[r0:r0 + 128, :], out_t[:],
                          reads=["outt"], fam="otw", waw=False)

        P.drain()

        blk = ctx.enter_context(nc.Block())

        def replay(handle, atoms):
            for a in atoms:
                if a[0] == "w":
                    handle.wait_ge(a[1], a[2])
                elif a[0] == "c":
                    instr = a[1](handle)
                    instr.then_inc(a[2], 1)
                elif a[0] == "d":
                    handle.dma_start(
                        out=a[1], in_=a[2],
                        allow_slow_non_contiguous=a[4]).then_inc(a[3], 16)
                elif a[0] == "b":
                    handle.nop().then_inc(a[1], 1)
                    handle.wait_ge(a[1], a[2])

        @blk.vector
        def _(v):
            replay(v, P.atoms["vector"])

        @blk.scalar
        def _(s):
            replay(s, P.atoms["scalar"])

        @blk.tensor
        def _(t):
            replay(t, P.atoms["tensor"])

        @blk.gpsimd
        def _(g):
            replay(g, P.atoms["gpsimd"])

        @blk.sync
        def _(sy_):
            replay(sy_, P.atoms["sync"])

    return nc


_NC_CACHE = [None]


def kernel(**inputs):
    n = 8
    x_full = np.ascontiguousarray(np.asarray(inputs["L0_attn_x"],
                                             dtype=np.float32))
    shards = [np.ascontiguousarray(x_full[i]) for i in range(n)]
    conv_w = np.ascontiguousarray(
        np.asarray(inputs["L0_attn_conv_w"], dtype=np.float32).reshape(DI, 4))
    w_dt = np.ascontiguousarray(
        np.asarray(inputs["L0_attn_w_dt"], dtype=np.float32).reshape(1, DI))
    common = {
        "W_in": np.ascontiguousarray(np.asarray(inputs["L0_attn_W_in"],
                                                dtype=np.float32)),
        "conv_w": conv_w,
        "conv_b": np.ascontiguousarray(np.asarray(inputs["L0_attn_conv_b"],
                                                  dtype=np.float32)),
        "W_xproj": np.ascontiguousarray(np.asarray(inputs["L0_attn_W_xproj"],
                                                   dtype=np.float32)),
        "w_dt": w_dt,
        "b_dt": np.ascontiguousarray(np.asarray(inputs["L0_attn_b_dt"],
                                                dtype=np.float32)),
        "A_log": np.ascontiguousarray(np.asarray(inputs["L0_attn_A_log"],
                                                 dtype=np.float32)),
        "D_param": np.ascontiguousarray(np.asarray(inputs["L0_attn_D_param"],
                                                   dtype=np.float32)),
        "W_out": np.ascontiguousarray(np.asarray(inputs["L0_attn_W_out"],
                                                 dtype=np.float32)),
        "w_gate": np.ascontiguousarray(np.asarray(inputs["L0_ffn_w_gate"],
                                                  dtype=np.float32)),
        "w_up": np.ascontiguousarray(np.asarray(inputs["L0_ffn_w_up"],
                                                dtype=np.float32)),
        "w_down": np.ascontiguousarray(np.asarray(inputs["L0_ffn_w_down"],
                                                  dtype=np.float32)),
    }
    in_maps = [dict(common, x=shards[i]) for i in range(n)]
    if _NC_CACHE[0] is None:
        _NC_CACHE[0] = build_nc()
    res = run_bass_kernel_spmd(_NC_CACHE[0], in_maps, core_ids=list(range(n)))
    out = np.stack([np.asarray(r["out"], dtype=np.float32)
                    for r in res.results], axis=0)
    return out
